# version 11
# baseline (speedup 1.0000x reference)
import contextlib
import numpy as np
import ml_dtypes
import concourse.bass as bass
import concourse.mybir as mybir
from concourse.bass_utils import run_bass_kernel_spmd

F32 = mybir.dt.float32
BF16 = mybir.dt.bfloat16
I32 = mybir.dt.int32
AF = mybir.ActivationFunctionType
ALU = mybir.AluOpType
AX = mybir.AxisListType

T = 4096
D = 1024
NCORES = 8
CH = 128
NCH = T // CH
NH = 32
HD = 64
DS = 128
DIN = 2048
CONVD = 2304
NT_CONV = 18
SEM_CAP = 2 ** 31 - 2 ** 16
PHYS = {'pe': 'pe', 'act': 'act', 'dve': 'dve', 'pool': 'pool', 'sp': 'sp',
        'actd': 'act', 'poold': 'pool', 'cc': 'pool', 'dved': 'dve', 'ped': 'pe'}


class Buf:
    __slots__ = ('w', 'r', 'name', 'dsem', 'dcnt', 'scoped', 'dkind')
    def __init__(self, name=''):
        self.w = None
        self.r = []
        self.name = name
        self.dsem = None
        self.dcnt = 0
        self.scoped = False
        self.dkind = None


class _Eng:
    def __init__(self, kind):
        self.kind = kind
        self.ops = []
        self.sem = None
        self.cnt = 0
        self.sems = []
        self.waited = {}


class Builder:

    def __init__(self, nc, stack):
        self.nc = nc
        self.stack = stack
        self.engs = {
            'pe': _Eng('c'), 'act': _Eng('c'), 'dve': _Eng('c'),
            'pool': _Eng('c'), 'sp': _Eng('d'), 'actd': _Eng('d'),
            'poold': _Eng('d'), 'cc': _Eng('d'), 'dved': _Eng('d'),
            'ped': _Eng('d'),
        }
        self.dma_bufs = []
        for e in self.engs.values():
            self._new_sem(e)

    def _new_sem(self, e):
        h = self.stack.enter_context(self.nc.semaphore())
        e.sem = h
        e.cnt = 0
        e.sems.append(h)
        self.sem_total = getattr(self, 'sem_total', {})
        self.sem_is_dma = getattr(self, 'sem_is_dma', {})
        self.sem_total[id(h)] = 0
        self.sem_is_dma[id(h)] = (e.kind == 'd')

    def op(self, eng, fn, reads=(), writes=(), inc=None):
        e = self.engs[eng]
        phys = PHYS[eng]
        pe_ = self.engs[phys]
        if inc is None:
            inc = 16 if e.kind == 'd' else 1
        deps = []
        for b in reads:
            if b.w is not None:
                deps.append(b.w)
        for b in writes:
            if b.w is not None:
                deps.append(b.w)
            deps.extend(b.r)

        if e.kind == 'd':
            assert len(writes) == 1, f'dma op must write exactly one Buf ({eng})'
            tb = writes[0]
            dk = 'sw' if eng in ('poold', 'cc') else 'hw'
            if tb.dsem is None:
                tb.dkind = dk
                pools = getattr(self, 'dsem_pool', None) or {}
                pool = pools.get(dk)
                if tb.scoped and pool:
                    tb.dsem, tb.dcnt = pool.pop()
                else:
                    tb.dsem = self.stack.enter_context(self.nc.semaphore())
                    tb.dcnt = 0
            else:
                assert tb.dkind == dk, f'mixed sw/hw DMA on Buf {tb.name}'
                self.dma_bufs = getattr(self, 'dma_bufs', [])
                self.dma_bufs.append(tb)
            sem = tb.dsem
            tb.dcnt += inc
            val = tb.dcnt
        else:
            if e.cnt + inc > SEM_CAP:
                self._new_sem(e)
            sem = e.sem
            val = e.cnt + inc
            e.cnt = val
        tok = (sem, val)
        e = pe_
        todo = []
        seen = {}
        for (s, v) in deps:
            k = id(s)
            if e.waited.get(k, 0) >= v:
                continue
            if seen.get(k, 0) >= v:
                continue
            seen[k] = v
        for (s, v) in deps:
            if seen.get(id(s), -1) == v:
                todo.append((s, v))
                seen[id(s)] = -2
        for (s, v) in todo:
            e.waited[id(s)] = max(e.waited.get(id(s), 0), v)
        def emit(engine_obj, _todo=tuple(todo), _fn=fn, _sem=sem, _inc=inc):
            for (s, v) in _todo:
                engine_obj.wait_ge(s, v)
            inst = _fn(engine_obj)
            inst.then_inc(_sem, _inc)
        e.ops.append(emit)
        for b in reads:
            b.r.append(tok)
        for b in writes:
            b.w = tok
            b.r = []
        return tok

    def raw_wait(self, eng, tok):
        e = self.engs[PHYS[eng]]
        if e.waited.get(id(tok[0]), 0) >= tok[1]:
            return
        e.waited[id(tok[0])] = tok[1]
        def emit(engine_obj, _t=tok):
            engine_obj.wait_ge(_t[0], _t[1])
        e.ops.append(emit)

    def barrier(self):
        toks = [(e.sem, e.cnt) for e in self.engs.values() if e.cnt > 0]
        toks += [(b.dsem, b.dcnt) for b in self.dma_bufs
                 if b.dcnt > 0 and b.dsem is not None]
        for en in ('pe', 'act', 'dve', 'pool', 'sp'):
            for t in toks:
                self.raw_wait(en, t)

    def push_scope(self):
        st = contextlib.ExitStack()
        self._scopes = getattr(self, '_scopes', [])
        self._scopes.append(st)
        self._dma_marks = getattr(self, '_dma_marks', [])
        self._dma_marks.append(len(getattr(self, 'dma_bufs', [])))
        return st

    def pop_scope(self):
        self.barrier()
        mark = self._dma_marks.pop()
        self.dsem_pool = getattr(self, 'dsem_pool', None) or {'sw': [], 'hw': []}
        for b in self.dma_bufs[mark:]:
            if b.scoped and b.dsem is not None:
                self.dsem_pool[b.dkind].append((b.dsem, b.dcnt))
                b.scoped = False
                b.dsem = None
        self._scopes.pop().close()

    @property
    def cur_stack(self):
        sc = getattr(self, '_scopes', [])
        return sc[-1] if sc else self.stack

    def emit_all(self):
        nc = self.nc
        with nc.Block() as block:
            @block.tensor
            def _(eng):
                for f in self.engs['pe'].ops:
                    f(eng)
            @block.scalar
            def _(eng):
                for f in self.engs['act'].ops:
                    f(eng)
            @block.vector
            def _(eng):
                for f in self.engs['dve'].ops:
                    f(eng)
            @block.gpsimd
            def _(eng):
                for f in self.engs['pool'].ops:
                    f(eng)
            @block.sync
            def _(eng):
                for f in self.engs['sp'].ops:
                    f(eng)


def striped_load(bld, bufs, dmas, reads, streams=('sp',)):
    for b in bufs:
        b.scoped = True
    for i, f in enumerate(dmas):
        bld.op(streams[i % len(streams)], f,
               reads=reads, writes=[bufs[i % len(bufs)]])


_SB_SERIAL = [0]

class SB:
    def __init__(self, bld, name, shape, dtype):
        _SB_SERIAL[0] += 1
        name = f'{name}_{_SB_SERIAL[0]}'
        self.t = bld.cur_stack.enter_context(bld.nc.sbuf_tensor(name, shape, dtype))
        self.b = Buf(name)
        self.b.scoped = True
    def __getitem__(self, idx):
        return self.t[idx]
    @property
    def ap(self):
        return self.t[:]


class Rot:
    def __init__(self, bld, name, shape, dtype, n=2):
        self.sbs = [SB(bld, name, shape, dtype) for _ in range(n)]
        self.n = n
    def __call__(self, i):
        return self.sbs[i % self.n]


class PS:
    def __init__(self, bld, name, shape, dtype):
        self.t = bld.stack.enter_context(bld.nc.psum_tensor(name, shape, dtype))
        self.b = Buf(name)
    def __getitem__(self, idx):
        return self.t[idx]


class DR:
    def __init__(self, nc, name, shape, dtype, kind='Internal', nbufs=1):
        self.t = nc.dram_tensor(name, shape, dtype, kind=kind)
        self.a = self.t.ap()
        self.bufs = [Buf(f'{name}[{i}]') for i in range(nbufs)]
        self.b = self.bufs[0]
    def __getitem__(self, idx):
        return self.a[idx]



def mm_group(bld, out_psum_bufs, fn, reads, tag=''):
    return bld.op('pe', fn, reads=reads, writes=out_psum_bufs)


def bc_row(bld, C, psum, sbuf_out, src_ap, n, src_bufs, extra_reads=()):
    ones = C['ones_f32']
    bld.op('pe', lambda e: e.matmul(out=psum[0:128, 0:n], lhsT=ones[0:1, 0:128],
                                    rhs=src_ap, start=True, stop=True),
           reads=[ones.b] + list(src_bufs), writes=[psum.b])
    sb, ap = sbuf_out
    bld.op('dve', lambda e: e.tensor_copy(out=ap, in_=psum[0:128, 0:n]),
           reads=[psum.b] + list(extra_reads), writes=[sb.b])


def part_reduce(bld, C, psum32, col_ap, col_buf, out_sb, out_ap, op):
    ident = C['ident_f32']
    bld.op('pe', lambda e: e.transpose(out=psum32[0:1, 0:128], in_=col_ap,
                                       identity=ident.ap),
           reads=[col_buf, ident.b], writes=[psum32.b])
    bld.op('dve', lambda e: e.tensor_reduce(out=out_ap, in_=psum32[0:1, 0:128],
                                            axis=AX.X, op=op),
           reads=[psum32.b], writes=[out_sb.b])



def emit_stage_a(bld, C, I, Dm, out_dram):
    nc = bld.nc
    bld.push_scope()

    dt_all = SB(bld, 'a_dt', [128, NCH * NH], F32)
    adt_all = SB(bld, 'a_adt', [128, NCH * NH], F32)
    dtb_bc = SB(bld, 'a_dtb', [128, NH], F32)
    a_bc = SB(bld, 'a_abc', [128, NH], F32)
    convw = SB(bld, 'a_cw', [128, NT_CONV * 4], F32)
    convb = SB(bld, 'a_cb', [128, NT_CONV], F32)
    S = [SB(bld, f'a_S{g}', [128, 512], F32) for g in range(4)]
    Sbf = [SB(bld, f'a_Sb{g}', [128, 512], BF16) for g in range(4)]
    smallR = Rot(bld, 'a_sm', [128, 512], F32, 2)
    xres = SB(bld, 'a_xr', [128, 1024], F32)
    xnew = SB(bld, 'a_xn', [128, 1024], F32)
    xout = SB(bld, 'a_xo', [128, 1024], F32)
    stat = SB(bld, 'a_stat', [128, 8], F32)
    st1 = SB(bld, 'a_st1', [1, 8], F32)
    zevacR = Rot(bld, 'a_zevr', [128, 512], F32, 3)
    p = C['p']
    pbx = C['pbx']

    u = I['u']
    W_in = I['L0_attn_W_in']
    conv_w = I['L0_attn_conv_w']
    conv_b = I['L0_attn_conv_b']
    dt_bias = I['L0_attn_dt_bias']
    A_log = I['L0_attn_A_log']
    norm_w = I['L0_attn_norm_w']
    W_out = I['L0_attn_W_out']

    ident_f = C['ident_f32']
    ident_b = C['ident_bf16']
    tri = C['tri_f32']
    triu_b = C['triu_bf16']
    ones = C['ones_f32']

    for g in range(4):
        bld.op('dve', lambda e, g=g: e.memset(S[g].ap, 0.0), reads=[], writes=[S[g].b])
        bld.op('pool', lambda e, g=g: e.memset(Sbf[g].ap, 0.0), reads=[], writes=[Sbf[g].b])

    for tc in range(32):
        r = slice(tc * 128, (tc + 1) * 128)
        bld.op('sp', lambda e, r=r: e.dma_start(out=xres.ap, in_=u[r, :]),
               reads=[u.b], writes=[xres.b])
        bld.op('poold', lambda e, r=r: e.dma_start(out=Dm['xbf'][r, :], in_=xres.ap),
               reads=[xres.b], writes=[Dm['xbf'].b])

    bld.push_scope()
    xfm = SB(bld, 'a_xfm', [128, 8 * T], BF16)
    wz = SB(bld, 'a_wz', [128, 8 * 2080], BF16)
    wxbc = [SB(bld, f'a_wxbc{i}', [128, 8 * 128], BF16) for i in range(2)]
    xfm_rb = [Buf(f'xfm{i}') for i in range(4)]
    striped_load(bld, xfm_rb,
                 [(lambda e, j=j: e.dma_start(
                       out=xfm[:, j * T:(j + 1) * T],
                       in_=Dm['xbf'][:, j * 128:(j + 1) * 128], transpose=True))
                  for j in range(8)],
                 reads=[Dm['xbf'].b])
    wia = W_in.a.rearrange('(k p) m -> p k m', p=128)
    for k in range(8):
        bld.op('poold', lambda e, k=k: e.dma_start(
                   out=wz[:, k * 2080:k * 2080 + 2048], in_=wia[:, k, 0:2048]),
               reads=[W_in.b], writes=[wz.b])
        bld.op('poold', lambda e, k=k: e.dma_start(
                   out=wz[:, k * 2080 + 2048:(k + 1) * 2080], in_=wia[:, k, 4352:4384]),
               reads=[W_in.b], writes=[wz.b])
    dtb1 = SB(bld, 'a_dtb1', [1, NH], F32)
    bld.op('sp', lambda e: e.dma_start(out=dtb1.ap, in_=dt_bias.a.rearrange('(o n) -> o n', o=1)),
           reads=[dt_bias.b], writes=[dtb1.b])
    bc_row(bld, C, p[5], (dtb_bc, dtb_bc.ap), dtb1.ap, NH, [dtb1.b])
    al1 = SB(bld, 'a_al1', [1, NH], F32)
    bld.op('sp', lambda e: e.dma_start(out=al1.ap, in_=A_log.a.rearrange('(o n) -> o n', o=1)),
           reads=[A_log.b], writes=[al1.b])
    aneg = SB(bld, 'a_aneg', [1, NH], F32)
    bld.op('act', lambda e: e.activation(out=aneg.ap, in_=al1.ap, func=AF.Exp),
           reads=[al1.b], writes=[aneg.b])
    bld.op('dve', lambda e: e.tensor_scalar(out=aneg.ap, in0=aneg.ap, scalar1=-1.0,
                                            scalar2=None, op0=ALU.mult),
           reads=[aneg.b], writes=[aneg.b])
    bc_row(bld, C, p[5], (a_bc, a_bc.ap), aneg.ap, NH, [aneg.b])
    def a1z_body(tc, small):
        rtok = slice(tc * 128, (tc + 1) * 128)
        pd = p[4] if tc % 2 == 0 else p[5]
        def zmm(e):
            last = None
            for k in range(8):
                e.ldweights(xfm[:, k * T + tc * 128:k * T + (tc + 1) * 128])
                for n in range(4):
                    last = e.matmul(out=p[n][0:128, 0:512],
                                    lhsT=xfm[:, k * T + tc * 128:k * T + (tc + 1) * 128],
                                    rhs=wz[:, k * 2080 + n * 512:k * 2080 + (n + 1) * 512],
                                    start=(k == 0), stop=(k == 7))
                last = e.matmul(out=pd[0:128, 0:NH],
                                lhsT=xfm[:, k * T + tc * 128:k * T + (tc + 1) * 128],
                                rhs=wz[:, k * 2080 + 2048:(k + 1) * 2080],
                                start=(k == 0), stop=(k == 7))
            return last
        bld.op('pe', zmm, reads=[*xfm_rb, wz.b],
               writes=[p[0].b, p[1].b, p[2].b, p[3].b, pd.b])
        for n in range(4):
            zev = zevacR(tc * 4 + n)
            bld.op('act', lambda e, n=n, zev=zev: e.activation(out=zev.ap,
                       in_=p[n][0:128, 0:512], func=AF.Silu),
                   reads=[p[n].b], writes=[zev.b])
            bld.op('actd', lambda e, n=n, zev=zev: e.dma_start(
                       out=Dm['z'][rtok, n * 512:(n + 1) * 512], in_=zev.ap),
                   reads=[zev.b], writes=[Dm['z'].b])
        c0 = tc * NH
        bld.op('dve', lambda e: e.tensor_tensor(out=small[:, 0:NH], in0=pd[0:128, 0:NH],
                                                in1=dtb_bc.ap, op=ALU.add),
               reads=[pd.b, dtb_bc.b], writes=[small.b])
        bld.op('act', lambda e: e.activation(out=small[:, NH:2 * NH], in_=small[:, 0:NH],
                                             func=AF.Abs),
               reads=[small.b], writes=[small.b])
        bld.op('act', lambda e: e.activation(out=small[:, 2 * NH:3 * NH],
                                             in_=small[:, NH:2 * NH], func=AF.Exp, scale=-1.0),
               reads=[small.b], writes=[small.b])
        bld.op('act', lambda e: e.activation(out=small[:, 3 * NH:4 * NH],
                                             in_=small[:, 2 * NH:3 * NH], func=AF.Ln,
                                             bias=C['cb'][:, 1:2]),
               reads=[small.b, C['cb'].b], writes=[small.b])
        bld.op('dve', lambda e: e.tensor_scalar(out=small[:, 4 * NH:5 * NH], in0=small[:, 0:NH],
                                                scalar1=0.0, scalar2=None, op0=ALU.max),
               reads=[small.b], writes=[small.b])
        bld.op('dve', lambda e: e.tensor_tensor(out=dt_all[:, c0:c0 + NH],
                                                in0=small[:, 3 * NH:4 * NH],
                                                in1=small[:, 4 * NH:5 * NH], op=ALU.add),
               reads=[small.b], writes=[dt_all.b])
    for tc in range(32):
        a1z_body(tc, smallR(tc))
    bld.op('dve', lambda e: e.tensor_tensor(
               out=adt_all.t[:].rearrange('p (c h) -> p c h', h=NH),
               in0=dt_all.t[:].rearrange('p (c h) -> p c h', h=NH),
               in1=a_bc.t[:].rearrange('p (h o) -> p o h', o=1).broadcast_to([128, NCH, NH]),
               op=ALU.mult),
           reads=[dt_all.b, a_bc.b], writes=[adt_all.b])

    for m in range(18):
        wb = wxbc[m % 2]
        bld.op('poold', lambda e, m=m, wb=wb: e.dma_start(
                   out=wb.t[:].rearrange('p (k c) -> p k c', c=128),
                   in_=wia[:, :, 2048 + m * 128:2048 + (m + 1) * 128]),
               reads=[W_in.b], writes=[wb.b])
        for grp in range(2):
            def xmm(e, m=m, grp=grp, wb=wb):
                last = None
                for k in range(8):
                    e.ldweights(wb[:, k * 128:(k + 1) * 128])
                    for tp in range(4):
                        tpi = grp * 4 + tp
                        last = e.matmul(out=p[tp][0:128, 0:512],
                                        lhsT=wb[:, k * 128:(k + 1) * 128],
                                        rhs=xfm[:, k * T + tpi * 512:k * T + (tpi + 1) * 512],
                                        start=(k == 0), stop=(k == 7))
                return last
            bld.op('pe', xmm, reads=[wb.b, *xfm_rb],
                   writes=[p[0].b, p[1].b, p[2].b, p[3].b])
            for tp in range(4):
                tpi = grp * 4 + tp
                zev = zevacR(m * 8 + grp * 4 + tp)
                bld.op('act', lambda e, tp=tp, zev=zev: e.copy(out=zev.ap,
                           in_=p[tp][0:128, 0:512]),
                       reads=[p[tp].b], writes=[zev.b])
                bld.op('actd', lambda e, m=m, tpi=tpi, zev=zev: e.dma_start(
                           out=Dm['xbc'][m * 128:(m + 1) * 128, tpi * 512:(tpi + 1) * 512],
                           in_=zev.ap),
                       reads=[zev.b], writes=[Dm['xbc'].b])

    bld.pop_scope()

    bld.push_scope()
    convb_t = SB(bld, 'a_cbt', [NT_CONV, 128], F32)
    diagw = SB(bld, 'a_dw', [128, NT_CONV * 4 * 128], BF16)
    cwinR = Rot(bld, 'a_cwin', [128, 515], BF16, 3)
    xcsbR = Rot(bld, 'a_xcsb', [128, 512], BF16, 3)
    cwa = conv_w.a.rearrange('(t p) j -> p t j', p=128)
    bld.op('sp', lambda e: e.dma_start(out=convw.t[:].rearrange('p (t j) -> p t j', j=4), in_=cwa),
           reads=[conv_w.b], writes=[convw.b])
    bld.op('sp', lambda e: e.dma_start(out=convb_t.ap, in_=conv_b.a.rearrange('(t p) -> t p', p=128)),
           reads=[conv_b.b], writes=[convb_t.b])
    bld.op('pe', lambda e: e.transpose(out=p[5][0:128, 0:NT_CONV], in_=convb_t.ap,
                                       identity=ident_f[0:NT_CONV, 0:NT_CONV]),
           reads=[convb_t.b, ident_f.b], writes=[p[5].b])
    bld.op('dve', lambda e: e.tensor_copy(out=convb.ap, in_=p[5][0:128, 0:NT_CONV]),
           reads=[p[5].b], writes=[convb.b])
    for t in range(NT_CONV):
        for j in range(4):
            idx = t * 4 + j
            bld.op('dve', lambda e, idx=idx, t=t, j=j: e.tensor_scalar(
                       out=diagw[:, idx * 128:(idx + 1) * 128], in0=ident_f.ap,
                       scalar1=convw[:, t * 4 + j:t * 4 + j + 1], scalar2=None,
                       op0=ALU.mult),
                   reads=[ident_f.b, convw.b], writes=[diagw.b])
    def a2_body(tp, t, it, cwin, xcsb):
        pc = p[it % 4]
        if tp == 0:
            bld.op('pool', lambda e: e.memset(cwin[:, 0:3], 0.0),
                   reads=[], writes=[cwin.b])
            bld.op('poold', lambda e: e.dma_start(
                       out=cwin[:, 3:515],
                       in_=Dm['xbc'][t * 128:(t + 1) * 128, 0:512]),
                   reads=[Dm['xbc'].b], writes=[cwin.b])
        else:
            bld.op('poold', lambda e: e.dma_start(
                       out=cwin.ap,
                       in_=Dm['xbc'][t * 128:(t + 1) * 128, tp * 512 - 3:(tp + 1) * 512]),
                   reads=[Dm['xbc'].b], writes=[cwin.b])
        def cmm(e):
            last = None
            for j in range(4):
                e.ldweights(diagw[:, (t * 4 + j) * 128:(t * 4 + j + 1) * 128])
                last = e.matmul(out=pc[0:128, 0:512],
                                lhsT=diagw[:, (t * 4 + j) * 128:(t * 4 + j + 1) * 128],
                                rhs=cwin[:, j:j + 512],
                                start=(j == 0), stop=(j == 3))
            return last
        bld.op('pe', cmm, reads=[diagw.b, cwin.b], writes=[pc.b])
        bld.op('act', lambda e: e.activation(out=xcsb.ap, in_=pc[0:128, 0:512],
                                             func=AF.Silu,
                                             bias=convb[:, t:t + 1]),
               reads=[pc.b, convb.b], writes=[xcsb.b])
        bld.op('actd', lambda e: e.dma_start(
                   out=Dm['xc'][t * 128:(t + 1) * 128, tp * 512:(tp + 1) * 512],
                   in_=xcsb.ap),
               reads=[xcsb.b], writes=[Dm['xc'].b])
    for tp in range(8):
        for t in range(NT_CONV):
            it = tp * NT_CONV + t
            a2_body(tp, t, it, cwinR(it), xcsbR(it))

    bld.pop_scope()

    bld.push_scope()
    wout = SB(bld, 'a_wo', [128, 16 * 1024], BF16)
    nwcol = SB(bld, 'a_nw', [128, 16], F32)
    acs_tR = Rot(bld, 'a_acs', [128, NH], F32, 2)
    negacsR = Rot(bld, 'a_nacs', [128, NH], F32, 2)
    asum_sbR = Rot(bld, 'a_asum', [1, NH], F32, 2)
    asum_bcR = Rot(bld, 'a_asbc', [128, NH], F32, 2)
    dec128R = Rot(bld, 'a_dec', [128, NH], F32, 2)
    dcolR = Rot(bld, 'a_dcol', [128, NH], F32, 2)
    dtdecR = Rot(bld, 'a_dtd', [128, NH], F32, 2)
    outdecR = Rot(bld, 'a_od', [128, NH], F32, 2)
    acsfmR = Rot(bld, 'a_acsfm', [32, 128], F32, 2)
    rbR = Rot(bld, 'a_rb', [128, 4096], F32, 2)
    lallR = Rot(bld, 'a_lall', [128, 4096], BF16, 2)
    xfmcR = Rot(bld, 'a_xfmc', [128, 2048], BF16, 2)
    bfmR = Rot(bld, 'a_bfm', [128, 128], BF16, 2)
    cfmR = Rot(bld, 'a_cfm', [128, 128], BF16, 2)
    xtmR = Rot(bld, 'a_xtm', [128, 2048], BF16, 2)
    btmR = Rot(bld, 'a_btm', [128, 128], BF16, 2)
    gmaskR = Rot(bld, 'a_gm', [128, 128], BF16, 2)
    xdR = Rot(bld, 'a_xd', [128, 2048], BF16, 2)
    xd2R = Rot(bld, 'a_xd2', [128, 2048], BF16, 1)
    yosc = SB(bld, 'a_yo', [128, 512], F32)
    yallR = Rot(bld, 'a_y', [128, 2048], F32, 2)
    stmp = SB(bld, 'a_st', [128, 512], F32)
    szR = Rot(bld, 'a_sz', [128, 2048], F32, 1)
    yzt = SB(bld, 'a_yzt', [128, 2048], F32)
    yzb = SB(bld, 'a_yzb', [128, 2048], BF16)
    yzfm = SB(bld, 'a_yzfm', [128, 2048], BF16)
    bld.push_scope()
    wtmp = SB(bld, 'a_wtmp', [128, 1024], F32)
    nw_t = SB(bld, 'a_nwt', [16, 128], F32)
    bld.op('sp', lambda e: e.dma_start(out=nw_t.ap, in_=norm_w.a.rearrange('(k p) -> k p', p=128)),
           reads=[norm_w.b], writes=[nw_t.b])
    bld.op('pe', lambda e: e.transpose(out=p[5][0:128, 0:16], in_=nw_t.ap,
                                       identity=ident_f[0:16, 0:16]),
           reads=[nw_t.b, ident_f.b], writes=[p[5].b])
    bld.op('dve', lambda e: e.tensor_copy(out=nwcol.ap, in_=p[5][0:128, 0:16]),
           reads=[p[5].b], writes=[nwcol.b])
    woa = W_out.a.rearrange('(k p) m -> p k m', p=128)
    for k in range(16):
        bld.op('sp', lambda e, k=k: e.dma_start(out=wtmp.ap, in_=woa[:, k, :]),
               reads=[W_out.b], writes=[wtmp.b])
        bld.op('dve', lambda e, k=k: e.tensor_scalar(
                   out=wout[:, k * 1024:(k + 1) * 1024], in0=wtmp.ap,
                   scalar1=nwcol[:, k:k + 1], scalar2=None, op0=ALU.mult),
               reads=[wtmp.b, nwcol.b], writes=[wout.b])
    bld.pop_scope()

    def a3_body(tc, acs_t, negacs, asum_sb, asum_bc, dec128, dcol, dtdec,
                outdec, acsfm, rb, lall, xfmc, bfm, cfm, xtm, btm, gmask,
                xd, xd2, yall, sz):
        rtok = slice(tc * 128, (tc + 1) * 128)
        c0 = tc * NH
        pa = p[4] if tc % 2 == 0 else p[5]
        bld.op('pe', lambda e: e.matmul(out=pa[0:128, 0:NH], lhsT=tri.ap,
                                        rhs=adt_all[:, c0:c0 + NH],
                                        start=True, stop=True),
               reads=[tri.b, adt_all.b], writes=[pa.b])
        bld.op('dve', lambda e: e.tensor_copy(out=acs_t.ap, in_=pa[0:128, 0:NH]),
               reads=[pa.b], writes=[acs_t.b])
        bld.op('dve', lambda e: e.tensor_scalar(out=negacs.ap, in0=acs_t.ap, scalar1=-1.0,
                                                scalar2=None, op0=ALU.mult),
               reads=[acs_t.b], writes=[negacs.b])
        bld.op('pe', lambda e: e.matmul(out=pa[0:1, 64:64 + NH],
                                        lhsT=ones[0:128, 0:1],
                                        rhs=adt_all[:, c0:c0 + NH],
                                        start=True, stop=True),
               reads=[ones.b, adt_all.b], writes=[pa.b])
        bld.op('dve', lambda e: e.tensor_copy(out=asum_sb.ap, in_=pa[0:1, 64:64 + NH]),
               reads=[pa.b], writes=[asum_sb.b])
        bc_row(bld, C, pa, (asum_bc, asum_bc.ap), asum_sb.ap, NH, [asum_sb.b])
        bld.op('act', lambda e: e.activation(out=dec128.ap, in_=asum_bc.ap, func=AF.Exp),
               reads=[asum_bc.b], writes=[dec128.b])
        bld.op('dve', lambda e: e.tensor_tensor(out=dcol.ap, in0=asum_bc.ap, in1=acs_t.ap,
                                                op=ALU.subtract),
               reads=[asum_bc.b, acs_t.b], writes=[dcol.b])
        bld.op('act', lambda e: e.activation(out=dcol.ap, in_=dcol.ap, func=AF.Exp),
               reads=[dcol.b], writes=[dcol.b])
        bld.op('dve', lambda e: e.tensor_tensor(out=dtdec.ap,
                                                in0=dt_all[:, c0:c0 + NH],
                                                in1=dcol.ap, op=ALU.mult),
               reads=[dt_all.b, dcol.b], writes=[dtdec.b])
        bld.op('act', lambda e: e.activation(out=outdec.ap, in_=acs_t.ap, func=AF.Exp),
               reads=[acs_t.b], writes=[outdec.b])
        bld.op('pe', lambda e: e.transpose(out=pa[0:NH, 0:128], in_=acs_t.ap,
                                           identity=ident_f.ap),
               reads=[acs_t.b, ident_f.b], writes=[pa.b])
        bld.op('dve', lambda e: e.tensor_copy(out=acsfm.ap, in_=pa[0:NH, 0:128]),
               reads=[pa.b], writes=[acsfm.b])
        acs_dr = Dm['acs']
        ab = acs_dr.bufs[tc % len(acs_dr.bufs)]
        bld.op('actd', lambda e: e.dma_start(out=acs_dr[tc, :, :], in_=acsfm.ap),
               reads=[acsfm.b], writes=[ab])
        bld.op('sp', lambda e: e.dma_start(
                   out=rb.ap,
                   in_=acs_dr.a.rearrange('c h l -> c (h l)')[tc:tc + 1, :]
                       .broadcast_to([128, 4096])),
               reads=[ab], writes=[rb.b])
        bld.op('sp', lambda e: e.dma_start(
                   out=xfmc.t[:].rearrange('p (t c) -> p t c', c=128),
                   in_=Dm['xc'].a.rearrange('(t p) n -> p t n', p=128)[:, 0:16, rtok]),
               reads=[Dm['xc'].b], writes=[xfmc.b])
        bld.op('sp', lambda e: e.dma_start(out=bfm.ap, in_=Dm['xc'][2048:2176, rtok]),
               reads=[Dm['xc'].b], writes=[bfm.b])
        bld.op('sp', lambda e: e.dma_start(out=cfm.ap, in_=Dm['xc'][2176:2304, rtok]),
               reads=[Dm['xc'].b], writes=[cfm.b])
        def xtr(e):
            last = None
            for t in range(16):
                last = e.transpose(out=pbx[0:128, t * 128:(t + 1) * 128],
                                   in_=xfmc[:, t * 128:(t + 1) * 128],
                                   identity=ident_b.ap)
            return last
        bld.op('pe', xtr, reads=[xfmc.b, ident_b.b], writes=[pbx.b])
        bld.op('dve', lambda e: e.tensor_copy(out=xtm.ap, in_=pbx[0:128, 0:2048]),
               reads=[pbx.b], writes=[xtm.b])
        bld.op('pe', lambda e: e.transpose(out=pbx[0:128, 0:128], in_=bfm.ap,
                                           identity=ident_b.ap),
               reads=[bfm.b, ident_b.b, xtm.b], writes=[pbx.b])
        bld.op('dve', lambda e: e.tensor_copy(out=btm.ap, in_=pbx[0:128, 0:128]),
               reads=[pbx.b], writes=[btm.b])
        bld.op('dve', lambda e: e.tensor_tensor(
                   out=xd.t[:].rearrange('p (h q) -> p h q', q=HD),
                   in0=xtm.t[:].rearrange('p (h q) -> p h q', q=HD),
                   in1=dt_all[:, c0:c0 + NH].rearrange('p (h o) -> p h o', o=1)
                       .broadcast_to([128, NH, HD]),
                   op=ALU.mult),
               reads=[xtm.b, dt_all.b], writes=[xd.b])
        bld.op('pool', lambda e: e.tensor_tensor(
                   out=xd2.t[:].rearrange('p (h q) -> p h q', q=HD),
                   in0=xtm.t[:].rearrange('p (h q) -> p h q', q=HD),
                   in1=dtdec.t[:].rearrange('p (h o) -> p h o', o=1).broadcast_to([128, NH, HD]),
                   op=ALU.mult),
               reads=[xtm.b, dtdec.b], writes=[xd2.b])
        bld.op('pool', lambda e: e.tensor_tensor(
                   out=rb.t[:].rearrange('p (h l) -> p h l', l=128),
                   in0=rb.t[:].rearrange('p (h l) -> p h l', l=128),
                   in1=acs_t.t[:].rearrange('p (h o) -> p h o', o=1).broadcast_to([128, NH, 128]),
                   op=ALU.subtract),
               reads=[rb.b, acs_t.b], writes=[rb.b])
        bld.op('dve', lambda e: e.tensor_tensor(
                   out=rb.t[:].rearrange('p (h l) -> p h l', l=128),
                   in0=rb.t[:].rearrange('p (h l) -> p h l', l=128),
                   in1=C['maskb_f32'].t[:].rearrange('s (l o) -> s o l', o=1)
                       .broadcast_to([128, NH, 128]),
                   op=ALU.add),
               reads=[rb.b, C['maskb_f32'].b], writes=[rb.b])
        bld.op('act', lambda e: e.activation(out=lall.ap, in_=rb.ap, func=AF.Exp),
               reads=[rb.b], writes=[lall.b])
        bld.op('pe', lambda e: e.matmul(out=pa[0:128, 128:256], lhsT=bfm.ap, rhs=cfm.ap,
                                        start=True, stop=True),
               reads=[bfm.b, cfm.b], writes=[pa.b])
        bld.op('dve', lambda e: e.scalar_tensor_tensor(out=gmask.ap, in0=pa[0:128, 128:256],
                                                       scalar=1.0, in1=triu_b.ap,
                                                       op0=ALU.mult, op1=ALU.mult),
               reads=[pa.b, triu_b.b], writes=[gmask.b])
        bld.op('dve', lambda e: e.tensor_tensor(
                   out=lall.t[:].rearrange('p (h l) -> p h l', l=128),
                   in0=lall.t[:].rearrange('p (h l) -> p h l', l=128),
                   in1=gmask.t[:].rearrange('s (l o) -> s o l', o=1).broadcast_to([128, NH, 128]),
                   op=ALU.mult),
               reads=[lall.b, gmask.b], writes=[lall.b])
        for g in range(4):
            def ydmm(e, g=g):
                last = None
                for hh in range(8):
                    h = g * 8 + hh
                    e.ldweights(lall[:, h * 128:(h + 1) * 128])
                    last = e.matmul(out=p[0][0:128, hh * 64:(hh + 1) * 64],
                                    lhsT=lall[:, h * 128:(h + 1) * 128],
                                    rhs=xd[:, h * 64:(h + 1) * 64],
                                    start=True, stop=True)
                return last
            bld.op('pe', ydmm, reads=[lall.b, xd.b], writes=[p[0].b])
            bld.op('pe', lambda e, g=g: (e.ldweights(cfm.ap),
                                         e.matmul(out=p[1][0:128, 0:512], lhsT=cfm.ap,
                                                  rhs=Sbf[g].ap, start=True, stop=True))[-1],
                   reads=[cfm.b, Sbf[g].b], writes=[p[1].b])
            bld.op('pe', lambda e, g=g: (e.ldweights(btm.ap),
                                         e.matmul(out=p[2][0:128, 0:512], lhsT=btm.ap,
                                                  rhs=xd2[:, g * 512:(g + 1) * 512],
                                                  start=True, stop=True))[-1],
                   reads=[btm.b, xd2.b], writes=[p[2].b])
            bld.op('dve', lambda e, g=g: e.tensor_tensor(
                       out=yosc.t[:].rearrange('p (h q) -> p h q', q=HD),
                       in0=p[1].t[:].rearrange('p (h q) -> p h q', q=HD),
                       in1=outdec[:, g * 8:(g + 1) * 8].rearrange('p (h o) -> p h o', o=1)
                           .broadcast_to([128, 8, HD]),
                       op=ALU.mult),
                   reads=[p[1].b, outdec.b], writes=[yosc.b])
            bld.op('dve', lambda e, g=g: e.tensor_tensor(out=yall[:, g * 512:(g + 1) * 512],
                                                         in0=p[0][0:128, 0:512],
                                                         in1=yosc.ap, op=ALU.add),
                   reads=[p[0].b, yosc.b], writes=[yall.b])
            bld.op('pool', lambda e, g=g: e.tensor_tensor(
                       out=stmp.t[:].rearrange('p (h q) -> p h q', q=HD),
                       in0=S[g].t[:].rearrange('p (h q) -> p h q', q=HD),
                       in1=dec128[:, g * 8:(g + 1) * 8].rearrange('p (h o) -> p h o', o=1)
                           .broadcast_to([128, 8, HD]),
                       op=ALU.mult),
                   reads=[S[g].b, dec128.b], writes=[stmp.b])
            bld.op('dve', lambda e, g=g: e.tensor_tensor(out=S[g].ap, in0=stmp.ap,
                                                         in1=p[2][0:128, 0:512], op=ALU.add),
                   reads=[stmp.b, p[2].b], writes=[S[g].b])
            bld.op('pool', lambda e, g=g: e.tensor_copy(out=Sbf[g].ap, in_=S[g].ap),
                   reads=[S[g].b], writes=[Sbf[g].b])
        bld.op('actd', lambda e: e.dma_start(out=sz.ap, in_=Dm['z'][rtok, :]),
               reads=[Dm['z'].b], writes=[sz.b])
        bld.op('dve', lambda e: e.reduce_sum(out=stat[:, 0:1], in_=yall.ap, axis=AX.X),
               reads=[yall.b], writes=[stat.b])
        bld.op('act', lambda e: e.activation(out=yzt.ap, in_=yall.ap, func=AF.Square,
                                             accum_out=stat[:, 1:2]),
               reads=[yall.b], writes=[yzt.b, stat.b])
        bld.op('dve', lambda e: e.tensor_scalar(out=stat[:, 2:3], in0=stat[:, 0:1],
                                                scalar1=1.0 / DIN, scalar2=None, op0=ALU.mult),
               reads=[stat.b], writes=[stat.b])
        bld.op('dve', lambda e: e.tensor_scalar(out=stat[:, 3:4], in0=stat[:, 1:2],
                                                scalar1=1.0 / DIN, scalar2=None, op0=ALU.mult),
               reads=[stat.b], writes=[stat.b])
        bld.op('dve', lambda e: e.tensor_tensor(out=stat[:, 4:5], in0=stat[:, 2:3],
                                                in1=stat[:, 2:3], op=ALU.mult),
               reads=[stat.b], writes=[stat.b])
        bld.op('dve', lambda e: e.tensor_tensor(out=stat[:, 5:6], in0=stat[:, 3:4],
                                                in1=stat[:, 4:5], op=ALU.subtract),
               reads=[stat.b], writes=[stat.b])
        bld.op('act', lambda e: e.activation(out=stat[:, 6:7], in_=stat[:, 5:6],
                                             func=AF.Ln, bias=C['cb'][:, 0:1]),
               reads=[stat.b, C['cb'].b], writes=[stat.b])
        bld.op('act', lambda e: e.activation(out=stat[:, 7:8], in_=stat[:, 6:7],
                                             func=AF.Exp, scale=-0.5),
               reads=[stat.b], writes=[stat.b])
        bld.op('dve', lambda e: e.tensor_scalar(out=yzt.ap, in0=yall.ap,
                                                scalar1=stat[:, 2:3], scalar2=stat[:, 7:8],
                                                op0=ALU.subtract, op1=ALU.mult),
               reads=[yall.b, stat.b], writes=[yzt.b])
        bld.op('pool', lambda e: e.tensor_tensor(out=yzb.ap, in0=yzt.ap, in1=sz.ap,
                                                 op=ALU.mult),
               reads=[yzt.b, sz.b], writes=[yzb.b])
        def yztr(e):
            last = None
            for t in range(16):
                last = e.transpose(out=pbx[0:128, t * 128:(t + 1) * 128],
                                   in_=yzb[:, t * 128:(t + 1) * 128],
                                   identity=ident_b.ap)
            return last
        bld.op('pe', yztr, reads=[yzb.b, ident_b.b], writes=[pbx.b])
        bld.op('dve', lambda e: e.tensor_copy(out=yzfm.ap, in_=pbx[0:128, 0:2048]),
               reads=[pbx.b], writes=[yzfm.b])
        pb2 = p[5] if tc % 2 == 0 else p[4]
        def omm(e, pb2=pb2):
            last = None
            for k in range(16):
                e.ldweights(yzfm[:, k * 128:(k + 1) * 128])
                last = e.matmul(out=p[3][0:128, 0:512],
                                lhsT=yzfm[:, k * 128:(k + 1) * 128],
                                rhs=wout[:, k * 1024:k * 1024 + 512],
                                start=(k == 0), stop=(k == 15))
                last = e.matmul(out=pb2[0:128, 0:512],
                                lhsT=yzfm[:, k * 128:(k + 1) * 128],
                                rhs=wout[:, k * 1024 + 512:(k + 1) * 1024],
                                start=(k == 0), stop=(k == 15))
            return last
        bld.op('pe', omm, reads=[yzfm.b, wout.b], writes=[p[3].b, pb2.b])
        bld.op('sp', lambda e: e.dma_start(out=xres.ap, in_=u[rtok, :]),
               reads=[u.b], writes=[xres.b])
        for n in range(2):
            pn = p[3] if n == 0 else pb2
            bld.op('dve', lambda e, n=n, pn=pn: e.scalar_tensor_tensor(
                       out=xnew[:, n * 512:(n + 1) * 512],
                       in0=pn[0:128, 0:512],
                       scalar=1.0, in1=xres[:, n * 512:(n + 1) * 512],
                       op0=ALU.mult, op1=ALU.add),
                   reads=[pn.b, xres.b], writes=[xnew.b])
        emit_ln(bld, C, xnew, xout, stat, st1, 1024, pa)
        bld.op('actd', lambda e: e.dma_start(out=out_dram[rtok, :], in_=xout.ap),
               reads=[xout.b], writes=[out_dram.b])
        bld.op('poold', lambda e: e.dma_start(out=Dm['xbf'][rtok, :], in_=xout.ap),
               reads=[xout.b], writes=[Dm['xbf'].b])

    for tc in range(NCH):
        a3_body(tc, acs_tR(tc), negacsR(tc), asum_sbR(tc), asum_bcR(tc),
                dec128R(tc), dcolR(tc), dtdecR(tc), outdecR(tc), acsfmR(tc),
                rbR(tc), lallR(tc), xfmcR(tc), bfmR(tc), cfmR(tc), xtmR(tc),
                btmR(tc), gmaskR(tc), xdR(tc), xd2R(tc), yallR(tc), szR(tc))
    bld.pop_scope()
    bld.pop_scope()


def emit_ln(bld, C, xin, xout, stat, st1, n, psum32):
    bld.op('dve', lambda e: e.reduce_sum(out=stat[:, 0:1], in_=xin[:, 0:n], axis=AX.X),
           reads=[xin.b], writes=[stat.b])
    bld.op('act', lambda e: e.activation(out=xout[:, 0:n], in_=xin[:, 0:n], func=AF.Square,
                                         accum_out=stat[:, 1:2]),
           reads=[xin.b], writes=[xout.b, stat.b])
    bld.op('dve', lambda e: e.tensor_scalar(out=stat[:, 2:3], in0=stat[:, 0:1],
                                            scalar1=1.0 / n, scalar2=None, op0=ALU.mult),
           reads=[stat.b], writes=[stat.b])
    bld.op('dve', lambda e: e.tensor_scalar(out=stat[:, 3:4], in0=stat[:, 1:2],
                                            scalar1=1.0 / n, scalar2=None, op0=ALU.mult),
           reads=[stat.b], writes=[stat.b])
    bld.op('dve', lambda e: e.tensor_tensor(out=stat[:, 4:5], in0=stat[:, 2:3],
                                            in1=stat[:, 2:3], op=ALU.mult),
           reads=[stat.b], writes=[stat.b])
    bld.op('dve', lambda e: e.tensor_tensor(out=stat[:, 5:6], in0=stat[:, 3:4],
                                            in1=stat[:, 4:5], op=ALU.subtract),
           reads=[stat.b], writes=[stat.b])
    bld.op('act', lambda e: e.activation(out=stat[:, 6:7], in_=stat[:, 5:6],
                                         func=AF.Ln, bias=C['cb'][:, 0:1]),
           reads=[stat.b, C['cb'].b], writes=[stat.b])
    bld.op('act', lambda e: e.activation(out=stat[:, 7:8], in_=stat[:, 6:7],
                                         func=AF.Exp, scale=-0.5),
           reads=[stat.b], writes=[stat.b])
    bld.op('dve', lambda e: e.tensor_scalar(out=xout[:, 0:n], in0=xin[:, 0:n],
                                            scalar1=stat[:, 2:3], scalar2=stat[:, 7:8],
                                            op0=ALU.subtract, op1=ALU.mult),
           reads=[xin.b, stat.b], writes=[xout.b])



class In:
    def __init__(self, nc, name, shape, dtype=F32):
        self.t = nc.dram_tensor(name, list(shape), dtype, kind='ExternalInput')
        self.a = self.t.ap()
        self.b = Buf(name)
    def __getitem__(self, idx):
        return self.a[idx]


def _mk_common(nc, stack):
    bld = Builder(nc, stack)
    I = {}
    I['u'] = In(nc, 'u', (T, D))
    for name, shape in INPUT_SPECS:
        I[name] = In(nc, name, shape)
    for name, shape, dt_ in CONST_SPECS:
        I[name] = In(nc, name, shape, BF16 if dt_ == 'bf16' else F32)
    C = {}
    def load_const(cname, iname, shape, dtype):
        sb = SB(bld, cname, list(shape), dtype)
        bld.op('sp', lambda e: e.dma_start(out=sb.ap, in_=I[iname][:, :]),
               reads=[I[iname].b], writes=[sb.b])
        C[cname] = sb
    load_const('ident_f32', 'c_ident_f32', (128, 128), F32)
    load_const('ident_bf16', 'c_ident_bf16', (128, 128), BF16)
    load_const('tri_f32', 'c_tri_f32', (128, 128), F32)
    load_const('triu_bf16', 'c_triu_bf16', (128, 128), BF16)
    load_const('ones_f32', 'c_ones_f32', (128, 128), F32)
    load_const('maskb_f32', 'c_maskb_f32', (128, 128), F32)
    cb = SB(bld, 'c_bias', [128, 4], F32)
    bld.op('dve', lambda e: e.memset(cb[:, 0:1], 1e-5), reads=[], writes=[cb.b])
    bld.op('dve', lambda e: e.memset(cb[:, 1:2], 1.0), reads=[], writes=[cb.b])
    bld.op('dve', lambda e: e.memset(cb[:, 2:3], 0.5), reads=[], writes=[cb.b])
    C['cb'] = cb
    C['p'] = [PS(bld, f'p{i}', [128, 512], F32) for i in range(6)]
    C['pbx'] = PS(bld, 'pbx', [128, 2048], BF16)
    return bld, I, C


def mk_dm(nc):
    return {
        'xres': DR(nc, 'd_xres', [T, D], F32),
        'xbf': DR(nc, 'd_xbf', [T, D], BF16),
        'z': DR(nc, 'd_z', [T, DIN], F32),
        'xbc': DR(nc, 'd_xbc', [CONVD, T], F32),
        'xc': DR(nc, 'd_xc', [CONVD, T], BF16),
        'acs': DR(nc, 'd_acs', [NCH, NH, CH], F32),
        'wqg': DR(nc, 'd_wqg', [4096, 1024], BF16),
        'wqu': DR(nc, 'd_wqu', [4096, 1024], BF16),
        'wqd': DR(nc, 'd_wqd', [1024, 4096], BF16),
        'xq': DR(nc, 'd_xq', [T, 1024], BF16),
        'sg': DR(nc, 'd_sg', [T, 4096], F32),
        'oB': DR(nc, 'd_oB', [T, 4096], BF16),
        'q': DR(nc, 'd_q', [T, 1024], F32),
        'k': DR(nc, 'd_k', [T, 1024], F32),
        'v': DR(nc, 'd_v', [T, 1024], BF16),
        'att': DR(nc, 'd_att', [T, 1024], BF16),
        'wqu2': DR(nc, 'd_wqu2', [3072, 1024], BF16),
        'wqg2': DR(nc, 'd_wqg2', [3072, 1024], BF16),
        'wqd2': DR(nc, 'd_wqd2', [1024, 3072], BF16),
        'up': DR(nc, 'd_up', [T, 3072], F32),
        'gt': DR(nc, 'd_gt', [T, 3072], F32),
        'oD': DR(nc, 'd_oD', [T, 3072], BF16),
        'ccin': DR(nc, 'd_ccin', [1, 8], F32),
        'ccout': DR(nc, 'd_ccout', [1, 8], F32),
    }


def build_program(stages=('a', 'b', 'c', 'd')):
    nc = bass.Bass('TRN2', target_bir_lowering=False)
    stack = contextlib.ExitStack()
    bld, I, C = _mk_common(nc, stack)
    y = DR(nc, 'y', [T, D], F32, kind='ExternalOutput')

    Dm = mk_dm(nc)

    cur = I['u']
    if 'a' not in stages:
        xtmp = SB(bld, 'init_x', [128, 1024], F32)
        for tc in range(32):
            r = slice(tc * 128, (tc + 1) * 128)
            bld.op('sp', lambda e, r=r: e.dma_start(out=xtmp.ap, in_=I['u'][r, :]),
                   reads=[I['u'].b], writes=[xtmp.b])
            bld.op('poold', lambda e, r=r: e.dma_start(out=Dm['xbf'][r, :], in_=xtmp.ap),
                   reads=[xtmp.b], writes=[Dm['xbf'].b])
    if 'a' in stages:
        out = y if stages[-1] == 'a' else Dm['xres']
        emit_stage_a(bld, C, I, Dm, out)
        cur = out
    if 'b' in stages:
        out = y if stages[-1] == 'b' else Dm['xres']
        emit_stage_b(bld, C, I, Dm, cur, out)
        cur = out
    if 'c' in stages:
        out = y if stages[-1] == 'c' else Dm['xres']
        emit_stage_c(bld, C, I, Dm, cur, out)
        cur = out
    if 'd' in stages:
        emit_stage_d(bld, C, I, Dm, cur, y)
        cur = y
    if cur is not y:
        raise RuntimeError('last stage must write y')

    bld.barrier()

    bld.emit_all()
    return nc, stack


INPUT_SPECS = [
    ('L0_attn_W_in', (1024, 4384)),
    ('L0_attn_conv_w', (2304, 4)),
    ('L0_attn_conv_b', (2304,)),
    ('L0_attn_dt_bias', (32,)),
    ('L0_attn_A_log', (32,)),
    ('L0_attn_norm_w', (2048,)),
    ('L0_attn_W_out', (2048, 1024)),
    ('L0_ffn_w_gate', (4096, 1024)),
    ('L0_ffn_w_up', (4096, 1024)),
    ('L0_ffn_w_down', (1024, 4096)),
    ('L1_attn_Wq', (1024, 1024)),
    ('L1_attn_Wk', (1024, 1024)),
    ('L1_attn_Wv', (1024, 1024)),
    ('L1_attn_Wo', (1024, 1024)),
    ('L1_attn_betas', (1, 16, 1, 64)),
    ('L1_attn_memory0', (1, 16, 64, 64)),
    ('L1_attn_z0', (1, 16, 1, 64)),
    ('L1_ffn_w_gate', (3072, 1024)),
    ('L1_ffn_w_up', (3072, 1024)),
    ('L1_ffn_w_down', (1024, 3072)),
]

CONST_SPECS = [
    ('c_ident_f32', (128, 128), 'f32'),
    ('c_ident_bf16', (128, 128), 'bf16'),
    ('c_tri_f32', (128, 128), 'f32'),
    ('c_triu_bf16', (128, 128), 'bf16'),
    ('c_ones_f32', (128, 128), 'f32'),
    ('c_maskb_f32', (128, 128), 'f32'),
]


def host_constants():
    ii = np.arange(128)
    c = {}
    c['c_ident_f32'] = np.eye(128, dtype=np.float32)
    c['c_ident_bf16'] = np.eye(128, dtype=ml_dtypes.bfloat16)
    c['c_tri_f32'] = (ii[:, None] <= ii[None, :]).astype(np.float32)
    c['c_triu_bf16'] = (ii[:, None] <= ii[None, :]).astype(ml_dtypes.bfloat16)
    c['c_ones_f32'] = np.ones((128, 128), np.float32)
    c['c_maskb_f32'] = np.where(ii[:, None] > ii[None, :], -1e30, 0.0).astype(np.float32)
    return c


def build_launch(li):
    nc = bass.Bass('TRN2', target_bir_lowering=False)
    stack = contextlib.ExitStack()
    bld, I, C = _mk_common(nc, stack)

    def mkin(name, shape, dtype=F32):
        return In(nc, name, shape, dtype)
    def mkout(name, shape, dtype=F32):
        return DR(nc, name, list(shape), dtype, kind='ExternalOutput')

    stat = SB(bld, 'ml_stat', [128, 8], F32)
    st1 = SB(bld, 'ml_st1', [1, 8], F32)
    gsc = SB(bld, 'ml_gsc', [1, 8], F32)
    bcs = SB(bld, 'ml_bcs', [128, 16], F32)
    p5 = C['p'][5]

    if li == 1:
        y = mkout('y', (T, D))
        mx1 = mkout('mx1', (1, 1))
        Dm = mk_dm(nc)
        wqs = emit_b_wq(bld, C, I, Dm)
        emit_stage_a(bld, C, I, Dm, Dm['xres'])
        emit_stage_b(bld, C, I, Dm, Dm['xres'], Dm['xres'], wqs)
        emit_stage_c(bld, C, I, Dm, Dm['xres'], y)
        emit_d_maxx(bld, C, Dm, y, st1, stat)
        gmax_out(bld, st1[0:1, 0:1], st1.b, 1, mx1)
    elif li == 2:
        xres = mkin('xres_i', (T, D))
        gmx1 = mkin('gmx1', (1, 8))
        up_o = mkout('up_o', (T, 3072))
        gt_o = mkout('gt_o', (T, 3072))
        m2 = mkout('m2', (1, 2))
        Dm = mk_dm(nc)
        gmax_in(bld, C, gmx1, 1, gsc, 0)
        emit_d_scales(bld, C, p5, gsc, 0, 0, 1, bcs)
        emit_d_wq(bld, C, I, Dm, [(I['L1_ffn_w_up'], Dm['wqu2'], 24, 1024),
                                  (I['L1_ffn_w_gate'], Dm['wqg2'], 24, 1024)])
        emit_d_xq(bld, C, Dm, xres, bcs)
        emit_d_upgate(bld, C, Dm, bcs, st1, stat, up_o, gt_o)
        gmax_out(bld, st1[0:1, 0:2], st1.b, 2, m2)
    elif li == 3:
        gt_i = mkin('gt_i', (T, 3072))
        gmx2 = mkin('gmx2', (2, 8))
        g2_o = mkout('g2_o', (T, 3072))
        m3 = mkout('m3', (1, 1))
        gmax_in(bld, C, gmx2, 2, gsc, 0)
        emit_d_scales(bld, C, p5, gsc, 1, 4, 5, bcs)
        emit_d_g2raw(bld, C, bcs, st1, stat, gt_i, g2_o)
        gmax_out(bld, st1[0:1, 0:1], st1.b, 1, m3)
    elif li == 4:
        g2_i = mkin('g2_i', (T, 3072))
        up_i = mkin('up_i', (T, 3072))
        gmx2 = mkin('gmx2', (2, 8))
        gmx3 = mkin('gmx3', (1, 8))
        or_o = mkout('or_o', (T, 3072))
        m4 = mkout('m4', (1, 1))
        gmax_in(bld, C, gmx2, 2, gsc, 0)
        gmax_in(bld, C, gmx3, 1, gsc, 2)
        emit_d_scales(bld, C, p5, gsc, 0, 2, 3, bcs)
        emit_d_scales(bld, C, p5, gsc, 2, 6, 7, bcs)
        emit_d_oraw(bld, C, bcs, st1, stat, g2_i, up_i, or_o)
        gmax_out(bld, st1[0:1, 0:1], st1.b, 1, m4)
    elif li == 5:
        or_i = mkin('or_i', (T, 3072))
        xres = mkin('xres_i', (T, D))
        gmx4 = mkin('gmx4', (1, 8))
        y = mkout('y', (T, D))
        Dm = mk_dm(nc)
        gmax_in(bld, C, gmx4, 1, gsc, 0)
        emit_d_scales(bld, C, p5, gsc, 0, 8, 9, bcs)
        emit_d_wq(bld, C, I, Dm, [(I['L1_ffn_w_down'], Dm['wqd2'], 8, 3072)])
        emit_d_down(bld, C, Dm, bcs, stat, st1, xres, or_i, y)
    bld.barrier()
    bld.emit_all()
    return nc, stack


LAUNCH_MODE = 'multi'


def kernel(**inputs):
    stages = inputs.pop('_stages', None)
    mode = inputs.pop('_mode', LAUNCH_MODE)
    consts = host_constants()
    u_full = np.ascontiguousarray(inputs['L0_attn_u'], dtype=np.float32)
    base = {}
    for name, _ in INPUT_SPECS:
        base[name] = np.ascontiguousarray(inputs[name], dtype=np.float32)
    base.update(consts)

    def run(nc, stack, extra_per_core):
        in_maps = []
        for core in range(NCORES):
            m = dict(base)
            m['u'] = u_full[core]
            m.update(extra_per_core[core])
            in_maps.append(m)
        try:
            res = run_bass_kernel_spmd(nc, in_maps, core_ids=list(range(NCORES)))
        finally:
            stack.close()
        return res.results

    if stages is not None or mode == 'fused':
        nc, stack = build_program(stages or ('a', 'b', 'c', 'd'))
        res = run(nc, stack, [{} for _ in range(NCORES)])
        return np.stack([np.asarray(r['y']) for r in res], 0).astype(np.float32)

    nc, stack = build_launch(1)
    r1 = run(nc, stack, [{} for _ in range(NCORES)])
    xres = [np.asarray(r['y']) for r in r1]
    gmx1 = np.stack([np.asarray(r['mx1'])[0] for r in r1], 1)

    nc, stack = build_launch(2)
    r2 = run(nc, stack, [{'xres_i': xres[c], 'gmx1': gmx1} for c in range(NCORES)])
    up = [np.asarray(r['up_o']) for r in r2]
    gt = [np.asarray(r['gt_o']) for r in r2]
    gmx2 = np.stack([np.asarray(r['m2'])[0] for r in r2], 1)

    nc, stack = build_launch(3)
    r3 = run(nc, stack, [{'gt_i': gt[c], 'gmx2': gmx2} for c in range(NCORES)])
    g2 = [np.asarray(r['g2_o']) for r in r3]
    gmx3 = np.stack([np.asarray(r['m3'])[0] for r in r3], 1)

    nc, stack = build_launch(4)
    r4 = run(nc, stack, [{'g2_i': g2[c], 'up_i': up[c], 'gmx2': gmx2,
                          'gmx3': gmx3} for c in range(NCORES)])
    orr = [np.asarray(r['or_o']) for r in r4]
    gmx4 = np.stack([np.asarray(r['m4'])[0] for r in r4], 1)

    nc, stack = build_launch(5)
    r5 = run(nc, stack, [{'or_i': orr[c], 'xres_i': xres[c], 'gmx4': gmx4}
                         for c in range(NCORES)])
    return np.stack([np.asarray(r['y']) for r in r5], 0).astype(np.float32)


def emit_b_wq(bld, C, I, Dm):
    p = C['p']
    cb = C['cb']
    minv = SB(bld, 'b_minv', [128, 4], F32)
    scw = SB(bld, 'b_scw', [128, 4], F32)
    WQ = [(I['L0_ffn_w_gate'], Dm['wqg'], 32, 1024, 0),
          (I['L0_ffn_w_up'], Dm['wqu'], 32, 1024, 1),
          (I['L0_ffn_w_down'], Dm['wqd'], 8, 4096, 2)]
    bld.push_scope()
    wld = SB(bld, 'b_wld', [128, 4096], F32)
    wsum = SB(bld, 'b_wsum', [128, 32], F32)
    wq_t = SB(bld, 'b_wqt', [128, 4096], F32)
    wq_i = SB(bld, 'b_wqi', [128, 4096], I32)
    wq_b = SB(bld, 'b_wqb', [128, 4096], BF16)
    tot = SB(bld, 'b_tot', [128, 1], F32)
    sc1 = SB(bld, 'b_sc1', [1, 4], F32)
    for (W, dq, nr, ncol, wi) in WQ:
        for r in range(nr):
            bld.op('sp', lambda e, W=W, r=r, ncol=ncol: e.dma_start(
                       out=wld[:, 0:ncol], in_=W[r * 128:(r + 1) * 128, :]),
                   reads=[W.b], writes=[wld.b])
            bld.op('dve', lambda e, r=r, ncol=ncol: e.tensor_reduce(
                       out=wsum[:, r:r + 1], in_=wld[:, 0:ncol], axis=AX.X,
                       op=ALU.add, apply_absolute_value=True),
                   reads=[wld.b], writes=[wsum.b])
        bld.op('dve', lambda e, nr=nr: e.tensor_reduce(out=tot.ap, in_=wsum[:, 0:nr],
                                                       axis=AX.X, op=ALU.add),
               reads=[wsum.b], writes=[tot.b])
        part_reduce(bld, C, p[5], tot.ap, tot.b, sc1, sc1[0:1, 0:1], ALU.add)
        bld.op('dve', lambda e, nr=nr, ncol=ncol: e.tensor_scalar(
                   out=sc1[0:1, 1:2], in0=sc1[0:1, 0:1],
                   scalar1=1.0 / (nr * 128.0 * ncol), scalar2=1e-5,
                   op0=ALU.mult, op1=ALU.max),
               reads=[sc1.b], writes=[sc1.b])
        bld.op('dve', lambda e: e.reciprocal(out=sc1[0:1, 2:3], in_=sc1[0:1, 1:2]),
               reads=[sc1.b], writes=[sc1.b])
        bc_row(bld, C, p[5], (minv, minv[:, wi:wi + 1]), sc1[0:1, 1:2], 1, [sc1.b])
        bc_row(bld, C, p[5], (scw, scw[:, wi:wi + 1]), sc1[0:1, 2:3], 1, [sc1.b])
        for r in range(nr):
            bld.op('sp', lambda e, W=W, r=r, ncol=ncol: e.dma_start(
                       out=wld[:, 0:ncol], in_=W[r * 128:(r + 1) * 128, :]),
                   reads=[W.b], writes=[wld.b])
            bld.op('act', lambda e, wi=wi, ncol=ncol: e.activation(
                       out=wq_t[:, 0:ncol], in_=wld[:, 0:ncol], func=AF.Identity,
                       scale=scw[:, wi:wi + 1]),
                   reads=[wld.b, scw.b], writes=[wq_t.b])
            bld.op('dve', lambda e, ncol=ncol: e.tensor_copy(
                       out=wq_i[:, 0:ncol], in_=wq_t[:, 0:ncol]),
                   reads=[wq_t.b], writes=[wq_i.b])
            bld.op('dve', lambda e, ncol=ncol: e.tensor_scalar(
                       out=wq_b[:, 0:ncol], in0=wq_i[:, 0:ncol], scalar1=-1.0,
                       scalar2=1.0, op0=ALU.max, op1=ALU.min),
                   reads=[wq_i.b], writes=[wq_b.b])
            bld.op('actd', lambda e, dq=dq, r=r, ncol=ncol: e.dma_start(
                       out=dq[r * 128:(r + 1) * 128, :], in_=wq_b[:, 0:ncol]),
                   reads=[wq_b.b], writes=[dq.b])
    bld.pop_scope()

    return minv, scw


def emit_stage_b(bld, C, I, Dm, cur, out_dram, wqs=None):
    nc = bld.nc
    if wqs is None:
        wqs = emit_b_wq(bld, C, I, Dm)
    minv, scw = wqs
    bld.push_scope()
    p = C['p']
    ident_f = C['ident_f32']
    ones = C['ones_f32']
    cb = C['cb']

    invx = SB(bld, 'b_invx', [128, 32], F32)
    invgu = SB(bld, 'b_invgu', [128, 32], F32)
    cmb = SB(bld, 'b_cmb', [128, 3 * 32], F32)
    stat = SB(bld, 'b_stat', [128, 8], F32)
    st1 = SB(bld, 'b_st1', [1, 8], F32)

    bld.push_scope()
    xtR = Rot(bld, 'b_xt', [128, 1024], F32, 3)
    qtR = Rot(bld, 'b_qt', [128, 1024], F32, 2)
    qiR = Rot(bld, 'b_qi', [128, 1024], I32, 2)
    qbR = Rot(bld, 'b_qb', [128, 1024], BF16, 2)
    mcolR = Rot(bld, 'b_mcol', [128, 4], F32, 2)
    def b0_body(tc, xt, qt, qi, qb, mcol):
        rtok = slice(tc * 128, (tc + 1) * 128)
        bld.op('actd', lambda e: e.dma_start(out=xt.ap, in_=cur[rtok, :]),
               reads=[cur.b], writes=[xt.b])
        bld.op('dve', lambda e: e.tensor_reduce(out=mcol[:, 0:1], in_=xt.ap, axis=AX.X,
                                                op=ALU.max, apply_absolute_value=True),
               reads=[xt.b], writes=[mcol.b])
        bld.op('dve', lambda e: e.tensor_scalar(out=mcol[:, 1:2], in0=mcol[:, 0:1],
                                                scalar1=1e-5, scalar2=None, op0=ALU.max),
               reads=[mcol.b], writes=[mcol.b])
        bld.op('dve', lambda e: e.reciprocal(out=mcol[:, 2:3], in_=mcol[:, 1:2]),
               reads=[mcol.b], writes=[mcol.b])
        bld.op('dve', lambda e: e.tensor_scalar(out=mcol[:, 3:4], in0=mcol[:, 2:3],
                                                scalar1=127.0, scalar2=None, op0=ALU.mult),
               reads=[mcol.b], writes=[mcol.b])
        bld.op('dve', lambda e: e.tensor_scalar(out=invx[:, tc:tc + 1],
                                                in0=mcol[:, 1:2],
                                                scalar1=1.0 / 127.0, scalar2=None,
                                                op0=ALU.mult),
               reads=[mcol.b], writes=[invx.b])
        bld.op('act', lambda e: e.activation(out=qt.ap, in_=xt.ap, func=AF.Identity,
                                             scale=mcol[:, 3:4]),
               reads=[xt.b, mcol.b], writes=[qt.b])
        bld.op('pool', lambda e: e.tensor_copy(out=qi.ap, in_=qt.ap),
               reads=[qt.b], writes=[qi.b])
        bld.op('dve', lambda e: e.tensor_copy(out=qb.ap, in_=qi.ap),
               reads=[qi.b], writes=[qb.b])
        bld.op('actd', lambda e: e.dma_start(out=Dm['xq'][rtok, :], in_=qb.ap),
               reads=[qb.b], writes=[Dm['xq'].b])
    for tc in range(32):
        b0_body(tc, xtR(tc), qtR(tc), qiR(tc), qbR(tc), mcolR(tc))
    for wi in range(2):
        bld.op('dve', lambda e, wi=wi: e.tensor_tensor(
                   out=cmb.t[:].rearrange('p (w c) -> p w c', c=32)[:, wi, :],
                   in0=invx.ap,
                   in1=minv[:, wi:wi + 1].broadcast_to([128, 32]), op=ALU.mult),
               reads=[invx.b, minv.b], writes=[cmb.b])

    bld.pop_scope()

    bld.push_scope()
    xqf = SB(bld, 'b_xqf', [128, 8 * T], BF16)
    wstr = SB(bld, 'b_wstr', [128, 8 * 4096], BF16)
    guR = Rot(bld, 'b_gu', [128, 4096], F32, 2)
    guiR = Rot(bld, 'b_gui', [128, 2048], I32, 1)
    gubR = Rot(bld, 'b_gub', [128, 2048], BF16, 1)
    sgbR = Rot(bld, 'b_sgb', [128, 512], F32, 2)
    usbR = Rot(bld, 'b_usb', [128, 512], F32, 2)
    statR2 = Rot(bld, 'b_st2', [128, 8], F32, 2)
    xqf_rb = [Buf(f'bxqf{i}') for i in range(4)]
    wstr_rb = [Buf(f'bwstr{i}') for i in range(4)]
    striped_load(bld, xqf_rb,
                 [(lambda e, j=j: e.dma_start(
                       out=xqf[:, j * T:(j + 1) * T],
                       in_=Dm['xq'][:, j * 128:(j + 1) * 128], transpose=True))
                  for j in range(8)],
                 reads=[Dm['xq'].b])
    def b12_body(phase, tc, gu, gui, gub, stat_):
        rtok = slice(tc * 128, (tc + 1) * 128)
        for og in range(2):
            def gmm(e, tc=tc, og=og):
                last = None
                for i in range(8):
                    e.ldweights(xqf[:, i * T + tc * 128:i * T + (tc + 1) * 128])
                    for op_ in range(4):
                        last = e.matmul(
                            out=p[op_][0:128, 0:512],
                            lhsT=xqf[:, i * T + tc * 128:i * T + (tc + 1) * 128],
                            rhs=wstr[:, i * 4096 + og * 2048 + op_ * 512:
                                     i * 4096 + og * 2048 + (op_ + 1) * 512],
                            start=(i == 0), stop=(i == 7))
                return last
            bld.op('pe', gmm, reads=[*xqf_rb, *wstr_rb],
                   writes=[p[0].b, p[1].b, p[2].b, p[3].b])
            for op_ in range(4):
                ocol = og * 2048 + op_ * 512
                psl = p[op_][0:128, 0:512]
                pbf = p[op_].b
                if phase == 0:
                    sgb = sgbR(tc * 4 + op_)
                    bld.op('act', lambda e, psl=psl, sgb=sgb: e.activation(
                               out=sgb.ap, in_=psl, func=AF.Silu,
                               scale=cmb[:, tc:tc + 1]),
                           reads=[pbf, cmb.b], writes=[sgb.b])
                    bld.op('actd', lambda e, rtok=rtok, ocol=ocol, sgb=sgb: e.dma_start(
                               out=Dm['sg'][rtok, ocol:ocol + 512], in_=sgb.ap),
                           reads=[sgb.b], writes=[Dm['sg'].b])
                else:
                    usb = usbR(tc * 4 + op_)
                    sgb = sgbR(tc * 4 + op_)
                    bld.op('act', lambda e, psl=psl, usb=usb: e.activation(
                               out=usb.ap, in_=psl, func=AF.Identity,
                               scale=cmb[:, 32 + tc:32 + tc + 1]),
                           reads=[pbf, cmb.b], writes=[usb.b])
                    bld.op('actd', lambda e, rtok=rtok, ocol=ocol, sgb=sgb: e.dma_start(
                               out=sgb.ap, in_=Dm['sg'][rtok, ocol:ocol + 512]),
                           reads=[Dm['sg'].b], writes=[sgb.b])
                    bld.op('dve', lambda e, ocol=ocol, usb=usb, sgb=sgb: e.tensor_tensor(
                               out=gu[:, ocol:ocol + 512], in0=usb.ap, in1=sgb.ap,
                               op=ALU.mult),
                           reads=[usb.b, sgb.b], writes=[gu.b])
        if phase == 1:
            bld.op('dve', lambda e: e.tensor_reduce(out=stat_[:, 0:1], in_=gu.ap,
                                                    axis=AX.X, op=ALU.max,
                                                    apply_absolute_value=True),
                   reads=[gu.b], writes=[stat_.b])
            bld.op('dve', lambda e: e.tensor_scalar(out=stat_[:, 1:2], in0=stat_[:, 0:1],
                                                    scalar1=1e-5, scalar2=None,
                                                    op0=ALU.max),
                   reads=[stat_.b], writes=[stat_.b])
            bld.op('dve', lambda e: e.reciprocal(out=stat_[:, 2:3], in_=stat_[:, 1:2]),
                   reads=[stat_.b], writes=[stat_.b])
            bld.op('dve', lambda e: e.tensor_scalar(out=stat_[:, 3:4], in0=stat_[:, 2:3],
                                                    scalar1=127.0, scalar2=None,
                                                    op0=ALU.mult),
                   reads=[stat_.b], writes=[stat_.b])
            bld.op('dve', lambda e: e.tensor_scalar(
                       out=invgu[:, tc:tc + 1], in0=stat_[:, 1:2],
                       scalar1=1.0 / 127.0, scalar2=None, op0=ALU.mult),
                   reads=[stat_.b], writes=[invgu.b])
            bld.op('act', lambda e: e.activation(out=gu.ap, in_=gu.ap, func=AF.Identity,
                                                 scale=stat_[:, 3:4]),
                   reads=[gu.b, stat_.b], writes=[gu.b])
            for hf in range(2):
                hs = slice(hf * 2048, (hf + 1) * 2048)
                bld.op('pool', lambda e, hs=hs: e.tensor_copy(out=gui.ap,
                           in_=gu[:, hs]),
                       reads=[gu.b], writes=[gui.b])
                bld.op('dve', lambda e: e.tensor_copy(out=gub.ap, in_=gui.ap),
                       reads=[gui.b], writes=[gub.b])
                bld.op('poold', lambda e, rtok=rtok, hs=hs: e.dma_start(
                           out=Dm['oB'][rtok, hs], in_=gub.ap),
                       reads=[gub.b], writes=[Dm['oB'].b])

    for phase in range(2):
        dq = Dm['wqg'] if phase == 0 else Dm['wqu']
        striped_load(bld, wstr_rb,
                     [(lambda e, j=j, dq=dq: e.dma_start(
                           out=wstr[:, j * 4096:(j + 1) * 4096],
                           in_=dq[:, j * 128:(j + 1) * 128], transpose=True))
                      for j in range(8)],
                     reads=[dq.b])
        for tc in range(32):
            b12_body(phase, tc, guR(tc), guiR(tc), gubR(tc), statR2(tc))
    bld.pop_scope()

    bld.push_scope()
    ostr = SB(bld, 'b_ostr', [128, 32 * 1024], BF16)
    wdt = SB(bld, 'b_wdt', [128, 32 * 1024], BF16)
    cmbd = SB(bld, 'b_cmbd', [128, 32], F32)
    xres = SB(bld, 'b_xr', [128, 1024], F32)
    xnew = SB(bld, 'b_xn', [128, 1024], F32)
    xout = SB(bld, 'b_xo', [128, 1024], F32)
    bld.op('dve', lambda e: e.tensor_tensor(
               out=cmbd.ap, in0=invgu.ap,
               in1=minv[:, 2:3].broadcast_to([128, 32]), op=ALU.mult),
           reads=[invgu.b, minv.b], writes=[cmbd.b])

    wdt_rb = [Buf(f'bwdt{g}') for g in range(8)]
    ostr_rb = [Buf(f'bostr{g}') for g in range(8)]
    striped_load(bld, wdt_rb,
                 [(lambda e, k=k: e.dma_start(
                       out=wdt[:, k * 1024:(k + 1) * 1024],
                       in_=Dm['wqd'][:, k * 128:(k + 1) * 128], transpose=True))
                  for k in range(32)],
                 reads=[Dm['wqd'].b])
    for w in range(4):
        striped_load(bld, ostr_rb,
                     [(lambda e, w=w, k=k: e.dma_start(
                           out=ostr[:, k * 1024:(k + 1) * 1024],
                           in_=Dm['oB'][w * 1024:(w + 1) * 1024,
                                        k * 128:(k + 1) * 128],
                           transpose=True))
                      for k in range(32)],
                     reads=[Dm['oB'].b])
        for tci in range(8):
            tc = w * 8 + tci
            rtok = slice(tc * 128, (tc + 1) * 128)
            def dmm(e, tci=tci):
                last = None
                for k in range(32):
                    e.ldweights(ostr[:, k * 1024 + tci * 128:k * 1024 + (tci + 1) * 128])
                    for np_ in range(2):
                        last = e.matmul(
                            out=p[np_][0:128, 0:512],
                            lhsT=ostr[:, k * 1024 + tci * 128:k * 1024 + (tci + 1) * 128],
                            rhs=wdt[:, k * 1024 + np_ * 512:k * 1024 + (np_ + 1) * 512],
                            start=(k == 0), stop=(k == 31))
                return last
            bld.op('pe', dmm, reads=[*ostr_rb, *wdt_rb], writes=[p[0].b, p[1].b])
            bld.op('actd', lambda e, rtok=rtok: e.dma_start(out=xres.ap, in_=cur[rtok, :]),
                   reads=[cur.b], writes=[xres.b])
            for np_ in range(2):
                bld.op('dve', lambda e, np_=np_, tc=tc: e.tensor_scalar(
                           out=xnew[:, np_ * 512:(np_ + 1) * 512],
                           in0=p[np_][0:128, 0:512], scalar1=cmbd[:, tc:tc + 1],
                           scalar2=None, op0=ALU.mult),
                       reads=[p[np_].b, cmbd.b], writes=[xnew.b])
            bld.op('dve', lambda e: e.tensor_tensor(out=xnew.ap, in0=xnew.ap, in1=xres.ap,
                                                    op=ALU.add),
                   reads=[xnew.b, xres.b], writes=[xnew.b])
            emit_ln(bld, C, xnew, xout, stat, st1, 1024, p[5])
            bld.op('actd', lambda e, rtok=rtok: e.dma_start(out=out_dram[rtok, :],
                                                            in_=xout.ap),
                   reads=[xout.b], writes=[out_dram.b])
            bld.op('poold', lambda e, rtok=rtok: e.dma_start(out=Dm['xbf'][rtok, :],
                                                             in_=xout.ap),
                   reads=[xout.b], writes=[Dm['xbf'].b])
    bld.pop_scope()
    bld.pop_scope()

def emit_stage_c(bld, C, I, Dm, cur, out_dram):
    nc = bld.nc
    bld.push_scope()
    p = C['p']
    pbx = C['pbx']
    ident_f = C['ident_f32']
    ident_b = C['ident_bf16']
    maskb = C['maskb_f32']
    ones = C['ones_f32']
    cb = C['cb']
    H, DK, DV, LSEG, NSEG = 16, 64, 64, 512, 8

    stat = SB(bld, 'c_stat', [128, 8], F32)
    st1 = SB(bld, 'c_st1', [1, 8], F32)
    xres = SB(bld, 'c_xr', [128, 1024], F32)
    xnew = SB(bld, 'c_xn', [128, 1024], F32)
    xout = SB(bld, 'c_xo', [128, 1024], F32)

    bld.push_scope()
    xf = SB(bld, 'c_xf', [128, 8 * T], BF16)
    wpj = SB(bld, 'c_wpj', [128, 8 * 1024], BF16)
    evR = Rot(bld, 'c_ev', [128, 512], F32, 3)
    evbR = Rot(bld, 'c_evb', [128, 512], BF16, 3)
    xf_rb = [Buf(f'cxf{i}') for i in range(4)]
    striped_load(bld, xf_rb,
                 [(lambda e, j=j: e.dma_start(
                       out=xf[:, j * T:(j + 1) * T],
                       in_=Dm['xbf'][:, j * 128:(j + 1) * 128], transpose=True))
                  for j in range(8)],
                 reads=[Dm['xbf'].b])
    for (W, dst, isbf) in [(I['L1_attn_Wq'], Dm['q'], False),
                           (I['L1_attn_Wk'], Dm['k'], False),
                           (I['L1_attn_Wv'], Dm['v'], True)]:
        wa = W.a.rearrange('(k p) m -> p k m', p=128)
        for k in range(8):
            bld.op('poold', lambda e, k=k, wa=wa: e.dma_start(
                       out=wpj[:, k * 1024:(k + 1) * 1024], in_=wa[:, k, :]),
                   reads=[W.b], writes=[wpj.b])
        for tc in range(32):
            rtok = slice(tc * 128, (tc + 1) * 128)
            pb = 0 if tc % 2 == 0 else 2
            def pmm(e, tc=tc, pb=pb):
                last = None
                for k in range(8):
                    e.ldweights(xf[:, k * T + tc * 128:k * T + (tc + 1) * 128])
                    for n in range(2):
                        last = e.matmul(
                            out=p[pb + n][0:128, 0:512],
                            lhsT=xf[:, k * T + tc * 128:k * T + (tc + 1) * 128],
                            rhs=wpj[:, k * 1024 + n * 512:k * 1024 + (n + 1) * 512],
                            start=(k == 0), stop=(k == 7))
                return last
            bld.op('pe', pmm, reads=[*xf_rb, wpj.b], writes=[p[pb].b, p[pb + 1].b])
            for n in range(2):
                ebuf = (evbR if isbf else evR)(tc * 2 + n)
                if isbf:
                    bld.op('act', lambda e, n=n, pb=pb, ebuf=ebuf: e.activation(out=ebuf.ap,
                               in_=p[pb + n][0:128, 0:512], func=AF.Identity),
                           reads=[p[pb + n].b], writes=[ebuf.b])
                else:
                    bld.op('act', lambda e, n=n, pb=pb, ebuf=ebuf: e.copy(out=ebuf.ap,
                               in_=p[pb + n][0:128, 0:512]),
                           reads=[p[pb + n].b], writes=[ebuf.b])
                bld.op('actd', lambda e, rtok=rtok, n=n, dst=dst, ebuf=ebuf: e.dma_start(
                           out=dst[rtok, n * 512:(n + 1) * 512], in_=ebuf.ap),
                       reads=[ebuf.b], writes=[dst.b])
    bld.pop_scope()

    bld.push_scope()
    gb = SB(bld, 'c_gb', [128, 1024], F32)
    MZ = SB(bld, 'c_MZ', [64, H * 65], F32)
    MZb = SB(bld, 'c_MZb', [64, H * 65], BF16)
    qblkR = Rot(bld, 'c_qblk', [128, 256], F32, 2)
    kblkR = Rot(bld, 'c_kblk', [128, 256], F32, 2)
    v1R = Rot(bld, 'c_v1', [128, 4 * 65], BF16, 2)
    sqR = Rot(bld, 'c_sq', [128, 256], F32, 2)
    skR = Rot(bld, 'c_sk', [128, 256], F32, 2)
    tm1R = Rot(bld, 'c_tm1', [128, 256], F32, 2)
    qbbR = Rot(bld, 'c_qbb', [128, 256], BF16, 2)
    sqbR = Rot(bld, 'c_sqb', [128, 256], BF16, 2)
    kbbR = Rot(bld, 'c_kbb', [128, 256], BF16, 2)
    skbR = Rot(bld, 'c_skb', [128, 256], BF16, 2)
    fmbR = Rot(bld, 'c_fmb', [64, 1536], BF16, 2)
    sdgR = Rot(bld, 'c_sdg', [128, 512], F32, 2)
    mxR = Rot(bld, 'c_mx', [128, 8], F32, 2)
    m1R = Rot(bld, 'c_m1', [1, 8], F32, 2)
    negmR = Rot(bld, 'c_negm', [128, 1], F32, 2)
    ptallR = Rot(bld, 'c_pt', [128, 4 * 512], BF16, 2)
    rcR = Rot(bld, 'c_rc', [128, 8], F32, 2)
    amnR = Rot(bld, 'c_amn', [128, 256], F32, 2)
    adnR = Rot(bld, 'c_adn', [128, 256], F32, 2)
    attR = Rot(bld, 'c_att', [128, 256], F32, 2)
    bet1 = SB(bld, 'c_bet1', [1, 1024], F32)

    bld.op('sp', lambda e: e.dma_start(out=bet1.ap,
               in_=I['L1_attn_betas'].a.rearrange('o h q d -> o (h q d)')),
           reads=[I['L1_attn_betas'].b], writes=[bet1.b])
    bld.op('act', lambda e: e.activation(out=bet1.ap, in_=bet1.ap, func=AF.Sigmoid),
           reads=[bet1.b], writes=[bet1.b])
    for n in range(2):
        bld.op('pe', lambda e, n=n: e.matmul(out=p[5][0:128, 0:512],
                   lhsT=ones[0:1, 0:128], rhs=bet1[0:1, n * 512:(n + 1) * 512],
                   start=True, stop=True),
               reads=[ones.b, bet1.b], writes=[p[5].b])
        bld.op('dve', lambda e, n=n: e.tensor_copy(out=gb[:, n * 512:(n + 1) * 512],
                   in_=p[5][0:128, 0:512]),
               reads=[p[5].b], writes=[gb.b])
    bld.op('sp', lambda e: e.dma_start(
               out=MZ.t[:].rearrange('p (h c) -> p h c', c=65)[:, :, 0:64],
               in_=I['L1_attn_memory0'].a.rearrange('o h d v -> d (o h) v')),
           reads=[I['L1_attn_memory0'].b], writes=[MZ.b])
    z16 = SB(bld, 'c_z16', [16, 64], F32)
    bld.op('sp', lambda e: e.dma_start(out=z16.ap,
               in_=I['L1_attn_z0'].a.rearrange('o h q d -> (o h q) d')),
           reads=[I['L1_attn_z0'].b], writes=[z16.b])
    bld.op('pe', lambda e: e.transpose(out=p[5][0:64, 0:16], in_=z16.ap,
               identity=ident_f[0:16, 0:16]),
           reads=[z16.b, ident_f.b], writes=[p[5].b])
    bld.op('dve', lambda e: e.tensor_copy(
               out=MZ.t[:].rearrange('p (h c) -> p h c', c=65)[:, :, 64:65],
               in_=p[5][0:64, 0:16].rearrange('p (h o) -> p h o', o=1)),
           reads=[p[5].b], writes=[MZ.b])
    bld.op('pool', lambda e: e.tensor_copy(out=MZb.ap, in_=MZ.ap),
           reads=[MZ.b], writes=[MZb.b])
    for vv1 in v1R.sbs:
        for c in range(4):
            bld.op('pool', lambda e, c=c, vv1=vv1: e.memset(vv1[:, c * 65 + 64:c * 65 + 65], 1.0),
                   reads=[], writes=[vv1.b])

    qv = Dm['q'].a.rearrange('(b t) (a d) -> b (t a) d', t=32, d=64)
    kv = Dm['k'].a.rearrange('(b t) (a d) -> b (t a) d', t=32, d=64)
    vv = Dm['v'].a.rearrange('(b t) (a d) -> b (t a) d', t=32, d=64)
    av = Dm['att'].a.rearrange('(b t) (a d) -> b (t a) d', t=32, d=64)

    def c_body(s, h, qblk, kblk, v1, sq, sk, tm1, qbb, sqb, kbb, skb, fmb,
               sdg, mx, m1, negm, ptall, rc, amn, adn, att):
        if True:
            base = s * H + h
            hc = slice(h * 65, (h + 1) * 65)
            bld.op('sp', lambda e, base=base: e.dma_start(
                       out=qblk.t[:].rearrange('p (c d) -> p c d', d=64),
                       in_=qv[base].rearrange('(c p) d -> p c d', p=128)),
                   reads=[Dm['q'].b], writes=[qblk.b])
            bld.op('sp', lambda e, base=base: e.dma_start(
                       out=kblk.t[:].rearrange('p (c d) -> p c d', d=64),
                       in_=kv[base].rearrange('(c p) d -> p c d', p=128)),
                   reads=[Dm['k'].b], writes=[kblk.b])
            bld.op('actd', lambda e, base=base: e.dma_start(
                       out=v1.t[:].rearrange('p (c e) -> p c e', e=65)[:, :, 0:64],
                       in_=vv[base].rearrange('(c p) d -> p c d', p=128)),
                   reads=[Dm['v'].b], writes=[v1.b])
            for (blk, so, sb_, bb) in [(qblk, sq, sqb, qbb), (kblk, sk, skb, kbb)]:
                bld.op('dve', lambda e, blk=blk: e.tensor_scalar(
                           out=tm1.ap, in0=blk.ap, scalar1=0.0, scalar2=None,
                           op0=ALU.min),
                       reads=[blk.b], writes=[tm1.b])
                bld.op('act', lambda e, so=so: e.activation(out=so.ap, in_=tm1.ap,
                           func=AF.Exp),
                       reads=[tm1.b], writes=[so.b])
                bld.op('dve', lambda e, blk=blk, so=so: e.tensor_tensor(
                           out=so.ap, in0=so.ap, in1=blk.ap, op=ALU.add),
                       reads=[so.b, blk.b], writes=[so.b])
                bld.op('dve', lambda e, so=so: e.tensor_tensor(
                           out=so.ap, in0=so.ap, in1=tm1.ap, op=ALU.subtract),
                       reads=[so.b, tm1.b], writes=[so.b])
                bld.op('pool', lambda e, so=so, sb_=sb_: e.tensor_copy(out=sb_.ap,
                           in_=so.ap),
                       reads=[so.b], writes=[sb_.b])
                bld.op('pool', lambda e, blk=blk, bb=bb: e.tensor_copy(out=bb.ap,
                           in_=blk.ap),
                       reads=[blk.b], writes=[bb.b])
            def ctr(e):
                last = None
                for ci in range(4):
                    last = e.transpose(out=pbx[0:64, ci * 128:(ci + 1) * 128],
                                       in_=qbb[:, ci * 64:(ci + 1) * 64],
                                       identity=ident_b.ap)
                    last = e.transpose(out=pbx[0:64, 512 + ci * 128:512 + (ci + 1) * 128],
                                       in_=sqb[:, ci * 64:(ci + 1) * 64],
                                       identity=ident_b.ap)
                    last = e.transpose(out=pbx[0:64, 1024 + ci * 128:1024 + (ci + 1) * 128],
                                       in_=kbb[:, ci * 64:(ci + 1) * 64],
                                       identity=ident_b.ap)
                return last
            bld.op('pe', ctr, reads=[qbb.b, sqb.b, kbb.b, ident_b.b], writes=[pbx.b])
            bld.op('dve', lambda e: e.tensor_copy(out=fmb.ap, in_=pbx[0:64, 0:1536]),
                   reads=[pbx.b], writes=[fmb.b])
            for j in range(4):
                w = 512 - j * 128
                bld.op('pe', lambda e, j=j, w=w: (
                           e.ldweights(fmb[0:64, 1024 + j * 128:1024 + (j + 1) * 128]),
                           e.matmul(out=p[j][0:128, 0:w],
                                    lhsT=fmb[0:64, 1024 + j * 128:1024 + (j + 1) * 128],
                                    rhs=fmb[0:64, j * 128:512],
                                    start=True, stop=True))[-1],
                       reads=[fmb.b], writes=[p[j].b])
                bld.op('dve', lambda e, j=j: e.scalar_tensor_tensor(
                           out=sdg[:, j * 128:(j + 1) * 128], in0=p[j][0:128, 0:128],
                           scalar=0.125, in1=maskb.ap, op0=ALU.mult, op1=ALU.add),
                       reads=[p[j].b, maskb.b], writes=[sdg.b])
                bld.op('dve', lambda e, j=j: e.tensor_reduce(
                           out=mx[:, j:j + 1], in_=sdg[:, j * 128:(j + 1) * 128],
                           axis=AX.X, op=ALU.max),
                       reads=[sdg.b], writes=[mx.b])
                if j < 3:
                    bld.op('dve', lambda e, j=j, w=w: e.tensor_reduce(
                               out=mx[:, 4 + j:5 + j], in_=p[j][0:128, 128:w],
                               axis=AX.X, op=ALU.max),
                           reads=[p[j].b], writes=[mx.b])
            bld.op('dve', lambda e: e.tensor_reduce(out=mx[:, 7:8], in_=mx[:, 4:7],
                       axis=AX.X, op=ALU.max),
                   reads=[mx.b], writes=[mx.b])
            bld.op('dve', lambda e: e.tensor_scalar(out=mx[:, 7:8], in0=mx[:, 7:8],
                       scalar1=0.125, scalar2=None, op0=ALU.mult),
                   reads=[mx.b], writes=[mx.b])
            bld.op('dve', lambda e: e.tensor_reduce(out=mx[:, 3:4], in_=mx[:, 0:4],
                       axis=AX.X, op=ALU.max),
                   reads=[mx.b], writes=[mx.b])
            bld.op('dve', lambda e: e.tensor_tensor(out=mx[:, 3:4], in0=mx[:, 3:4],
                       in1=mx[:, 7:8], op=ALU.max),
                   reads=[mx.b], writes=[mx.b])
            part_reduce(bld, C, p[4], mx[:, 3:4], mx.b, m1, m1[0:1, 0:1], ALU.max)
            bld.op('dve', lambda e: e.tensor_scalar(out=m1[0:1, 1:2], in0=m1[0:1, 0:1],
                       scalar1=-1.0, scalar2=None, op0=ALU.mult),
                   reads=[m1.b], writes=[m1.b])
            bc_row(bld, C, p[4], (negm, negm.ap), m1[0:1, 1:2], 1, [m1.b])
            pt3 = ptall.t[:].rearrange('p (c l) -> p c l', l=512)
            for j in range(4):
                w = 512 - j * 128
                bld.op('act', lambda e, j=j: e.activation(
                           out=pt3[:, j, j * 128:(j + 1) * 128],
                           in_=sdg[:, j * 128:(j + 1) * 128], func=AF.Exp,
                           bias=negm.ap),
                       reads=[sdg.b, negm.b], writes=[ptall.b])
                if j < 3:
                    bld.op('act', lambda e, j=j, w=w: e.activation(
                               out=pt3[:, j, (j + 1) * 128:512],
                               in_=p[j][0:128, 128:w], func=AF.Exp, scale=0.125,
                               bias=negm.ap),
                           reads=[p[j].b, negm.b], writes=[ptall.b])
            def admm(e):
                last = None
                for i in range(4):
                    for j in range(i + 1):
                        e.ldweights(ptall[:, j * 512 + i * 128:j * 512 + (i + 1) * 128])
                        last = e.matmul(out=p[4][0:128, i * 65:(i + 1) * 65],
                                        lhsT=ptall[:, j * 512 + i * 128:j * 512 + (i + 1) * 128],
                                        rhs=v1[:, j * 65:(j + 1) * 65],
                                        start=(j == 0), stop=(j == i))
                return last
            bld.op('pe', admm, reads=[ptall.b, v1.b], writes=[p[4].b])
            def ammm(e, hc=hc):
                last = None
                for i in range(4):
                    e.ldweights(fmb[0:64, 512 + i * 128:512 + (i + 1) * 128])
                    last = e.matmul(out=p[5][0:128, i * 65:(i + 1) * 65],
                                    lhsT=fmb[0:64, 512 + i * 128:512 + (i + 1) * 128],
                                    rhs=MZb[:, hc], start=True, stop=True)
                return last
            bld.op('pe', ammm, reads=[fmb.b, MZb.b], writes=[p[5].b])
            p4v = p[4].t[:, 0:260].rearrange('p (c e) -> p c e', e=65)
            p5v = p[5].t[:, 0:260].rearrange('p (c e) -> p c e', e=65)
            bld.op('dve', lambda e, p5v=p5v: e.reciprocal(out=rc[:, 0:4],
                       in_=p5v[:, 0:4, 64]),
                   reads=[p[5].b], writes=[rc.b])
            bld.op('dve', lambda e, p4v=p4v: e.reciprocal(out=rc[:, 4:8],
                       in_=p4v[:, 0:4, 64]),
                   reads=[p[4].b], writes=[rc.b])
            bld.op('dve', lambda e, p5v=p5v: e.tensor_tensor(
                       out=amn.t[:].rearrange('p (c d) -> p c d', d=64),
                       in0=p5v[:, 0:4, 0:64],
                       in1=rc[:, 0:4].rearrange('p (c o) -> p c o', o=1)
                           .broadcast_to([128, 4, 64]),
                       op=ALU.mult),
                   reads=[p[5].b, rc.b], writes=[amn.b])
            bld.op('dve', lambda e, p4v=p4v: e.tensor_tensor(
                       out=adn.t[:].rearrange('p (c d) -> p c d', d=64),
                       in0=p4v[:, 0:4, 0:64],
                       in1=rc[:, 4:8].rearrange('p (c o) -> p c o', o=1)
                           .broadcast_to([128, 4, 64]),
                       op=ALU.mult),
                   reads=[p[4].b, rc.b], writes=[adn.b])
            bld.op('dve', lambda e: e.tensor_tensor(out=amn.ap, in0=amn.ap, in1=adn.ap,
                       op=ALU.subtract),
                   reads=[amn.b, adn.b], writes=[amn.b])
            bld.op('pool', lambda e, h=h: e.tensor_tensor(
                       out=amn.t[:].rearrange('p (c d) -> p c d', d=64),
                       in0=amn.t[:].rearrange('p (c d) -> p c d', d=64),
                       in1=gb[:, h * 64:(h + 1) * 64]
                           .rearrange('p (o d) -> p o d', o=1)
                           .broadcast_to([128, 4, 64]),
                       op=ALU.mult),
                   reads=[amn.b, gb.b], writes=[amn.b])
            bld.op('dve', lambda e: e.tensor_tensor(out=att.ap, in0=adn.ap, in1=amn.ap,
                       op=ALU.add),
                   reads=[adn.b, amn.b], writes=[att.b])
            bld.op('poold', lambda e, base=base: e.dma_start(
                       out=av[base].rearrange('(c p) d -> p c d', p=128),
                       in_=att.t[:].rearrange('p (c d) -> p c d', d=64)),
                   reads=[att.b], writes=[Dm['att'].b])
            def mzmm(e):
                last = None
                for ci in range(4):
                    e.ldweights(skb[:, ci * 64:(ci + 1) * 64])
                    last = e.matmul(out=p[5][0:64, 0:65],
                                    lhsT=skb[:, ci * 64:(ci + 1) * 64],
                                    rhs=v1[:, ci * 65:(ci + 1) * 65],
                                    start=(ci == 0), stop=(ci == 3))
                return last
            bld.op('pe', mzmm, reads=[skb.b, v1.b], writes=[p[5].b])
            bld.op('dve', lambda e, hc=hc: e.tensor_tensor(out=MZ[:, hc],
                       in0=MZ[:, hc], in1=p[5][0:64, 0:65], op=ALU.add),
                   reads=[MZ.b, p[5].b], writes=[MZ.b])
            bld.op('pool', lambda e, hc=hc: e.tensor_copy(out=MZb[:, hc], in_=MZ[:, hc]),
                   reads=[MZ.b], writes=[MZb.b])

    for s in range(NSEG):
        for h in range(H):
            it = s * H + h
            c_body(s, h, qblkR(it), kblkR(it), v1R(it), sqR(it), skR(it),
                   tm1R(it), qbbR(it), sqbR(it), kbbR(it), skbR(it), fmbR(it),
                   sdgR(it), mxR(it), m1R(it), negmR(it), ptallR(it), rcR(it),
                   amnR(it), adnR(it), attR(it))
    bld.pop_scope()

    bld.push_scope()
    af = SB(bld, 'c_af', [128, 8 * T], BF16)
    wo3 = SB(bld, 'c_wo3', [128, 8 * 1024], BF16)
    af_rb = [Buf(f'caf{i}') for i in range(4)]
    striped_load(bld, af_rb,
                 [(lambda e, j=j: e.dma_start(
                       out=af[:, j * T:(j + 1) * T],
                       in_=Dm['att'][:, j * 128:(j + 1) * 128], transpose=True))
                  for j in range(8)],
                 reads=[Dm['att'].b])
    woa = I['L1_attn_Wo'].a.rearrange('(k p) m -> p k m', p=128)
    for k in range(8):
        bld.op('poold', lambda e, k=k: e.dma_start(
                   out=wo3[:, k * 1024:(k + 1) * 1024], in_=woa[:, k, :]),
               reads=[I['L1_attn_Wo'].b], writes=[wo3.b])
    xresR = Rot(bld, 'c_xr2', [128, 1024], F32, 2)
    xnewR = Rot(bld, 'c_xn2', [128, 1024], F32, 2)
    xoutR = Rot(bld, 'c_xo2', [128, 1024], F32, 2)
    statR = Rot(bld, 'c_ls2', [128, 8], F32, 2)
    for tc in range(32):
        rtok = slice(tc * 128, (tc + 1) * 128)
        pb = 0 if tc % 2 == 0 else 2
        xres2, xnew2, xout2, stat2 = xresR(tc), xnewR(tc), xoutR(tc), statR(tc)
        def omm2(e, tc=tc, pb=pb):
            last = None
            for k in range(8):
                e.ldweights(af[:, k * T + tc * 128:k * T + (tc + 1) * 128])
                for n in range(2):
                    last = e.matmul(out=p[pb + n][0:128, 0:512],
                                    lhsT=af[:, k * T + tc * 128:k * T + (tc + 1) * 128],
                                    rhs=wo3[:, k * 1024 + n * 512:k * 1024 + (n + 1) * 512],
                                    start=(k == 0), stop=(k == 7))
            return last
        bld.op('pe', omm2, reads=[*af_rb, wo3.b], writes=[p[pb].b, p[pb + 1].b])
        bld.op('sp', lambda e, rtok=rtok, xres2=xres2: e.dma_start(out=xres2.ap, in_=cur[rtok, :]),
               reads=[cur.b], writes=[xres2.b])
        for n in range(2):
            bld.op('dve', lambda e, n=n, pb=pb, xnew2=xnew2, xres2=xres2: e.scalar_tensor_tensor(
                       out=xnew2[:, n * 512:(n + 1) * 512], in0=p[pb + n][0:128, 0:512],
                       scalar=1.0, in1=xres2[:, n * 512:(n + 1) * 512],
                       op0=ALU.mult, op1=ALU.add),
                   reads=[p[pb + n].b, xres2.b], writes=[xnew2.b])
        emit_ln(bld, C, xnew2, xout2, stat2, st1, 1024, p[5])
        bld.op('actd', lambda e, rtok=rtok, xout2=xout2: e.dma_start(out=out_dram[rtok, :], in_=xout2.ap),
               reads=[xout2.b], writes=[out_dram.b])
        bld.op('poold', lambda e, rtok=rtok, xout2=xout2: e.dma_start(out=Dm['xbf'][rtok, :], in_=xout2.ap),
               reads=[xout2.b], writes=[Dm['xbf'].b])
    bld.pop_scope()
    bld.pop_scope()


def cc_allreduce_max(bld, C, Dm, vals_sb, vals_buf, k, out_sb):
    bld.op('actd', lambda e: e.dma_start(out=Dm['ccin'][0:1, 0:k], in_=vals_sb),
           reads=[vals_buf], writes=[Dm['ccin'].b])
    bld.op('cc', lambda e: e.collective_compute(
               'AllReduce', ALU.max, replica_groups=[list(range(NCORES))],
               ins=[Dm['ccin'][0:1, 0:k]], outs=[Dm['ccout'][0:1, 0:k]]),
           reads=[Dm['ccin'].b], writes=[Dm['ccout'].b])
    bld.op('sp', lambda e: e.dma_start(out=out_sb.ap[0:1, 0:k], in_=Dm['ccout'][0:1, 0:k]),
           reads=[Dm['ccout'].b], writes=[out_sb.b])


def gmax_out(bld, vals_ap, vals_buf, k, out_dr):
    bld.op('actd', lambda e: e.dma_start(out=out_dr[0:1, 0:k], in_=vals_ap),
           reads=[vals_buf], writes=[out_dr.b])


def gmax_in(bld, C, in_dr, k, out_sb, dst=0):
    tmp = SB(bld, 'gmx', [k, 8], F32)
    bld.op('sp', lambda e: e.dma_start(out=tmp.ap, in_=in_dr[:, :]),
           reads=[in_dr.b], writes=[tmp.b])
    col = SB(bld, 'gmc', [k, 1], F32)
    bld.op('dve', lambda e: e.tensor_reduce(out=col.ap, in_=tmp.ap, axis=AX.X,
               op=ALU.max),
           reads=[tmp.b], writes=[col.b])
    p = C['p'][5]
    ident = C['ident_f32']
    bld.op('pe', lambda e: e.transpose(out=p[0:1, 0:k], in_=col.ap,
               identity=ident[0:k, 0:k]),
           reads=[col.b, ident.b], writes=[p.b])
    bld.op('dve', lambda e: e.tensor_copy(out=out_sb.ap[0:1, dst:dst + k],
               in_=p[0:1, 0:k]),
           reads=[p.b], writes=[out_sb.b])


def emit_d_wq(bld, C, I, Dm, mats):
    bld.push_scope()
    wld = SB(bld, 'd_wld', [128, 3072], F32)
    bmax = SB(bld, 'd_bmax', [128, 96], F32)
    binv = SB(bld, 'd_binv', [128, 96], F32)
    wt = SB(bld, 'd_wt', [128, 3072], F32)
    wqi2 = SB(bld, 'd_wqi2', [128, 3072], I32)
    wb_ = SB(bld, 'd_wb', [128, 3072], BF16)
    for (W, dq, nr, ncol) in mats:
        nb = ncol // 32
        for r in range(nr):
            w3 = wld.t[:, 0:ncol].rearrange('p (b q) -> p b q', q=32)
            bld.op('sp', lambda e, W=W, r=r, ncol=ncol: e.dma_start(
                       out=wld[:, 0:ncol], in_=W[r * 128:(r + 1) * 128, :]),
                   reads=[W.b], writes=[wld.b])
            bld.op('dve', lambda e, w3=w3, nb=nb: e.tensor_reduce(
                       out=bmax[:, 0:nb], in_=w3, axis=AX.X, op=ALU.max,
                       apply_absolute_value=True),
                   reads=[wld.b], writes=[bmax.b])
            bld.op('dve', lambda e, nb=nb: e.tensor_scalar(
                       out=bmax[:, 0:nb], in0=bmax[:, 0:nb], scalar1=1.0 / 7.0,
                       scalar2=1e-12, op0=ALU.mult, op1=ALU.max),
                   reads=[bmax.b], writes=[bmax.b])
            bld.op('dve', lambda e, nb=nb: e.reciprocal(out=binv[:, 0:nb],
                       in_=bmax[:, 0:nb]),
                   reads=[bmax.b], writes=[binv.b])
            bld.op('dve', lambda e, w3=w3, nb=nb, ncol=ncol: e.tensor_tensor(
                       out=wt.t[:, 0:ncol].rearrange('p (b q) -> p b q', q=32),
                       in0=w3,
                       in1=binv[:, 0:nb].rearrange('p (b o) -> p b o', o=1)
                           .broadcast_to([128, nb, 32]),
                       op=ALU.mult),
                   reads=[wld.b, binv.b], writes=[wt.b])
            bld.op('pool', lambda e, ncol=ncol: e.tensor_copy(
                       out=wqi2[:, 0:ncol], in_=wt[:, 0:ncol]),
                   reads=[wt.b], writes=[wqi2.b])
            bld.op('dve', lambda e, ncol=ncol: e.tensor_scalar(
                       out=wt[:, 0:ncol], in0=wqi2[:, 0:ncol], scalar1=-8.0,
                       scalar2=7.0, op0=ALU.max, op1=ALU.min),
                   reads=[wqi2.b], writes=[wt.b])
            bld.op('dve', lambda e, ncol=ncol, nb=nb: e.tensor_tensor(
                       out=wb_.t[:, 0:ncol].rearrange('p (b q) -> p b q', q=32),
                       in0=wt.t[:, 0:ncol].rearrange('p (b q) -> p b q', q=32),
                       in1=bmax[:, 0:nb].rearrange('p (b o) -> p b o', o=1)
                           .broadcast_to([128, nb, 32]),
                       op=ALU.mult),
                   reads=[wt.b, bmax.b], writes=[wb_.b])
            bld.op('actd', lambda e, dq=dq, r=r, ncol=ncol: e.dma_start(
                       out=dq[r * 128:(r + 1) * 128, :], in_=wb_[:, 0:ncol]),
                   reads=[wb_.b], writes=[dq.b])
    bld.pop_scope()


def emit_d_scales(bld, C, p5, gsc, src, dst_inv, dst_sc, bcs):
    t0, t1 = 5, 6
    bld.op('dve', lambda e: e.tensor_scalar(out=gsc[0:1, t0:t0 + 1],
               in0=gsc[0:1, src:src + 1], scalar1=1.0 / 32767.0, scalar2=1e-12,
               op0=ALU.mult, op1=ALU.max),
           reads=[gsc.b], writes=[gsc.b])
    bld.op('dve', lambda e: e.reciprocal(out=gsc[0:1, t1:t1 + 1],
               in_=gsc[0:1, t0:t0 + 1]),
           reads=[gsc.b], writes=[gsc.b])
    bc_row(bld, C, p5, (bcs, bcs[:, dst_inv:dst_inv + 1]), gsc[0:1, t1:t1 + 1], 1, [gsc.b])
    bc_row(bld, C, p5, (bcs, bcs[:, dst_sc:dst_sc + 1]), gsc[0:1, t0:t0 + 1], 1, [gsc.b])


def emit_d_maxx(bld, C, D_, cur, st1, stat):
    bld.push_scope()
    xt = SB(bld, 'dm_xt', [128, 1024], F32)
    rmax = SB(bld, 'dm_rmax', [128, 32], F32)
    for tc in range(32):
        rtok = slice(tc * 128, (tc + 1) * 128)
        bld.op('sp', lambda e, rtok=rtok: e.dma_start(out=xt.ap, in_=cur[rtok, :]),
               reads=[cur.b], writes=[xt.b])
        bld.op('dve', lambda e, tc=tc: e.tensor_reduce(out=rmax[:, tc:tc + 1],
                   in_=xt.ap, axis=AX.X, op=ALU.max, apply_absolute_value=True),
               reads=[xt.b], writes=[rmax.b])
    bld.op('dve', lambda e: e.tensor_reduce(out=stat[:, 0:1], in_=rmax.ap,
               axis=AX.X, op=ALU.max),
           reads=[rmax.b], writes=[stat.b])
    part_reduce(bld, C, C['p'][5], stat[:, 0:1], stat.b, st1, st1[0:1, 0:1], ALU.max)
    bld.pop_scope()


def emit_d_xq(bld, C, Dm, cur, bcs):
    bld.push_scope()
    xt = Rot(bld, 'd_xt', [128, 1024], F32, 3)
    qt = Rot(bld, 'd_qt', [128, 1024], F32, 2)
    qi2 = Rot(bld, 'd_qi2', [128, 1024], I32, 2)
    qb2 = Rot(bld, 'd_qb2', [128, 1024], BF16, 2)
    for tc in range(32):
        rtok = slice(tc * 128, (tc + 1) * 128)
        a, b_, c_, d_ = xt(tc), qt(tc), qi2(tc), qb2(tc)
        bld.op('sp', lambda e, rtok=rtok, a=a: e.dma_start(out=a.ap, in_=cur[rtok, :]),
               reads=[cur.b], writes=[a.b])
        bld.op('act', lambda e, a=a, b_=b_: e.activation(out=b_.ap, in_=a.ap,
                   func=AF.Identity, scale=bcs[:, 0:1]),
               reads=[a.b, bcs.b], writes=[b_.b])
        bld.op('pool', lambda e, b_=b_, c_=c_: e.tensor_copy(out=c_.ap, in_=b_.ap),
               reads=[b_.b], writes=[c_.b])
        bld.op('dve', lambda e, c_=c_, d_=d_: e.tensor_copy(out=d_.ap, in_=c_.ap),
               reads=[c_.b], writes=[d_.b])
        bld.op('poold', lambda e, rtok=rtok, d_=d_: e.dma_start(out=Dm['xq'][rtok, :],
                   in_=d_.ap),
               reads=[d_.b], writes=[Dm['xq'].b])
    bld.pop_scope()


def emit_d_upgate(bld, C, Dm, bcs, st1, stat, d_up, d_gt):
    O = 3072
    p = C['p']
    bld.push_scope()
    xqf = SB(bld, 'd_xqf', [128, 8 * T], BF16)
    wstr = SB(bld, 'd_wstr', [128, 8 * O], BF16)
    ubR = Rot(bld, 'd_ub', [128, 512], F32, 3)
    um6 = SB(bld, 'd_um6', [128, 8], F32)
    mxc = SB(bld, 'd_mxc', [128, 64], F32)
    xqf_rb = [Buf(f'dxqf{i}') for i in range(4)]
    wstr_rb = [Buf(f'dwstr{i}') for i in range(4)]
    striped_load(bld, xqf_rb,
                 [(lambda e, j=j: e.dma_start(
                       out=xqf[:, j * T:(j + 1) * T],
                       in_=Dm['xq'][:, j * 128:(j + 1) * 128], transpose=True))
                  for j in range(8)],
                 reads=[Dm['xq'].b])
    for (mi, dqw, ddst) in [(0, Dm['wqu2'], d_up), (1, Dm['wqg2'], d_gt)]:
        striped_load(bld, wstr_rb,
                     [(lambda e, j=j, dqw=dqw: e.dma_start(
                           out=wstr[:, j * O:(j + 1) * O],
                           in_=dqw[:, j * 128:(j + 1) * 128], transpose=True))
                      for j in range(8)],
                     reads=[dqw.b])
        for tc in range(32):
            rtok = slice(tc * 128, (tc + 1) * 128)
            for og in range(2):
                pb = 0 if og == 0 else 3
                def umm(e, tc=tc, og=og, pb=pb):
                    last = None
                    for i in range(8):
                        e.ldweights(xqf[:, i * T + tc * 128:i * T + (tc + 1) * 128])
                        for op_ in range(3):
                            last = e.matmul(
                                out=p[pb + op_][0:128, 0:512],
                                lhsT=xqf[:, i * T + tc * 128:i * T + (tc + 1) * 128],
                                rhs=wstr[:, i * O + og * 1536 + op_ * 512:
                                         i * O + og * 1536 + (op_ + 1) * 512],
                                start=(i == 0), stop=(i == 7))
                    return last
                bld.op('pe', umm, reads=[*xqf_rb, *wstr_rb],
                       writes=[p[pb].b, p[pb + 1].b, p[pb + 2].b])
                for op_ in range(3):
                    ocol = og * 1536 + op_ * 512
                    bi = og * 3 + op_
                    ub = ubR(bi)
                    bld.op('act', lambda e, op_=op_, pb=pb, ub=ub: e.activation(out=ub.ap,
                               in_=p[pb + op_][0:128, 0:512], func=AF.Identity,
                               scale=bcs[:, 1:2]),
                           reads=[p[pb + op_].b, bcs.b], writes=[ub.b])
                    bld.op('dve', lambda e, bi=bi, ub=ub: e.tensor_reduce(
                               out=um6[:, bi:bi + 1], in_=ub.ap, axis=AX.X,
                               op=ALU.max, apply_absolute_value=True),
                           reads=[ub.b], writes=[um6.b])
                    bld.op('actd', lambda e, rtok=rtok, ocol=ocol, ddst=ddst, ub=ub: e.dma_start(
                               out=ddst[rtok, ocol:ocol + 512], in_=ub.ap),
                           reads=[ub.b], writes=[ddst.b])
            bld.op('dve', lambda e, mi=mi, tc=tc: e.tensor_reduce(
                       out=mxc[:, mi * 32 + tc:mi * 32 + tc + 1], in_=um6[:, 0:6],
                       axis=AX.X, op=ALU.max),
                   reads=[um6.b], writes=[mxc.b])
    for mi in range(2):
        bld.op('dve', lambda e, mi=mi: e.tensor_reduce(out=stat[:, 0:1],
                   in_=mxc[:, mi * 32:(mi + 1) * 32], axis=AX.X, op=ALU.max),
               reads=[mxc.b], writes=[stat.b])
        part_reduce(bld, C, C['p'][5], stat[:, 0:1], stat.b, st1,
                    st1[0:1, mi:mi + 1], ALU.max)
    bld.pop_scope()


def emit_d_g2raw(bld, C, bcs, st1, stat, d_gt_in, d_g2_out):
    bld.push_scope()
    gt_R = Rot(bld, 'd_gt2', [128, 3072], F32, 2)
    ti2R = Rot(bld, 'd_ti2', [128, 3072], I32, 2)
    g1R = Rot(bld, 'd_g1', [128, 3072], F32, 2)
    sg2R = Rot(bld, 'd_sg2', [128, 3072], F32, 2)
    g2m = SB(bld, 'd_g2m', [128, 32], F32)
    for tc in range(32):
        rtok = slice(tc * 128, (tc + 1) * 128)
        gt_, ti2, g1, sg2 = gt_R(tc), ti2R(tc), g1R(tc), sg2R(tc)
        bld.op('sp', lambda e, rtok=rtok, gt_=gt_: e.dma_start(out=gt_.ap, in_=d_gt_in[rtok, :]),
               reads=[d_gt_in.b], writes=[gt_.b])
        bld.op('act', lambda e, gt_=gt_, g1=g1: e.activation(out=g1.ap, in_=gt_.ap, func=AF.Identity,
                   scale=bcs[:, 4:5]),
               reads=[gt_.b, bcs.b], writes=[g1.b])
        bld.op('pool', lambda e, g1=g1, ti2=ti2: e.tensor_copy(out=ti2.ap, in_=g1.ap),
               reads=[g1.b], writes=[ti2.b])
        bld.op('dve', lambda e, ti2=ti2, g1=g1: e.tensor_scalar(out=g1.ap, in0=ti2.ap,
                   scalar1=bcs[:, 5:6], scalar2=None, op0=ALU.mult),
               reads=[ti2.b, bcs.b], writes=[g1.b])
        bld.op('act', lambda e, g1=g1, sg2=sg2: e.activation(out=sg2.ap, in_=g1.ap, func=AF.Sigmoid),
               reads=[g1.b], writes=[sg2.b])
        bld.op('act', lambda e, sg2=sg2: e.activation(out=sg2.ap, in_=sg2.ap, func=AF.Identity,
                   scale=65536.0),
               reads=[sg2.b], writes=[sg2.b])
        bld.op('pool', lambda e, sg2=sg2, ti2=ti2: e.tensor_copy(out=ti2.ap, in_=sg2.ap),
               reads=[sg2.b], writes=[ti2.b])
        bld.op('dve', lambda e, ti2=ti2, sg2=sg2: e.tensor_scalar(out=sg2.ap, in0=ti2.ap, scalar1=65535.0,
                   scalar2=1.0 / 65536.0, op0=ALU.min, op1=ALU.mult),
               reads=[ti2.b], writes=[sg2.b])
        bld.op('dve', lambda e, g1=g1, sg2=sg2: e.tensor_tensor(out=g1.ap, in0=g1.ap, in1=sg2.ap,
                   op=ALU.mult),
               reads=[g1.b, sg2.b], writes=[g1.b])
        bld.op('dve', lambda e, tc=tc, g1=g1: e.tensor_reduce(out=g2m[:, tc:tc + 1], in_=g1.ap,
                   axis=AX.X, op=ALU.max, apply_absolute_value=True),
               reads=[g1.b], writes=[g2m.b])
        bld.op('actd', lambda e, rtok=rtok, g1=g1: e.dma_start(out=d_g2_out[rtok, :], in_=g1.ap),
               reads=[g1.b], writes=[d_g2_out.b])
    bld.op('dve', lambda e: e.tensor_reduce(out=stat[:, 0:1], in_=g2m.ap, axis=AX.X,
               op=ALU.max),
           reads=[g2m.b], writes=[stat.b])
    part_reduce(bld, C, C['p'][5], stat[:, 0:1], stat.b, st1, st1[0:1, 0:1], ALU.max)
    bld.pop_scope()


def emit_d_oraw(bld, C, bcs, st1, stat, d_g2_in, d_up_in, d_or_out):
    bld.push_scope()
    gt_R = Rot(bld, 'd_gt2', [128, 3072], F32, 2)
    ti2R = Rot(bld, 'd_ti2', [128, 3072], I32, 2)
    tj2R = Rot(bld, 'd_tj2', [128, 3072], I32, 2)
    g1R = Rot(bld, 'd_g1', [128, 3072], F32, 2)
    sg2R = Rot(bld, 'd_sg2', [128, 3072], F32, 2)
    up_R = Rot(bld, 'd_up2', [128, 3072], F32, 2)
    om = SB(bld, 'd_om', [128, 32], F32)
    for tc in range(32):
        rtok = slice(tc * 128, (tc + 1) * 128)
        gt_, ti2, g1, sg2, up_, tj2 = (gt_R(tc), ti2R(tc), g1R(tc), sg2R(tc),
                                       up_R(tc), tj2R(tc))
        bld.op('sp', lambda e, rtok=rtok, gt_=gt_: e.dma_start(out=gt_.ap, in_=d_g2_in[rtok, :]),
               reads=[d_g2_in.b], writes=[gt_.b])
        bld.op('actd', lambda e, rtok=rtok, up_=up_: e.dma_start(out=up_.ap, in_=d_up_in[rtok, :]),
               reads=[d_up_in.b], writes=[up_.b])
        bld.op('act', lambda e, gt_=gt_, g1=g1: e.activation(out=g1.ap, in_=gt_.ap, func=AF.Identity,
                   scale=bcs[:, 6:7]),
               reads=[gt_.b, bcs.b], writes=[g1.b])
        bld.op('pool', lambda e, g1=g1, ti2=ti2: e.tensor_copy(out=ti2.ap, in_=g1.ap),
               reads=[g1.b], writes=[ti2.b])
        bld.op('dve', lambda e, ti2=ti2, g1=g1: e.tensor_scalar(out=g1.ap, in0=ti2.ap,
                   scalar1=bcs[:, 7:8], scalar2=None, op0=ALU.mult),
               reads=[ti2.b, bcs.b], writes=[g1.b])
        bld.op('act', lambda e, up_=up_, sg2=sg2: e.activation(out=sg2.ap, in_=up_.ap, func=AF.Identity,
                   scale=bcs[:, 2:3]),
               reads=[up_.b, bcs.b], writes=[sg2.b])
        bld.op('pool', lambda e, sg2=sg2, tj2=tj2: e.tensor_copy(out=tj2.ap, in_=sg2.ap),
               reads=[sg2.b], writes=[tj2.b])
        bld.op('dve', lambda e, tj2=tj2, sg2=sg2: e.tensor_scalar(out=sg2.ap, in0=tj2.ap,
                   scalar1=bcs[:, 3:4], scalar2=None, op0=ALU.mult),
               reads=[tj2.b, bcs.b], writes=[sg2.b])
        bld.op('dve', lambda e, g1=g1, sg2=sg2: e.tensor_tensor(out=g1.ap, in0=g1.ap, in1=sg2.ap,
                   op=ALU.mult),
               reads=[g1.b, sg2.b], writes=[g1.b])
        bld.op('dve', lambda e, tc=tc, g1=g1: e.tensor_reduce(out=om[:, tc:tc + 1], in_=g1.ap,
                   axis=AX.X, op=ALU.max, apply_absolute_value=True),
               reads=[g1.b], writes=[om.b])
        bld.op('actd', lambda e, rtok=rtok, g1=g1: e.dma_start(out=d_or_out[rtok, :], in_=g1.ap),
               reads=[g1.b], writes=[d_or_out.b])
    bld.op('dve', lambda e: e.tensor_reduce(out=stat[:, 0:1], in_=om.ap, axis=AX.X,
               op=ALU.max),
           reads=[om.b], writes=[stat.b])
    part_reduce(bld, C, C['p'][5], stat[:, 0:1], stat.b, st1, st1[0:1, 0:1], ALU.max)
    bld.pop_scope()


def emit_d_down(bld, C, Dm, bcs, stat, st1, cur, d_or_in, out_dram):
    p = C['p']
    bld.push_scope()
    up_R = Rot(bld, 'd_up2', [128, 3072], F32, 2)
    ti2R = Rot(bld, 'd_ti2', [128, 3072], I32, 2)
    g1R = Rot(bld, 'd_g1', [128, 3072], F32, 2)
    ob2R = Rot(bld, 'd_ob2', [128, 3072], BF16, 2)
    for tc in range(32):
        rtok = slice(tc * 128, (tc + 1) * 128)
        up_, ti2, g1, ob2 = up_R(tc), ti2R(tc), g1R(tc), ob2R(tc)
        bld.op('sp', lambda e, rtok=rtok, up_=up_: e.dma_start(out=up_.ap, in_=d_or_in[rtok, :]),
               reads=[d_or_in.b], writes=[up_.b])
        bld.op('act', lambda e, up_=up_, g1=g1: e.activation(out=g1.ap, in_=up_.ap, func=AF.Identity,
                   scale=bcs[:, 8:9]),
               reads=[up_.b, bcs.b], writes=[g1.b])
        bld.op('pool', lambda e, g1=g1, ti2=ti2: e.tensor_copy(out=ti2.ap, in_=g1.ap),
               reads=[g1.b], writes=[ti2.b])
        bld.op('dve', lambda e, ti2=ti2, ob2=ob2: e.tensor_copy(out=ob2.ap, in_=ti2.ap),
               reads=[ti2.b], writes=[ob2.b])
        bld.op('poold', lambda e, rtok=rtok, ob2=ob2: e.dma_start(out=Dm['oD'][rtok, :], in_=ob2.ap),
               reads=[ob2.b], writes=[Dm['oD'].b])
    bld.pop_scope()
    bld.push_scope()
    xresR = Rot(bld, 'd_xr', [128, 1024], F32, 2)
    xnewR = Rot(bld, 'd_xn', [128, 1024], F32, 2)
    xoutR = Rot(bld, 'd_xo', [128, 1024], F32, 2)
    statR = Rot(bld, 'd_lst', [128, 8], F32, 2)
    ostrR = Rot(bld, 'do_str', [128, 24 * 512], BF16, 2)
    wdt = SB(bld, 'do_wdt', [128, 24 * 1024], BF16)
    wdt_rb = [Buf(f'dwdt{g}') for g in range(6)]
    ostr_rb = [[Buf(f'dostr{par}_{g}') for g in range(6)] for par in range(2)]
    striped_load(bld, wdt_rb,
                 [(lambda e, k=k: e.dma_start(
                       out=wdt[:, k * 1024:(k + 1) * 1024],
                       in_=Dm['wqd2'][:, k * 128:(k + 1) * 128], transpose=True))
                  for k in range(24)],
                 reads=[Dm['wqd2'].b])
    for w in range(8):
        ostr = ostrR(w)
        striped_load(bld, ostr_rb[w % 2],
                     [(lambda e, w=w, k=k, ostr=ostr: e.dma_start(
                           out=ostr[:, k * 512:(k + 1) * 512],
                           in_=Dm['oD'][w * 512:(w + 1) * 512,
                                        k * 128:(k + 1) * 128],
                           transpose=True))
                      for k in range(24)],
                     reads=[Dm['oD'].b])
        for tci in range(4):
            tc = w * 4 + tci
            rtok = slice(tc * 128, (tc + 1) * 128)
            xres, xnew, xout, stat_ = xresR(tc), xnewR(tc), xoutR(tc), statR(tc)
            pp = (p[0], p[1]) if tc % 2 == 0 else (p[2], p[3])
            def dmm2(e, tci=tci, ostr=ostr, pp=pp):
                last = None
                for k in range(24):
                    e.ldweights(ostr[:, k * 512 + tci * 128:k * 512 + (tci + 1) * 128])
                    for np_ in range(2):
                        last = e.matmul(
                            out=pp[np_][0:128, 0:512],
                            lhsT=ostr[:, k * 512 + tci * 128:k * 512 + (tci + 1) * 128],
                            rhs=wdt[:, k * 1024 + np_ * 512:k * 1024 + (np_ + 1) * 512],
                            start=(k == 0), stop=(k == 23))
                return last
            bld.op('pe', dmm2, reads=[*ostr_rb[w % 2], *wdt_rb], writes=[pp[0].b, pp[1].b])
            bld.op('sp', lambda e, rtok=rtok, xres=xres: e.dma_start(out=xres.ap, in_=cur[rtok, :]),
                   reads=[cur.b], writes=[xres.b])
            for np_ in range(2):
                bld.op('dve', lambda e, np_=np_, xnew=xnew, xres=xres, pp=pp: e.scalar_tensor_tensor(
                           out=xnew[:, np_ * 512:(np_ + 1) * 512],
                           in0=pp[np_][0:128, 0:512], scalar=bcs[:, 9:10],
                           in1=xres[:, np_ * 512:(np_ + 1) * 512],
                           op0=ALU.mult, op1=ALU.add),
                       reads=[pp[np_].b, bcs.b, xres.b], writes=[xnew.b])
            emit_ln(bld, C, xnew, xout, stat_, st1, 1024, p[4])
            bld.op('actd', lambda e, rtok=rtok, xout=xout: e.dma_start(out=out_dram[rtok, :],
                       in_=xout.ap),
                   reads=[xout.b], writes=[out_dram.b])
    bld.pop_scope()


def emit_stage_d(bld, C, I, Dm, cur, out_dram):
    stat = SB(bld, 'd_stat', [128, 8], F32)
    st1 = SB(bld, 'd_st1', [1, 8], F32)
    gsc = SB(bld, 'd_gsc', [1, 8], F32)
    bcs = SB(bld, 'd_bcs', [128, 16], F32)
    p5 = C['p'][5]
    emit_d_wq(bld, C, I, Dm, [(I['L1_ffn_w_up'], Dm['wqu2'], 24, 1024),
                              (I['L1_ffn_w_gate'], Dm['wqg2'], 24, 1024),
                              (I['L1_ffn_w_down'], Dm['wqd2'], 8, 3072)])
    emit_d_maxx(bld, C, Dm, cur, st1, stat)
    cc_allreduce_max(bld, C, Dm, st1[0:1, 0:1], st1.b, 1, gsc)
    emit_d_scales(bld, C, p5, gsc, 0, 0, 1, bcs)
    emit_d_xq(bld, C, Dm, cur, bcs)
    emit_d_upgate(bld, C, Dm, bcs, st1, stat, Dm['up'], Dm['gt'])
    cc_allreduce_max(bld, C, Dm, st1[0:1, 0:2], st1.b, 2, gsc)
    emit_d_scales(bld, C, p5, gsc, 0, 2, 3, bcs)
    emit_d_scales(bld, C, p5, gsc, 1, 4, 5, bcs)
    emit_d_g2raw(bld, C, bcs, st1, stat, Dm['gt'], Dm['gt'])
    cc_allreduce_max(bld, C, Dm, st1[0:1, 0:1], st1.b, 1, gsc)
    emit_d_scales(bld, C, p5, gsc, 0, 6, 7, bcs)
    emit_d_oraw(bld, C, bcs, st1, stat, Dm['gt'], Dm['up'], Dm['up'])
    cc_allreduce_max(bld, C, Dm, st1[0:1, 0:1], st1.b, 1, gsc)
    emit_d_scales(bld, C, p5, gsc, 0, 8, 9, bcs)
    emit_d_down(bld, C, Dm, bcs, stat, st1, cur, Dm['up'], out_dram)


# version 12 (final)
# speedup vs baseline: 1.0180x; 1.0180x over previous
import contextlib
import numpy as np
import ml_dtypes
import concourse.bass as bass
import concourse.mybir as mybir
from concourse.bass_utils import run_bass_kernel_spmd

F32 = mybir.dt.float32
BF16 = mybir.dt.bfloat16
I32 = mybir.dt.int32
AF = mybir.ActivationFunctionType
ALU = mybir.AluOpType
AX = mybir.AxisListType

T = 4096
D = 1024
NCORES = 8
CH = 128
NCH = T // CH
NH = 32
HD = 64
DS = 128
DIN = 2048
CONVD = 2304
NT_CONV = 18
SEM_CAP = 2 ** 31 - 2 ** 16
PHYS = {'pe': 'pe', 'act': 'act', 'dve': 'dve', 'pool': 'pool', 'sp': 'sp',
        'actd': 'act', 'poold': 'pool', 'cc': 'pool', 'dved': 'dve', 'ped': 'pe'}


class Buf:
    __slots__ = ('w', 'r', 'name', 'dsem', 'dcnt', 'scoped', 'dkind')
    def __init__(self, name=''):
        self.w = None
        self.r = []
        self.name = name
        self.dsem = None
        self.dcnt = 0
        self.scoped = False
        self.dkind = None


class _Eng:
    def __init__(self, kind):
        self.kind = kind
        self.ops = []
        self.sem = None
        self.cnt = 0
        self.sems = []
        self.waited = {}


class Builder:

    def __init__(self, nc, stack):
        self.nc = nc
        self.stack = stack
        self.engs = {
            'pe': _Eng('c'), 'act': _Eng('c'), 'dve': _Eng('c'),
            'pool': _Eng('c'), 'sp': _Eng('d'), 'actd': _Eng('d'),
            'poold': _Eng('d'), 'cc': _Eng('d'), 'dved': _Eng('d'),
            'ped': _Eng('d'),
        }
        self.dma_bufs = []
        for e in self.engs.values():
            self._new_sem(e)

    def _new_sem(self, e):
        h = self.stack.enter_context(self.nc.semaphore())
        e.sem = h
        e.cnt = 0
        e.sems.append(h)
        self.sem_total = getattr(self, 'sem_total', {})
        self.sem_is_dma = getattr(self, 'sem_is_dma', {})
        self.sem_total[id(h)] = 0
        self.sem_is_dma[id(h)] = (e.kind == 'd')

    def op(self, eng, fn, reads=(), writes=(), inc=None):
        e = self.engs[eng]
        phys = PHYS[eng]
        pe_ = self.engs[phys]
        if inc is None:
            inc = 16 if e.kind == 'd' else 1
        deps = []
        for b in reads:
            if b.w is not None:
                deps.append(b.w)
        for b in writes:
            if b.w is not None:
                deps.append(b.w)
            deps.extend(b.r)

        if e.kind == 'd':
            assert len(writes) == 1, f'dma op must write exactly one Buf ({eng})'
            tb = writes[0]
            dk = 'sw' if eng in ('poold', 'cc') else 'hw'
            if tb.dsem is None:
                tb.dkind = dk
                pools = getattr(self, 'dsem_pool', None) or {}
                pool = pools.get(dk)
                if tb.scoped and pool:
                    tb.dsem, tb.dcnt = pool.pop()
                else:
                    tb.dsem = self.stack.enter_context(self.nc.semaphore())
                    tb.dcnt = 0
            else:
                assert tb.dkind == dk, f'mixed sw/hw DMA on Buf {tb.name}'
                self.dma_bufs = getattr(self, 'dma_bufs', [])
                self.dma_bufs.append(tb)
            sem = tb.dsem
            tb.dcnt += inc
            val = tb.dcnt
        else:
            if e.cnt + inc > SEM_CAP:
                self._new_sem(e)
            sem = e.sem
            val = e.cnt + inc
            e.cnt = val
        tok = (sem, val)
        e = pe_
        todo = []
        seen = {}
        for (s, v) in deps:
            k = id(s)
            if e.waited.get(k, 0) >= v:
                continue
            if seen.get(k, 0) >= v:
                continue
            seen[k] = v
        for (s, v) in deps:
            if seen.get(id(s), -1) == v:
                todo.append((s, v))
                seen[id(s)] = -2
        for (s, v) in todo:
            e.waited[id(s)] = max(e.waited.get(id(s), 0), v)
        def emit(engine_obj, _todo=tuple(todo), _fn=fn, _sem=sem, _inc=inc):
            for (s, v) in _todo:
                engine_obj.wait_ge(s, v)
            inst = _fn(engine_obj)
            inst.then_inc(_sem, _inc)
        e.ops.append(emit)
        for b in reads:
            b.r.append(tok)
        for b in writes:
            b.w = tok
            b.r = []
        return tok

    def raw_wait(self, eng, tok):
        e = self.engs[PHYS[eng]]
        if e.waited.get(id(tok[0]), 0) >= tok[1]:
            return
        e.waited[id(tok[0])] = tok[1]
        def emit(engine_obj, _t=tok):
            engine_obj.wait_ge(_t[0], _t[1])
        e.ops.append(emit)

    def barrier(self):
        toks = [(e.sem, e.cnt) for e in self.engs.values() if e.cnt > 0]
        toks += [(b.dsem, b.dcnt) for b in self.dma_bufs
                 if b.dcnt > 0 and b.dsem is not None]
        for en in ('pe', 'act', 'dve', 'pool', 'sp'):
            for t in toks:
                self.raw_wait(en, t)

    def push_scope(self):
        st = contextlib.ExitStack()
        self._scopes = getattr(self, '_scopes', [])
        self._scopes.append(st)
        self._dma_marks = getattr(self, '_dma_marks', [])
        self._dma_marks.append(len(getattr(self, 'dma_bufs', [])))
        return st

    def pop_scope(self):
        self.barrier()
        mark = self._dma_marks.pop()
        self.dsem_pool = getattr(self, 'dsem_pool', None) or {'sw': [], 'hw': []}
        for b in self.dma_bufs[mark:]:
            if b.scoped and b.dsem is not None:
                self.dsem_pool[b.dkind].append((b.dsem, b.dcnt))
                b.scoped = False
                b.dsem = None
        self._scopes.pop().close()

    @property
    def cur_stack(self):
        sc = getattr(self, '_scopes', [])
        return sc[-1] if sc else self.stack

    def emit_all(self):
        nc = self.nc
        with nc.Block() as block:
            @block.tensor
            def _(eng):
                for f in self.engs['pe'].ops:
                    f(eng)
            @block.scalar
            def _(eng):
                for f in self.engs['act'].ops:
                    f(eng)
            @block.vector
            def _(eng):
                for f in self.engs['dve'].ops:
                    f(eng)
            @block.gpsimd
            def _(eng):
                for f in self.engs['pool'].ops:
                    f(eng)
            @block.sync
            def _(eng):
                for f in self.engs['sp'].ops:
                    f(eng)


def striped_load(bld, bufs, dmas, reads, streams=('sp',)):
    for b in bufs:
        b.scoped = True
    for i, f in enumerate(dmas):
        bld.op(streams[i % len(streams)], f,
               reads=reads, writes=[bufs[i % len(bufs)]])


_SB_SERIAL = [0]

class SB:
    def __init__(self, bld, name, shape, dtype):
        _SB_SERIAL[0] += 1
        name = f'{name}_{_SB_SERIAL[0]}'
        self.t = bld.cur_stack.enter_context(bld.nc.sbuf_tensor(name, shape, dtype))
        self.b = Buf(name)
        self.b.scoped = True
    def __getitem__(self, idx):
        return self.t[idx]
    @property
    def ap(self):
        return self.t[:]


class Rot:
    def __init__(self, bld, name, shape, dtype, n=2):
        self.sbs = [SB(bld, name, shape, dtype) for _ in range(n)]
        self.n = n
    def __call__(self, i):
        return self.sbs[i % self.n]


class PS:
    def __init__(self, bld, name, shape, dtype):
        self.t = bld.stack.enter_context(bld.nc.psum_tensor(name, shape, dtype))
        self.b = Buf(name)
    def __getitem__(self, idx):
        return self.t[idx]


class DR:
    def __init__(self, nc, name, shape, dtype, kind='Internal', nbufs=1):
        self.t = nc.dram_tensor(name, shape, dtype, kind=kind)
        self.a = self.t.ap()
        self.bufs = [Buf(f'{name}[{i}]') for i in range(nbufs)]
        self.b = self.bufs[0]
    def __getitem__(self, idx):
        return self.a[idx]



def mm_group(bld, out_psum_bufs, fn, reads, tag=''):
    return bld.op('pe', fn, reads=reads, writes=out_psum_bufs)


def bc_row(bld, C, psum, sbuf_out, src_ap, n, src_bufs, extra_reads=()):
    ones = C['ones_f32']
    bld.op('pe', lambda e: e.matmul(out=psum[0:128, 0:n], lhsT=ones[0:1, 0:128],
                                    rhs=src_ap, start=True, stop=True),
           reads=[ones.b] + list(src_bufs), writes=[psum.b])
    sb, ap = sbuf_out
    bld.op('dve', lambda e: e.tensor_copy(out=ap, in_=psum[0:128, 0:n]),
           reads=[psum.b] + list(extra_reads), writes=[sb.b])


def part_reduce(bld, C, psum32, col_ap, col_buf, out_sb, out_ap, op):
    ident = C['ident_f32']
    bld.op('pe', lambda e: e.transpose(out=psum32[0:1, 0:128], in_=col_ap,
                                       identity=ident.ap),
           reads=[col_buf, ident.b], writes=[psum32.b])
    bld.op('dve', lambda e: e.tensor_reduce(out=out_ap, in_=psum32[0:1, 0:128],
                                            axis=AX.X, op=op),
           reads=[psum32.b], writes=[out_sb.b])



def emit_stage_a(bld, C, I, Dm, out_dram):
    nc = bld.nc
    bld.push_scope()

    dt_all = SB(bld, 'a_dt', [128, NCH * NH], F32)
    adt_all = SB(bld, 'a_adt', [128, NCH * NH], F32)
    dtb_bc = SB(bld, 'a_dtb', [128, NH], F32)
    a_bc = SB(bld, 'a_abc', [128, NH], F32)
    convw = SB(bld, 'a_cw', [128, NT_CONV * 4], F32)
    convb = SB(bld, 'a_cb', [128, NT_CONV], F32)
    S = [SB(bld, f'a_S{g}', [128, 512], F32) for g in range(4)]
    Sbf = [SB(bld, f'a_Sb{g}', [128, 512], BF16) for g in range(4)]
    smallR = Rot(bld, 'a_sm', [128, 512], F32, 2)
    xres = SB(bld, 'a_xr', [128, 1024], F32)
    xnew = SB(bld, 'a_xn', [128, 1024], F32)
    xout = SB(bld, 'a_xo', [128, 1024], F32)
    stat = SB(bld, 'a_stat', [128, 8], F32)
    st1 = SB(bld, 'a_st1', [1, 8], F32)
    zevacR = Rot(bld, 'a_zevr', [128, 512], F32, 3)
    p = C['p']
    pbx = C['pbx']

    u = I['u']
    W_in = I['L0_attn_W_in']
    conv_w = I['L0_attn_conv_w']
    conv_b = I['L0_attn_conv_b']
    dt_bias = I['L0_attn_dt_bias']
    A_log = I['L0_attn_A_log']
    norm_w = I['L0_attn_norm_w']
    W_out = I['L0_attn_W_out']

    ident_f = C['ident_f32']
    ident_b = C['ident_bf16']
    tri = C['tri_f32']
    triu_b = C['triu_bf16']
    ones = C['ones_f32']

    for g in range(4):
        bld.op('dve', lambda e, g=g: e.memset(S[g].ap, 0.0), reads=[], writes=[S[g].b])
        bld.op('pool', lambda e, g=g: e.memset(Sbf[g].ap, 0.0), reads=[], writes=[Sbf[g].b])

    for tc in range(32):
        r = slice(tc * 128, (tc + 1) * 128)
        bld.op('sp', lambda e, r=r: e.dma_start(out=xres.ap, in_=u[r, :]),
               reads=[u.b], writes=[xres.b])
        bld.op('poold', lambda e, r=r: e.dma_start(out=Dm['xbf'][r, :], in_=xres.ap),
               reads=[xres.b], writes=[Dm['xbf'].b])

    bld.push_scope()
    xfm = SB(bld, 'a_xfm', [128, 8 * T], BF16)
    wz = SB(bld, 'a_wz', [128, 8 * 2080], BF16)
    wxbc = [SB(bld, f'a_wxbc{i}', [128, 8 * 128], BF16) for i in range(2)]
    xfm_rb = [Buf(f'xfm{i}') for i in range(4)]
    striped_load(bld, xfm_rb,
                 [(lambda e, j=j: e.dma_start(
                       out=xfm[:, j * T:(j + 1) * T],
                       in_=Dm['xbf'][:, j * 128:(j + 1) * 128], transpose=True))
                  for j in range(8)],
                 reads=[Dm['xbf'].b])
    wia = W_in.a.rearrange('(k p) m -> p k m', p=128)
    for k in range(8):
        bld.op('poold', lambda e, k=k: e.dma_start(
                   out=wz[:, k * 2080:k * 2080 + 2048], in_=wia[:, k, 0:2048]),
               reads=[W_in.b], writes=[wz.b])
        bld.op('poold', lambda e, k=k: e.dma_start(
                   out=wz[:, k * 2080 + 2048:(k + 1) * 2080], in_=wia[:, k, 4352:4384]),
               reads=[W_in.b], writes=[wz.b])
    dtb1 = SB(bld, 'a_dtb1', [1, NH], F32)
    bld.op('sp', lambda e: e.dma_start(out=dtb1.ap, in_=dt_bias.a.rearrange('(o n) -> o n', o=1)),
           reads=[dt_bias.b], writes=[dtb1.b])
    bc_row(bld, C, p[5], (dtb_bc, dtb_bc.ap), dtb1.ap, NH, [dtb1.b])
    al1 = SB(bld, 'a_al1', [1, NH], F32)
    bld.op('sp', lambda e: e.dma_start(out=al1.ap, in_=A_log.a.rearrange('(o n) -> o n', o=1)),
           reads=[A_log.b], writes=[al1.b])
    aneg = SB(bld, 'a_aneg', [1, NH], F32)
    bld.op('act', lambda e: e.activation(out=aneg.ap, in_=al1.ap, func=AF.Exp),
           reads=[al1.b], writes=[aneg.b])
    bld.op('dve', lambda e: e.tensor_scalar(out=aneg.ap, in0=aneg.ap, scalar1=-1.0,
                                            scalar2=None, op0=ALU.mult),
           reads=[aneg.b], writes=[aneg.b])
    bc_row(bld, C, p[5], (a_bc, a_bc.ap), aneg.ap, NH, [aneg.b])
    def a1z_body(tc, small):
        rtok = slice(tc * 128, (tc + 1) * 128)
        pd = p[4] if tc % 2 == 0 else p[5]
        def zmm(e):
            last = None
            for k in range(8):
                e.ldweights(xfm[:, k * T + tc * 128:k * T + (tc + 1) * 128])
                for n in range(4):
                    last = e.matmul(out=p[n][0:128, 0:512],
                                    lhsT=xfm[:, k * T + tc * 128:k * T + (tc + 1) * 128],
                                    rhs=wz[:, k * 2080 + n * 512:k * 2080 + (n + 1) * 512],
                                    start=(k == 0), stop=(k == 7))
                last = e.matmul(out=pd[0:128, 0:NH],
                                lhsT=xfm[:, k * T + tc * 128:k * T + (tc + 1) * 128],
                                rhs=wz[:, k * 2080 + 2048:(k + 1) * 2080],
                                start=(k == 0), stop=(k == 7))
            return last
        bld.op('pe', zmm, reads=[*xfm_rb, wz.b],
               writes=[p[0].b, p[1].b, p[2].b, p[3].b, pd.b])
        for n in range(4):
            zev = zevacR(tc * 4 + n)
            bld.op('act', lambda e, n=n, zev=zev: e.activation(out=zev.ap,
                       in_=p[n][0:128, 0:512], func=AF.Silu),
                   reads=[p[n].b], writes=[zev.b])
            bld.op('actd', lambda e, n=n, zev=zev: e.dma_start(
                       out=Dm['z'][rtok, n * 512:(n + 1) * 512], in_=zev.ap),
                   reads=[zev.b], writes=[Dm['z'].b])
        c0 = tc * NH
        bld.op('dve', lambda e: e.tensor_tensor(out=small[:, 0:NH], in0=pd[0:128, 0:NH],
                                                in1=dtb_bc.ap, op=ALU.add),
               reads=[pd.b, dtb_bc.b], writes=[small.b])
        bld.op('act', lambda e: e.activation(out=small[:, NH:2 * NH], in_=small[:, 0:NH],
                                             func=AF.Abs),
               reads=[small.b], writes=[small.b])
        bld.op('act', lambda e: e.activation(out=small[:, 2 * NH:3 * NH],
                                             in_=small[:, NH:2 * NH], func=AF.Exp, scale=-1.0),
               reads=[small.b], writes=[small.b])
        bld.op('act', lambda e: e.activation(out=small[:, 3 * NH:4 * NH],
                                             in_=small[:, 2 * NH:3 * NH], func=AF.Ln,
                                             bias=C['cb'][:, 1:2]),
               reads=[small.b, C['cb'].b], writes=[small.b])
        bld.op('dve', lambda e: e.tensor_scalar(out=small[:, 4 * NH:5 * NH], in0=small[:, 0:NH],
                                                scalar1=0.0, scalar2=None, op0=ALU.max),
               reads=[small.b], writes=[small.b])
        bld.op('dve', lambda e: e.tensor_tensor(out=dt_all[:, c0:c0 + NH],
                                                in0=small[:, 3 * NH:4 * NH],
                                                in1=small[:, 4 * NH:5 * NH], op=ALU.add),
               reads=[small.b], writes=[dt_all.b])
    for tc in range(32):
        a1z_body(tc, smallR(tc))
    bld.op('dve', lambda e: e.tensor_tensor(
               out=adt_all.t[:].rearrange('p (c h) -> p c h', h=NH),
               in0=dt_all.t[:].rearrange('p (c h) -> p c h', h=NH),
               in1=a_bc.t[:].rearrange('p (h o) -> p o h', o=1).broadcast_to([128, NCH, NH]),
               op=ALU.mult),
           reads=[dt_all.b, a_bc.b], writes=[adt_all.b])

    for m in range(18):
        wb = wxbc[m % 2]
        bld.op('poold', lambda e, m=m, wb=wb: e.dma_start(
                   out=wb.t[:].rearrange('p (k c) -> p k c', c=128),
                   in_=wia[:, :, 2048 + m * 128:2048 + (m + 1) * 128]),
               reads=[W_in.b], writes=[wb.b])
        for grp in range(2):
            def xmm(e, m=m, grp=grp, wb=wb):
                last = None
                for k in range(8):
                    e.ldweights(wb[:, k * 128:(k + 1) * 128])
                    for tp in range(4):
                        tpi = grp * 4 + tp
                        last = e.matmul(out=p[tp][0:128, 0:512],
                                        lhsT=wb[:, k * 128:(k + 1) * 128],
                                        rhs=xfm[:, k * T + tpi * 512:k * T + (tpi + 1) * 512],
                                        start=(k == 0), stop=(k == 7))
                return last
            bld.op('pe', xmm, reads=[wb.b, *xfm_rb],
                   writes=[p[0].b, p[1].b, p[2].b, p[3].b])
            for tp in range(4):
                tpi = grp * 4 + tp
                zev = zevacR(m * 8 + grp * 4 + tp)
                bld.op('act', lambda e, tp=tp, zev=zev: e.copy(out=zev.ap,
                           in_=p[tp][0:128, 0:512]),
                       reads=[p[tp].b], writes=[zev.b])
                bld.op('actd', lambda e, m=m, tpi=tpi, zev=zev: e.dma_start(
                           out=Dm['xbc'][m * 128:(m + 1) * 128, tpi * 512:(tpi + 1) * 512],
                           in_=zev.ap),
                       reads=[zev.b], writes=[Dm['xbc'].b])

    bld.pop_scope()

    bld.push_scope()
    convb_t = SB(bld, 'a_cbt', [NT_CONV, 128], F32)
    diagw = SB(bld, 'a_dw', [128, NT_CONV * 4 * 128], BF16)
    cwinR = Rot(bld, 'a_cwin', [128, 515], BF16, 3)
    xcsbR = Rot(bld, 'a_xcsb', [128, 512], BF16, 3)
    cwa = conv_w.a.rearrange('(t p) j -> p t j', p=128)
    bld.op('sp', lambda e: e.dma_start(out=convw.t[:].rearrange('p (t j) -> p t j', j=4), in_=cwa),
           reads=[conv_w.b], writes=[convw.b])
    bld.op('sp', lambda e: e.dma_start(out=convb_t.ap, in_=conv_b.a.rearrange('(t p) -> t p', p=128)),
           reads=[conv_b.b], writes=[convb_t.b])
    bld.op('pe', lambda e: e.transpose(out=p[5][0:128, 0:NT_CONV], in_=convb_t.ap,
                                       identity=ident_f[0:NT_CONV, 0:NT_CONV]),
           reads=[convb_t.b, ident_f.b], writes=[p[5].b])
    bld.op('dve', lambda e: e.tensor_copy(out=convb.ap, in_=p[5][0:128, 0:NT_CONV]),
           reads=[p[5].b], writes=[convb.b])
    for t in range(NT_CONV):
        for j in range(4):
            idx = t * 4 + j
            bld.op('dve', lambda e, idx=idx, t=t, j=j: e.tensor_scalar(
                       out=diagw[:, idx * 128:(idx + 1) * 128], in0=ident_f.ap,
                       scalar1=convw[:, t * 4 + j:t * 4 + j + 1], scalar2=None,
                       op0=ALU.mult),
                   reads=[ident_f.b, convw.b], writes=[diagw.b])
    def a2_body(tp, t, it, cwin, xcsb):
        pc = p[it % 4]
        if tp == 0:
            bld.op('pool', lambda e: e.memset(cwin[:, 0:3], 0.0),
                   reads=[], writes=[cwin.b])
            bld.op('poold', lambda e: e.dma_start(
                       out=cwin[:, 3:515],
                       in_=Dm['xbc'][t * 128:(t + 1) * 128, 0:512]),
                   reads=[Dm['xbc'].b], writes=[cwin.b])
        else:
            bld.op('poold', lambda e: e.dma_start(
                       out=cwin.ap,
                       in_=Dm['xbc'][t * 128:(t + 1) * 128, tp * 512 - 3:(tp + 1) * 512]),
                   reads=[Dm['xbc'].b], writes=[cwin.b])
        def cmm(e):
            last = None
            for j in range(4):
                e.ldweights(diagw[:, (t * 4 + j) * 128:(t * 4 + j + 1) * 128])
                last = e.matmul(out=pc[0:128, 0:512],
                                lhsT=diagw[:, (t * 4 + j) * 128:(t * 4 + j + 1) * 128],
                                rhs=cwin[:, j:j + 512],
                                start=(j == 0), stop=(j == 3))
            return last
        bld.op('pe', cmm, reads=[diagw.b, cwin.b], writes=[pc.b])
        bld.op('act', lambda e: e.activation(out=xcsb.ap, in_=pc[0:128, 0:512],
                                             func=AF.Silu,
                                             bias=convb[:, t:t + 1]),
               reads=[pc.b, convb.b], writes=[xcsb.b])
        bld.op('actd', lambda e: e.dma_start(
                   out=Dm['xc'][t * 128:(t + 1) * 128, tp * 512:(tp + 1) * 512],
                   in_=xcsb.ap),
               reads=[xcsb.b], writes=[Dm['xc'].b])
    for tp in range(8):
        for t in range(NT_CONV):
            it = tp * NT_CONV + t
            a2_body(tp, t, it, cwinR(it), xcsbR(it))

    bld.pop_scope()

    bld.push_scope()
    wout = SB(bld, 'a_wo', [128, 16 * 1024], BF16)
    nwcol = SB(bld, 'a_nw', [128, 16], F32)
    acs_tR = Rot(bld, 'a_acs', [128, NH], F32, 2)
    negacsR = Rot(bld, 'a_nacs', [128, NH], F32, 2)
    asum_sbR = Rot(bld, 'a_asum', [1, NH], F32, 2)
    asum_bcR = Rot(bld, 'a_asbc', [128, NH], F32, 2)
    dec128R = Rot(bld, 'a_dec', [128, NH], F32, 2)
    dcolR = Rot(bld, 'a_dcol', [128, NH], F32, 2)
    dtdecR = Rot(bld, 'a_dtd', [128, NH], F32, 2)
    outdecR = Rot(bld, 'a_od', [128, NH], F32, 2)
    acsfmR = Rot(bld, 'a_acsfm', [32, 128], F32, 2)
    rbR = Rot(bld, 'a_rb', [128, 4096], F32, 2)
    lallR = Rot(bld, 'a_lall', [128, 4096], BF16, 2)
    xfmcR = Rot(bld, 'a_xfmc', [128, 2048], BF16, 2)
    bfmR = Rot(bld, 'a_bfm', [128, 128], BF16, 2)
    cfmR = Rot(bld, 'a_cfm', [128, 128], BF16, 2)
    xtmR = Rot(bld, 'a_xtm', [128, 2048], BF16, 2)
    btmR = Rot(bld, 'a_btm', [128, 128], BF16, 2)
    gmaskR = Rot(bld, 'a_gm', [128, 128], BF16, 2)
    xdR = Rot(bld, 'a_xd', [128, 2048], BF16, 2)
    xd2R = Rot(bld, 'a_xd2', [128, 2048], BF16, 1)
    yosc = SB(bld, 'a_yo', [128, 512], F32)
    yallR = Rot(bld, 'a_y', [128, 2048], F32, 2)
    stmp = SB(bld, 'a_st', [128, 512], F32)
    szR = Rot(bld, 'a_sz', [128, 2048], F32, 1)
    yzt = SB(bld, 'a_yzt', [128, 2048], F32)
    yzb = SB(bld, 'a_yzb', [128, 2048], BF16)
    yzfm = SB(bld, 'a_yzfm', [128, 2048], BF16)
    bld.push_scope()
    wtmp = SB(bld, 'a_wtmp', [128, 1024], F32)
    nw_t = SB(bld, 'a_nwt', [16, 128], F32)
    bld.op('sp', lambda e: e.dma_start(out=nw_t.ap, in_=norm_w.a.rearrange('(k p) -> k p', p=128)),
           reads=[norm_w.b], writes=[nw_t.b])
    bld.op('pe', lambda e: e.transpose(out=p[5][0:128, 0:16], in_=nw_t.ap,
                                       identity=ident_f[0:16, 0:16]),
           reads=[nw_t.b, ident_f.b], writes=[p[5].b])
    bld.op('dve', lambda e: e.tensor_copy(out=nwcol.ap, in_=p[5][0:128, 0:16]),
           reads=[p[5].b], writes=[nwcol.b])
    woa = W_out.a.rearrange('(k p) m -> p k m', p=128)
    for k in range(16):
        bld.op('sp', lambda e, k=k: e.dma_start(out=wtmp.ap, in_=woa[:, k, :]),
               reads=[W_out.b], writes=[wtmp.b])
        bld.op('dve', lambda e, k=k: e.tensor_scalar(
                   out=wout[:, k * 1024:(k + 1) * 1024], in0=wtmp.ap,
                   scalar1=nwcol[:, k:k + 1], scalar2=None, op0=ALU.mult),
               reads=[wtmp.b, nwcol.b], writes=[wout.b])
    bld.pop_scope()

    def a3_body(tc, acs_t, negacs, asum_sb, asum_bc, dec128, dcol, dtdec,
                outdec, acsfm, rb, lall, xfmc, bfm, cfm, xtm, btm, gmask,
                xd, xd2, yall, sz):
        rtok = slice(tc * 128, (tc + 1) * 128)
        c0 = tc * NH
        pa = p[4] if tc % 2 == 0 else p[5]
        bld.op('pe', lambda e: e.matmul(out=pa[0:128, 0:NH], lhsT=tri.ap,
                                        rhs=adt_all[:, c0:c0 + NH],
                                        start=True, stop=True),
               reads=[tri.b, adt_all.b], writes=[pa.b])
        bld.op('dve', lambda e: e.tensor_copy(out=acs_t.ap, in_=pa[0:128, 0:NH]),
               reads=[pa.b], writes=[acs_t.b])
        bld.op('dve', lambda e: e.tensor_scalar(out=negacs.ap, in0=acs_t.ap, scalar1=-1.0,
                                                scalar2=None, op0=ALU.mult),
               reads=[acs_t.b], writes=[negacs.b])
        bld.op('pe', lambda e: e.matmul(out=pa[0:1, 64:64 + NH],
                                        lhsT=ones[0:128, 0:1],
                                        rhs=adt_all[:, c0:c0 + NH],
                                        start=True, stop=True),
               reads=[ones.b, adt_all.b], writes=[pa.b])
        bld.op('dve', lambda e: e.tensor_copy(out=asum_sb.ap, in_=pa[0:1, 64:64 + NH]),
               reads=[pa.b], writes=[asum_sb.b])
        bc_row(bld, C, pa, (asum_bc, asum_bc.ap), asum_sb.ap, NH, [asum_sb.b])
        bld.op('act', lambda e: e.activation(out=dec128.ap, in_=asum_bc.ap, func=AF.Exp),
               reads=[asum_bc.b], writes=[dec128.b])
        bld.op('dve', lambda e: e.tensor_tensor(out=dcol.ap, in0=asum_bc.ap, in1=acs_t.ap,
                                                op=ALU.subtract),
               reads=[asum_bc.b, acs_t.b], writes=[dcol.b])
        bld.op('act', lambda e: e.activation(out=dcol.ap, in_=dcol.ap, func=AF.Exp),
               reads=[dcol.b], writes=[dcol.b])
        bld.op('dve', lambda e: e.tensor_tensor(out=dtdec.ap,
                                                in0=dt_all[:, c0:c0 + NH],
                                                in1=dcol.ap, op=ALU.mult),
               reads=[dt_all.b, dcol.b], writes=[dtdec.b])
        bld.op('act', lambda e: e.activation(out=outdec.ap, in_=acs_t.ap, func=AF.Exp),
               reads=[acs_t.b], writes=[outdec.b])
        bld.op('pe', lambda e: e.transpose(out=pa[0:NH, 0:128], in_=acs_t.ap,
                                           identity=ident_f.ap),
               reads=[acs_t.b, ident_f.b], writes=[pa.b])
        bld.op('dve', lambda e: e.tensor_copy(out=acsfm.ap, in_=pa[0:NH, 0:128]),
               reads=[pa.b], writes=[acsfm.b])
        acs_dr = Dm['acs']
        ab = acs_dr.bufs[tc % len(acs_dr.bufs)]
        bld.op('actd', lambda e: e.dma_start(out=acs_dr[tc, :, :], in_=acsfm.ap),
               reads=[acsfm.b], writes=[ab])
        bld.op('sp', lambda e: e.dma_start(
                   out=rb.ap,
                   in_=acs_dr.a.rearrange('c h l -> c (h l)')[tc:tc + 1, :]
                       .broadcast_to([128, 4096])),
               reads=[ab], writes=[rb.b])
        bld.op('sp', lambda e: e.dma_start(
                   out=xfmc.t[:].rearrange('p (t c) -> p t c', c=128),
                   in_=Dm['xc'].a.rearrange('(t p) n -> p t n', p=128)[:, 0:16, rtok]),
               reads=[Dm['xc'].b], writes=[xfmc.b])
        bld.op('sp', lambda e: e.dma_start(out=bfm.ap, in_=Dm['xc'][2048:2176, rtok]),
               reads=[Dm['xc'].b], writes=[bfm.b])
        bld.op('sp', lambda e: e.dma_start(out=cfm.ap, in_=Dm['xc'][2176:2304, rtok]),
               reads=[Dm['xc'].b], writes=[cfm.b])
        def xtr(e):
            last = None
            for t in range(16):
                last = e.transpose(out=pbx[0:128, t * 128:(t + 1) * 128],
                                   in_=xfmc[:, t * 128:(t + 1) * 128],
                                   identity=ident_b.ap)
            return last
        bld.op('pe', xtr, reads=[xfmc.b, ident_b.b], writes=[pbx.b])
        bld.op('dve', lambda e: e.tensor_copy(out=xtm.ap, in_=pbx[0:128, 0:2048]),
               reads=[pbx.b], writes=[xtm.b])
        bld.op('pe', lambda e: e.transpose(out=pbx[0:128, 0:128], in_=bfm.ap,
                                           identity=ident_b.ap),
               reads=[bfm.b, ident_b.b, xtm.b], writes=[pbx.b])
        bld.op('dve', lambda e: e.tensor_copy(out=btm.ap, in_=pbx[0:128, 0:128]),
               reads=[pbx.b], writes=[btm.b])
        bld.op('dve', lambda e: e.tensor_tensor(
                   out=xd.t[:].rearrange('p (h q) -> p h q', q=HD),
                   in0=xtm.t[:].rearrange('p (h q) -> p h q', q=HD),
                   in1=dt_all[:, c0:c0 + NH].rearrange('p (h o) -> p h o', o=1)
                       .broadcast_to([128, NH, HD]),
                   op=ALU.mult),
               reads=[xtm.b, dt_all.b], writes=[xd.b])
        bld.op('pool', lambda e: e.tensor_tensor(
                   out=xd2.t[:].rearrange('p (h q) -> p h q', q=HD),
                   in0=xtm.t[:].rearrange('p (h q) -> p h q', q=HD),
                   in1=dtdec.t[:].rearrange('p (h o) -> p h o', o=1).broadcast_to([128, NH, HD]),
                   op=ALU.mult),
               reads=[xtm.b, dtdec.b], writes=[xd2.b])
        bld.op('pool', lambda e: e.tensor_tensor(
                   out=rb.t[:].rearrange('p (h l) -> p h l', l=128),
                   in0=rb.t[:].rearrange('p (h l) -> p h l', l=128),
                   in1=acs_t.t[:].rearrange('p (h o) -> p h o', o=1).broadcast_to([128, NH, 128]),
                   op=ALU.subtract),
               reads=[rb.b, acs_t.b], writes=[rb.b])
        bld.op('dve', lambda e: e.tensor_tensor(
                   out=rb.t[:].rearrange('p (h l) -> p h l', l=128),
                   in0=rb.t[:].rearrange('p (h l) -> p h l', l=128),
                   in1=C['maskb_f32'].t[:].rearrange('s (l o) -> s o l', o=1)
                       .broadcast_to([128, NH, 128]),
                   op=ALU.add),
               reads=[rb.b, C['maskb_f32'].b], writes=[rb.b])
        bld.op('act', lambda e: e.activation(out=lall.ap, in_=rb.ap, func=AF.Exp),
               reads=[rb.b], writes=[lall.b])
        bld.op('pe', lambda e: e.matmul(out=pa[0:128, 128:256], lhsT=bfm.ap, rhs=cfm.ap,
                                        start=True, stop=True),
               reads=[bfm.b, cfm.b], writes=[pa.b])
        bld.op('dve', lambda e: e.scalar_tensor_tensor(out=gmask.ap, in0=pa[0:128, 128:256],
                                                       scalar=1.0, in1=triu_b.ap,
                                                       op0=ALU.mult, op1=ALU.mult),
               reads=[pa.b, triu_b.b], writes=[gmask.b])
        bld.op('dve', lambda e: e.tensor_tensor(
                   out=lall.t[:].rearrange('p (h l) -> p h l', l=128),
                   in0=lall.t[:].rearrange('p (h l) -> p h l', l=128),
                   in1=gmask.t[:].rearrange('s (l o) -> s o l', o=1).broadcast_to([128, NH, 128]),
                   op=ALU.mult),
               reads=[lall.b, gmask.b], writes=[lall.b])
        for g in range(4):
            def ydmm(e, g=g):
                last = None
                for hh in range(8):
                    h = g * 8 + hh
                    e.ldweights(lall[:, h * 128:(h + 1) * 128])
                    last = e.matmul(out=p[0][0:128, hh * 64:(hh + 1) * 64],
                                    lhsT=lall[:, h * 128:(h + 1) * 128],
                                    rhs=xd[:, h * 64:(h + 1) * 64],
                                    start=True, stop=True)
                return last
            bld.op('pe', ydmm, reads=[lall.b, xd.b], writes=[p[0].b])
            bld.op('pe', lambda e, g=g: (e.ldweights(cfm.ap),
                                         e.matmul(out=p[1][0:128, 0:512], lhsT=cfm.ap,
                                                  rhs=Sbf[g].ap, start=True, stop=True))[-1],
                   reads=[cfm.b, Sbf[g].b], writes=[p[1].b])
            bld.op('pe', lambda e, g=g: (e.ldweights(btm.ap),
                                         e.matmul(out=p[2][0:128, 0:512], lhsT=btm.ap,
                                                  rhs=xd2[:, g * 512:(g + 1) * 512],
                                                  start=True, stop=True))[-1],
                   reads=[btm.b, xd2.b], writes=[p[2].b])
            bld.op('dve', lambda e, g=g: e.tensor_tensor(
                       out=yosc.t[:].rearrange('p (h q) -> p h q', q=HD),
                       in0=p[1].t[:].rearrange('p (h q) -> p h q', q=HD),
                       in1=outdec[:, g * 8:(g + 1) * 8].rearrange('p (h o) -> p h o', o=1)
                           .broadcast_to([128, 8, HD]),
                       op=ALU.mult),
                   reads=[p[1].b, outdec.b], writes=[yosc.b])
            bld.op('dve', lambda e, g=g: e.tensor_tensor(out=yall[:, g * 512:(g + 1) * 512],
                                                         in0=p[0][0:128, 0:512],
                                                         in1=yosc.ap, op=ALU.add),
                   reads=[p[0].b, yosc.b], writes=[yall.b])
            bld.op('pool', lambda e, g=g: e.tensor_tensor(
                       out=stmp.t[:].rearrange('p (h q) -> p h q', q=HD),
                       in0=S[g].t[:].rearrange('p (h q) -> p h q', q=HD),
                       in1=dec128[:, g * 8:(g + 1) * 8].rearrange('p (h o) -> p h o', o=1)
                           .broadcast_to([128, 8, HD]),
                       op=ALU.mult),
                   reads=[S[g].b, dec128.b], writes=[stmp.b])
            bld.op('dve', lambda e, g=g: e.tensor_tensor(out=S[g].ap, in0=stmp.ap,
                                                         in1=p[2][0:128, 0:512], op=ALU.add),
                   reads=[stmp.b, p[2].b], writes=[S[g].b])
            bld.op('pool', lambda e, g=g: e.tensor_copy(out=Sbf[g].ap, in_=S[g].ap),
                   reads=[S[g].b], writes=[Sbf[g].b])
        bld.op('actd', lambda e: e.dma_start(out=sz.ap, in_=Dm['z'][rtok, :]),
               reads=[Dm['z'].b], writes=[sz.b])
        bld.op('dve', lambda e: e.reduce_sum(out=stat[:, 0:1], in_=yall.ap, axis=AX.X),
               reads=[yall.b], writes=[stat.b])
        bld.op('act', lambda e: e.activation(out=yzt.ap, in_=yall.ap, func=AF.Square,
                                             accum_out=stat[:, 1:2]),
               reads=[yall.b], writes=[yzt.b, stat.b])
        bld.op('dve', lambda e: e.tensor_scalar(out=stat[:, 2:3], in0=stat[:, 0:1],
                                                scalar1=1.0 / DIN, scalar2=None, op0=ALU.mult),
               reads=[stat.b], writes=[stat.b])
        bld.op('dve', lambda e: e.tensor_scalar(out=stat[:, 3:4], in0=stat[:, 1:2],
                                                scalar1=1.0 / DIN, scalar2=None, op0=ALU.mult),
               reads=[stat.b], writes=[stat.b])
        bld.op('dve', lambda e: e.tensor_tensor(out=stat[:, 4:5], in0=stat[:, 2:3],
                                                in1=stat[:, 2:3], op=ALU.mult),
               reads=[stat.b], writes=[stat.b])
        bld.op('dve', lambda e: e.tensor_tensor(out=stat[:, 5:6], in0=stat[:, 3:4],
                                                in1=stat[:, 4:5], op=ALU.subtract),
               reads=[stat.b], writes=[stat.b])
        bld.op('act', lambda e: e.activation(out=stat[:, 6:7], in_=stat[:, 5:6],
                                             func=AF.Ln, bias=C['cb'][:, 0:1]),
               reads=[stat.b, C['cb'].b], writes=[stat.b])
        bld.op('act', lambda e: e.activation(out=stat[:, 7:8], in_=stat[:, 6:7],
                                             func=AF.Exp, scale=-0.5),
               reads=[stat.b], writes=[stat.b])
        bld.op('dve', lambda e: e.tensor_scalar(out=yzt.ap, in0=yall.ap,
                                                scalar1=stat[:, 2:3], scalar2=stat[:, 7:8],
                                                op0=ALU.subtract, op1=ALU.mult),
               reads=[yall.b, stat.b], writes=[yzt.b])
        bld.op('pool', lambda e: e.tensor_tensor(out=yzb.ap, in0=yzt.ap, in1=sz.ap,
                                                 op=ALU.mult),
               reads=[yzt.b, sz.b], writes=[yzb.b])
        def yztr(e):
            last = None
            for t in range(16):
                last = e.transpose(out=pbx[0:128, t * 128:(t + 1) * 128],
                                   in_=yzb[:, t * 128:(t + 1) * 128],
                                   identity=ident_b.ap)
            return last
        bld.op('pe', yztr, reads=[yzb.b, ident_b.b], writes=[pbx.b])
        bld.op('dve', lambda e: e.tensor_copy(out=yzfm.ap, in_=pbx[0:128, 0:2048]),
               reads=[pbx.b], writes=[yzfm.b])
        pb2 = p[5] if tc % 2 == 0 else p[4]
        def omm(e, pb2=pb2):
            last = None
            for k in range(16):
                e.ldweights(yzfm[:, k * 128:(k + 1) * 128])
                last = e.matmul(out=p[3][0:128, 0:512],
                                lhsT=yzfm[:, k * 128:(k + 1) * 128],
                                rhs=wout[:, k * 1024:k * 1024 + 512],
                                start=(k == 0), stop=(k == 15))
                last = e.matmul(out=pb2[0:128, 0:512],
                                lhsT=yzfm[:, k * 128:(k + 1) * 128],
                                rhs=wout[:, k * 1024 + 512:(k + 1) * 1024],
                                start=(k == 0), stop=(k == 15))
            return last
        bld.op('pe', omm, reads=[yzfm.b, wout.b], writes=[p[3].b, pb2.b])
        bld.op('sp', lambda e: e.dma_start(out=xres.ap, in_=u[rtok, :]),
               reads=[u.b], writes=[xres.b])
        for n in range(2):
            pn = p[3] if n == 0 else pb2
            bld.op('dve', lambda e, n=n, pn=pn: e.scalar_tensor_tensor(
                       out=xnew[:, n * 512:(n + 1) * 512],
                       in0=pn[0:128, 0:512],
                       scalar=1.0, in1=xres[:, n * 512:(n + 1) * 512],
                       op0=ALU.mult, op1=ALU.add),
                   reads=[pn.b, xres.b], writes=[xnew.b])
        emit_ln(bld, C, xnew, xout, stat, st1, 1024, pa)
        bld.op('actd', lambda e: e.dma_start(out=out_dram[rtok, :], in_=xout.ap),
               reads=[xout.b], writes=[out_dram.b])
        bld.op('poold', lambda e: e.dma_start(out=Dm['xbf'][rtok, :], in_=xout.ap),
               reads=[xout.b], writes=[Dm['xbf'].b])

    for tc in range(NCH):
        a3_body(tc, acs_tR(tc), negacsR(tc), asum_sbR(tc), asum_bcR(tc),
                dec128R(tc), dcolR(tc), dtdecR(tc), outdecR(tc), acsfmR(tc),
                rbR(tc), lallR(tc), xfmcR(tc), bfmR(tc), cfmR(tc), xtmR(tc),
                btmR(tc), gmaskR(tc), xdR(tc), xd2R(tc), yallR(tc), szR(tc))
    bld.pop_scope()
    bld.pop_scope()


def emit_ln(bld, C, xin, xout, stat, st1, n, psum32):
    bld.op('dve', lambda e: e.reduce_sum(out=stat[:, 0:1], in_=xin[:, 0:n], axis=AX.X),
           reads=[xin.b], writes=[stat.b])
    bld.op('act', lambda e: e.activation(out=xout[:, 0:n], in_=xin[:, 0:n], func=AF.Square,
                                         accum_out=stat[:, 1:2]),
           reads=[xin.b], writes=[xout.b, stat.b])
    bld.op('dve', lambda e: e.tensor_scalar(out=stat[:, 2:3], in0=stat[:, 0:1],
                                            scalar1=1.0 / n, scalar2=None, op0=ALU.mult),
           reads=[stat.b], writes=[stat.b])
    bld.op('dve', lambda e: e.tensor_scalar(out=stat[:, 3:4], in0=stat[:, 1:2],
                                            scalar1=1.0 / n, scalar2=None, op0=ALU.mult),
           reads=[stat.b], writes=[stat.b])
    bld.op('dve', lambda e: e.tensor_tensor(out=stat[:, 4:5], in0=stat[:, 2:3],
                                            in1=stat[:, 2:3], op=ALU.mult),
           reads=[stat.b], writes=[stat.b])
    bld.op('dve', lambda e: e.tensor_tensor(out=stat[:, 5:6], in0=stat[:, 3:4],
                                            in1=stat[:, 4:5], op=ALU.subtract),
           reads=[stat.b], writes=[stat.b])
    bld.op('act', lambda e: e.activation(out=stat[:, 6:7], in_=stat[:, 5:6],
                                         func=AF.Ln, bias=C['cb'][:, 0:1]),
           reads=[stat.b, C['cb'].b], writes=[stat.b])
    bld.op('act', lambda e: e.activation(out=stat[:, 7:8], in_=stat[:, 6:7],
                                         func=AF.Exp, scale=-0.5),
           reads=[stat.b], writes=[stat.b])
    bld.op('dve', lambda e: e.tensor_scalar(out=xout[:, 0:n], in0=xin[:, 0:n],
                                            scalar1=stat[:, 2:3], scalar2=stat[:, 7:8],
                                            op0=ALU.subtract, op1=ALU.mult),
           reads=[xin.b, stat.b], writes=[xout.b])



class In:
    def __init__(self, nc, name, shape, dtype=F32):
        self.t = nc.dram_tensor(name, list(shape), dtype, kind='ExternalInput')
        self.a = self.t.ap()
        self.b = Buf(name)
    def __getitem__(self, idx):
        return self.a[idx]


def _mk_common(nc, stack):
    bld = Builder(nc, stack)
    I = {}
    I['u'] = In(nc, 'u', (T, D))
    for name, shape in INPUT_SPECS:
        I[name] = In(nc, name, shape)
    for name, shape, dt_ in CONST_SPECS:
        I[name] = In(nc, name, shape, BF16 if dt_ == 'bf16' else F32)
    C = {}
    def load_const(cname, iname, shape, dtype):
        sb = SB(bld, cname, list(shape), dtype)
        bld.op('sp', lambda e: e.dma_start(out=sb.ap, in_=I[iname][:, :]),
               reads=[I[iname].b], writes=[sb.b])
        C[cname] = sb
    load_const('ident_f32', 'c_ident_f32', (128, 128), F32)
    load_const('ident_bf16', 'c_ident_bf16', (128, 128), BF16)
    load_const('tri_f32', 'c_tri_f32', (128, 128), F32)
    load_const('triu_bf16', 'c_triu_bf16', (128, 128), BF16)
    load_const('ones_f32', 'c_ones_f32', (128, 128), F32)
    load_const('maskb_f32', 'c_maskb_f32', (128, 128), F32)
    cb = SB(bld, 'c_bias', [128, 4], F32)
    bld.op('dve', lambda e: e.memset(cb[:, 0:1], 1e-5), reads=[], writes=[cb.b])
    bld.op('dve', lambda e: e.memset(cb[:, 1:2], 1.0), reads=[], writes=[cb.b])
    bld.op('dve', lambda e: e.memset(cb[:, 2:3], 0.5), reads=[], writes=[cb.b])
    C['cb'] = cb
    C['p'] = [PS(bld, f'p{i}', [128, 512], F32) for i in range(6)]
    C['pbx'] = PS(bld, 'pbx', [128, 2048], BF16)
    return bld, I, C


def mk_dm(nc):
    return {
        'xres': DR(nc, 'd_xres', [T, D], F32),
        'xbf': DR(nc, 'd_xbf', [T, D], BF16),
        'z': DR(nc, 'd_z', [T, DIN], F32),
        'xbc': DR(nc, 'd_xbc', [CONVD, T], F32),
        'xc': DR(nc, 'd_xc', [CONVD, T], BF16),
        'acs': DR(nc, 'd_acs', [NCH, NH, CH], F32),
        'wqg': DR(nc, 'd_wqg', [4096, 1024], BF16),
        'wqu': DR(nc, 'd_wqu', [4096, 1024], BF16),
        'wqd': DR(nc, 'd_wqd', [1024, 4096], BF16),
        'xq': DR(nc, 'd_xq', [T, 1024], BF16),
        'sg': DR(nc, 'd_sg', [T, 4096], F32),
        'oB': DR(nc, 'd_oB', [T, 4096], BF16),
        'q': DR(nc, 'd_q', [T, 1024], F32),
        'k': DR(nc, 'd_k', [T, 1024], F32),
        'v': DR(nc, 'd_v', [T, 1024], BF16),
        'att': DR(nc, 'd_att', [T, 1024], BF16),
        'wqu2': DR(nc, 'd_wqu2', [3072, 1024], BF16),
        'wqg2': DR(nc, 'd_wqg2', [3072, 1024], BF16),
        'wqd2': DR(nc, 'd_wqd2', [1024, 3072], BF16),
        'up': DR(nc, 'd_up', [T, 3072], F32),
        'gt': DR(nc, 'd_gt', [T, 3072], F32),
        'oD': DR(nc, 'd_oD', [T, 3072], BF16),
        'ccin': DR(nc, 'd_ccin', [1, 8], F32),
        'ccout': DR(nc, 'd_ccout', [1, 8], F32),
    }


def build_program(stages=('a', 'b', 'c', 'd')):
    nc = bass.Bass('TRN2', target_bir_lowering=False)
    stack = contextlib.ExitStack()
    bld, I, C = _mk_common(nc, stack)
    y = DR(nc, 'y', [T, D], F32, kind='ExternalOutput')

    Dm = mk_dm(nc)

    cur = I['u']
    if 'a' not in stages:
        xtmp = SB(bld, 'init_x', [128, 1024], F32)
        for tc in range(32):
            r = slice(tc * 128, (tc + 1) * 128)
            bld.op('sp', lambda e, r=r: e.dma_start(out=xtmp.ap, in_=I['u'][r, :]),
                   reads=[I['u'].b], writes=[xtmp.b])
            bld.op('poold', lambda e, r=r: e.dma_start(out=Dm['xbf'][r, :], in_=xtmp.ap),
                   reads=[xtmp.b], writes=[Dm['xbf'].b])
    if 'a' in stages:
        out = y if stages[-1] == 'a' else Dm['xres']
        emit_stage_a(bld, C, I, Dm, out)
        cur = out
    if 'b' in stages:
        out = y if stages[-1] == 'b' else Dm['xres']
        emit_stage_b(bld, C, I, Dm, cur, out)
        cur = out
    if 'c' in stages:
        out = y if stages[-1] == 'c' else Dm['xres']
        emit_stage_c(bld, C, I, Dm, cur, out)
        cur = out
    if 'd' in stages:
        emit_stage_d(bld, C, I, Dm, cur, y)
        cur = y
    if cur is not y:
        raise RuntimeError('last stage must write y')

    bld.barrier()

    bld.emit_all()
    return nc, stack


INPUT_SPECS = [
    ('L0_attn_W_in', (1024, 4384)),
    ('L0_attn_conv_w', (2304, 4)),
    ('L0_attn_conv_b', (2304,)),
    ('L0_attn_dt_bias', (32,)),
    ('L0_attn_A_log', (32,)),
    ('L0_attn_norm_w', (2048,)),
    ('L0_attn_W_out', (2048, 1024)),
    ('L0_ffn_w_gate', (4096, 1024)),
    ('L0_ffn_w_up', (4096, 1024)),
    ('L0_ffn_w_down', (1024, 4096)),
    ('L1_attn_Wq', (1024, 1024)),
    ('L1_attn_Wk', (1024, 1024)),
    ('L1_attn_Wv', (1024, 1024)),
    ('L1_attn_Wo', (1024, 1024)),
    ('L1_attn_betas', (1, 16, 1, 64)),
    ('L1_attn_memory0', (1, 16, 64, 64)),
    ('L1_attn_z0', (1, 16, 1, 64)),
    ('L1_ffn_w_gate', (3072, 1024)),
    ('L1_ffn_w_up', (3072, 1024)),
    ('L1_ffn_w_down', (1024, 3072)),
]

CONST_SPECS = [
    ('c_ident_f32', (128, 128), 'f32'),
    ('c_ident_bf16', (128, 128), 'bf16'),
    ('c_tri_f32', (128, 128), 'f32'),
    ('c_triu_bf16', (128, 128), 'bf16'),
    ('c_ones_f32', (128, 128), 'f32'),
    ('c_maskb_f32', (128, 128), 'f32'),
]


def host_constants():
    ii = np.arange(128)
    c = {}
    c['c_ident_f32'] = np.eye(128, dtype=np.float32)
    c['c_ident_bf16'] = np.eye(128, dtype=ml_dtypes.bfloat16)
    c['c_tri_f32'] = (ii[:, None] <= ii[None, :]).astype(np.float32)
    c['c_triu_bf16'] = (ii[:, None] <= ii[None, :]).astype(ml_dtypes.bfloat16)
    c['c_ones_f32'] = np.ones((128, 128), np.float32)
    c['c_maskb_f32'] = np.where(ii[:, None] > ii[None, :], -1e30, 0.0).astype(np.float32)
    return c


def build_launch(li):
    nc = bass.Bass('TRN2', target_bir_lowering=False)
    stack = contextlib.ExitStack()
    bld, I, C = _mk_common(nc, stack)

    def mkin(name, shape, dtype=F32):
        return In(nc, name, shape, dtype)
    def mkout(name, shape, dtype=F32):
        return DR(nc, name, list(shape), dtype, kind='ExternalOutput')

    stat = SB(bld, 'ml_stat', [128, 8], F32)
    st1 = SB(bld, 'ml_st1', [1, 8], F32)
    gsc = SB(bld, 'ml_gsc', [1, 8], F32)
    bcs = SB(bld, 'ml_bcs', [128, 16], F32)
    p5 = C['p'][5]

    if li == 1:
        y = mkout('y', (T, D))
        mx1 = mkout('mx1', (1, 1))
        Dm = mk_dm(nc)
        wqs = emit_b_wq(bld, C, I, Dm)
        emit_stage_a(bld, C, I, Dm, Dm['xres'])
        emit_stage_b(bld, C, I, Dm, Dm['xres'], Dm['xres'], wqs)
        emit_stage_c(bld, C, I, Dm, Dm['xres'], y)
        emit_d_maxx(bld, C, Dm, y, st1, stat)
        gmax_out(bld, st1[0:1, 0:1], st1.b, 1, mx1)
    elif li == 2:
        xres = mkin('xres_i', (T, D))
        gmx1 = mkin('gmx1', (1, 8))
        up_o = mkout('up_o', (T, 3072))
        gt_o = mkout('gt_o', (T, 3072))
        m2 = mkout('m2', (1, 2))
        Dm = mk_dm(nc)
        gmax_in(bld, C, gmx1, 1, gsc, 0)
        emit_d_scales(bld, C, p5, gsc, 0, 0, 1, bcs)
        emit_d_wq(bld, C, I, Dm, [(I['L1_ffn_w_up'], Dm['wqu2'], 24, 1024),
                                  (I['L1_ffn_w_gate'], Dm['wqg2'], 24, 1024)])
        emit_d_xq(bld, C, Dm, xres, bcs)
        emit_d_upgate(bld, C, Dm, bcs, st1, stat, up_o, gt_o)
        gmax_out(bld, st1[0:1, 0:2], st1.b, 2, m2)
    elif li == 3:
        gt_i = mkin('gt_i', (T, 3072))
        gmx2 = mkin('gmx2', (2, 8))
        g2_o = mkout('g2_o', (T, 3072))
        m3 = mkout('m3', (1, 1))
        gmax_in(bld, C, gmx2, 2, gsc, 0)
        emit_d_scales(bld, C, p5, gsc, 1, 4, 5, bcs)
        emit_d_g2raw(bld, C, bcs, st1, stat, gt_i, g2_o)
        gmax_out(bld, st1[0:1, 0:1], st1.b, 1, m3)
    elif li == 4:
        g2_i = mkin('g2_i', (T, 3072))
        up_i = mkin('up_i', (T, 3072))
        gmx2 = mkin('gmx2', (2, 8))
        gmx3 = mkin('gmx3', (1, 8))
        or_o = mkout('or_o', (T, 3072))
        m4 = mkout('m4', (1, 1))
        gmax_in(bld, C, gmx2, 2, gsc, 0)
        gmax_in(bld, C, gmx3, 1, gsc, 2)
        emit_d_scales(bld, C, p5, gsc, 0, 2, 3, bcs)
        emit_d_scales(bld, C, p5, gsc, 2, 6, 7, bcs)
        emit_d_oraw(bld, C, bcs, st1, stat, g2_i, up_i, or_o)
        gmax_out(bld, st1[0:1, 0:1], st1.b, 1, m4)
    elif li == 5:
        or_i = mkin('or_i', (T, 3072))
        xres = mkin('xres_i', (T, D))
        gmx4 = mkin('gmx4', (1, 8))
        y = mkout('y', (T, D))
        Dm = mk_dm(nc)
        gmax_in(bld, C, gmx4, 1, gsc, 0)
        emit_d_scales(bld, C, p5, gsc, 0, 8, 9, bcs)
        emit_d_wq(bld, C, I, Dm, [(I['L1_ffn_w_down'], Dm['wqd2'], 8, 3072)])
        emit_d_down(bld, C, Dm, bcs, stat, st1, xres, or_i, y)
    bld.barrier()
    bld.emit_all()
    return nc, stack


LAUNCH_MODE = 'multi'


def kernel(**inputs):
    stages = inputs.pop('_stages', None)
    mode = inputs.pop('_mode', LAUNCH_MODE)
    consts = host_constants()
    u_full = np.ascontiguousarray(inputs['L0_attn_u'], dtype=np.float32)
    base = {}
    for name, _ in INPUT_SPECS:
        base[name] = np.ascontiguousarray(inputs[name], dtype=np.float32)
    base.update(consts)

    def run(nc, stack, extra_per_core):
        in_maps = []
        for core in range(NCORES):
            m = dict(base)
            m['u'] = u_full[core]
            m.update(extra_per_core[core])
            in_maps.append(m)
        try:
            res = run_bass_kernel_spmd(nc, in_maps, core_ids=list(range(NCORES)))
        finally:
            stack.close()
        return res.results

    if stages is not None or mode == 'fused':
        nc, stack = build_program(stages or ('a', 'b', 'c', 'd'))
        res = run(nc, stack, [{} for _ in range(NCORES)])
        return np.stack([np.asarray(r['y']) for r in res], 0).astype(np.float32)

    nc, stack = build_launch(1)
    r1 = run(nc, stack, [{} for _ in range(NCORES)])
    xres = [np.asarray(r['y']) for r in r1]
    gmx1 = np.stack([np.asarray(r['mx1'])[0] for r in r1], 1)

    nc, stack = build_launch(2)
    r2 = run(nc, stack, [{'xres_i': xres[c], 'gmx1': gmx1} for c in range(NCORES)])
    up = [np.asarray(r['up_o']) for r in r2]
    gt = [np.asarray(r['gt_o']) for r in r2]
    gmx2 = np.stack([np.asarray(r['m2'])[0] for r in r2], 1)

    nc, stack = build_launch(3)
    r3 = run(nc, stack, [{'gt_i': gt[c], 'gmx2': gmx2} for c in range(NCORES)])
    g2 = [np.asarray(r['g2_o']) for r in r3]
    gmx3 = np.stack([np.asarray(r['m3'])[0] for r in r3], 1)

    nc, stack = build_launch(4)
    r4 = run(nc, stack, [{'g2_i': g2[c], 'up_i': up[c], 'gmx2': gmx2,
                          'gmx3': gmx3} for c in range(NCORES)])
    orr = [np.asarray(r['or_o']) for r in r4]
    gmx4 = np.stack([np.asarray(r['m4'])[0] for r in r4], 1)

    nc, stack = build_launch(5)
    r5 = run(nc, stack, [{'or_i': orr[c], 'xres_i': xres[c], 'gmx4': gmx4}
                         for c in range(NCORES)])
    return np.stack([np.asarray(r['y']) for r in r5], 0).astype(np.float32)


def emit_b_wq(bld, C, I, Dm):
    p = C['p']
    cb = C['cb']
    minv = SB(bld, 'b_minv', [128, 4], F32)
    scw = SB(bld, 'b_scw', [128, 4], F32)
    WQ = [(I['L0_ffn_w_gate'], Dm['wqg'], 32, 1024, 0),
          (I['L0_ffn_w_up'], Dm['wqu'], 32, 1024, 1),
          (I['L0_ffn_w_down'], Dm['wqd'], 8, 4096, 2)]
    bld.push_scope()
    wld = SB(bld, 'b_wld', [128, 4096], F32)
    wsum = SB(bld, 'b_wsum', [128, 32], F32)
    wq_t = SB(bld, 'b_wqt', [128, 4096], F32)
    wq_i = SB(bld, 'b_wqi', [128, 4096], I32)
    wq_b = SB(bld, 'b_wqb', [128, 4096], BF16)
    tot = SB(bld, 'b_tot', [128, 1], F32)
    sc1 = SB(bld, 'b_sc1', [1, 4], F32)
    for (W, dq, nr, ncol, wi) in WQ:
        for r in range(nr):
            bld.op('sp', lambda e, W=W, r=r, ncol=ncol: e.dma_start(
                       out=wld[:, 0:ncol], in_=W[r * 128:(r + 1) * 128, :]),
                   reads=[W.b], writes=[wld.b])
            bld.op('dve', lambda e, r=r, ncol=ncol: e.tensor_reduce(
                       out=wsum[:, r:r + 1], in_=wld[:, 0:ncol], axis=AX.X,
                       op=ALU.add, apply_absolute_value=True),
                   reads=[wld.b], writes=[wsum.b])
        bld.op('dve', lambda e, nr=nr: e.tensor_reduce(out=tot.ap, in_=wsum[:, 0:nr],
                                                       axis=AX.X, op=ALU.add),
               reads=[wsum.b], writes=[tot.b])
        part_reduce(bld, C, p[5], tot.ap, tot.b, sc1, sc1[0:1, 0:1], ALU.add)
        bld.op('dve', lambda e, nr=nr, ncol=ncol: e.tensor_scalar(
                   out=sc1[0:1, 1:2], in0=sc1[0:1, 0:1],
                   scalar1=1.0 / (nr * 128.0 * ncol), scalar2=1e-5,
                   op0=ALU.mult, op1=ALU.max),
               reads=[sc1.b], writes=[sc1.b])
        bld.op('dve', lambda e: e.reciprocal(out=sc1[0:1, 2:3], in_=sc1[0:1, 1:2]),
               reads=[sc1.b], writes=[sc1.b])
        bc_row(bld, C, p[5], (minv, minv[:, wi:wi + 1]), sc1[0:1, 1:2], 1, [sc1.b])
        bc_row(bld, C, p[5], (scw, scw[:, wi:wi + 1]), sc1[0:1, 2:3], 1, [sc1.b])
        for r in range(nr):
            bld.op('sp', lambda e, W=W, r=r, ncol=ncol: e.dma_start(
                       out=wld[:, 0:ncol], in_=W[r * 128:(r + 1) * 128, :]),
                   reads=[W.b], writes=[wld.b])
            bld.op('act', lambda e, wi=wi, ncol=ncol: e.activation(
                       out=wq_t[:, 0:ncol], in_=wld[:, 0:ncol], func=AF.Identity,
                       scale=scw[:, wi:wi + 1]),
                   reads=[wld.b, scw.b], writes=[wq_t.b])
            bld.op('dve', lambda e, ncol=ncol: e.tensor_copy(
                       out=wq_i[:, 0:ncol], in_=wq_t[:, 0:ncol]),
                   reads=[wq_t.b], writes=[wq_i.b])
            bld.op('dve', lambda e, ncol=ncol: e.tensor_scalar(
                       out=wq_b[:, 0:ncol], in0=wq_i[:, 0:ncol], scalar1=-1.0,
                       scalar2=1.0, op0=ALU.max, op1=ALU.min),
                   reads=[wq_i.b], writes=[wq_b.b])
            bld.op('actd', lambda e, dq=dq, r=r, ncol=ncol: e.dma_start(
                       out=dq[r * 128:(r + 1) * 128, :], in_=wq_b[:, 0:ncol]),
                   reads=[wq_b.b], writes=[dq.b])
    bld.pop_scope()

    return minv, scw


def emit_stage_b(bld, C, I, Dm, cur, out_dram, wqs=None):
    nc = bld.nc
    if wqs is None:
        wqs = emit_b_wq(bld, C, I, Dm)
    minv, scw = wqs
    bld.push_scope()
    p = C['p']
    ident_f = C['ident_f32']
    ones = C['ones_f32']
    cb = C['cb']

    invx = SB(bld, 'b_invx', [128, 32], F32)
    invgu = SB(bld, 'b_invgu', [128, 32], F32)
    cmb = SB(bld, 'b_cmb', [128, 3 * 32], F32)
    stat = SB(bld, 'b_stat', [128, 8], F32)
    st1 = SB(bld, 'b_st1', [1, 8], F32)

    bld.push_scope()
    xtR = Rot(bld, 'b_xt', [128, 1024], F32, 3)
    qtR = Rot(bld, 'b_qt', [128, 1024], F32, 2)
    qiR = Rot(bld, 'b_qi', [128, 1024], I32, 2)
    qbR = Rot(bld, 'b_qb', [128, 1024], BF16, 2)
    mcolR = Rot(bld, 'b_mcol', [128, 4], F32, 2)
    def b0_body(tc, xt, qt, qi, qb, mcol):
        rtok = slice(tc * 128, (tc + 1) * 128)
        bld.op('sp', lambda e: e.dma_start(out=xt.ap, in_=cur[rtok, :]),
               reads=[cur.b], writes=[xt.b])
        bld.op('dve', lambda e: e.tensor_reduce(out=mcol[:, 0:1], in_=xt.ap, axis=AX.X,
                                                op=ALU.max, apply_absolute_value=True),
               reads=[xt.b], writes=[mcol.b])
        bld.op('dve', lambda e: e.tensor_scalar(out=mcol[:, 1:2], in0=mcol[:, 0:1],
                                                scalar1=1e-5, scalar2=None, op0=ALU.max),
               reads=[mcol.b], writes=[mcol.b])
        bld.op('dve', lambda e: e.reciprocal(out=mcol[:, 2:3], in_=mcol[:, 1:2]),
               reads=[mcol.b], writes=[mcol.b])
        bld.op('dve', lambda e: e.tensor_scalar(out=mcol[:, 3:4], in0=mcol[:, 2:3],
                                                scalar1=127.0, scalar2=None, op0=ALU.mult),
               reads=[mcol.b], writes=[mcol.b])
        bld.op('dve', lambda e: e.tensor_scalar(out=invx[:, tc:tc + 1],
                                                in0=mcol[:, 1:2],
                                                scalar1=1.0 / 127.0, scalar2=None,
                                                op0=ALU.mult),
               reads=[mcol.b], writes=[invx.b])
        bld.op('act', lambda e: e.activation(out=qt.ap, in_=xt.ap, func=AF.Identity,
                                             scale=mcol[:, 3:4]),
               reads=[xt.b, mcol.b], writes=[qt.b])
        bld.op('pool', lambda e: e.tensor_copy(out=qi.ap, in_=qt.ap),
               reads=[qt.b], writes=[qi.b])
        bld.op('dve', lambda e: e.tensor_copy(out=qb.ap, in_=qi.ap),
               reads=[qi.b], writes=[qb.b])
        bld.op('actd', lambda e: e.dma_start(out=Dm['xq'][rtok, :], in_=qb.ap),
               reads=[qb.b], writes=[Dm['xq'].b])
    for tc in range(32):
        b0_body(tc, xtR(tc), qtR(tc), qiR(tc), qbR(tc), mcolR(tc))
    for wi in range(2):
        bld.op('dve', lambda e, wi=wi: e.tensor_tensor(
                   out=cmb.t[:].rearrange('p (w c) -> p w c', c=32)[:, wi, :],
                   in0=invx.ap,
                   in1=minv[:, wi:wi + 1].broadcast_to([128, 32]), op=ALU.mult),
               reads=[invx.b, minv.b], writes=[cmb.b])

    bld.pop_scope()

    bld.push_scope()
    xqf = SB(bld, 'b_xqf', [128, 8 * T], BF16)
    wstr = SB(bld, 'b_wstr', [128, 8 * 4096], BF16)
    guR = Rot(bld, 'b_gu', [128, 4096], F32, 2)
    guiR = Rot(bld, 'b_gui', [128, 2048], I32, 1)
    gubR = Rot(bld, 'b_gub', [128, 2048], BF16, 1)
    sgbR = Rot(bld, 'b_sgb', [128, 512], F32, 2)
    usbR = Rot(bld, 'b_usb', [128, 512], F32, 2)
    statR2 = Rot(bld, 'b_st2', [128, 8], F32, 2)
    xqf_rb = [Buf(f'bxqf{i}') for i in range(4)]
    wstr_rb = [Buf(f'bwstr{i}') for i in range(4)]
    striped_load(bld, xqf_rb,
                 [(lambda e, j=j: e.dma_start(
                       out=xqf[:, j * T:(j + 1) * T],
                       in_=Dm['xq'][:, j * 128:(j + 1) * 128], transpose=True))
                  for j in range(8)],
                 reads=[Dm['xq'].b])
    def b12_body(phase, tc, gu, gui, gub, stat_):
        rtok = slice(tc * 128, (tc + 1) * 128)
        for og in range(2):
            def gmm(e, tc=tc, og=og):
                last = None
                for i in range(8):
                    e.ldweights(xqf[:, i * T + tc * 128:i * T + (tc + 1) * 128])
                    for op_ in range(4):
                        last = e.matmul(
                            out=p[op_][0:128, 0:512],
                            lhsT=xqf[:, i * T + tc * 128:i * T + (tc + 1) * 128],
                            rhs=wstr[:, i * 4096 + og * 2048 + op_ * 512:
                                     i * 4096 + og * 2048 + (op_ + 1) * 512],
                            start=(i == 0), stop=(i == 7))
                return last
            bld.op('pe', gmm, reads=[*xqf_rb, *wstr_rb],
                   writes=[p[0].b, p[1].b, p[2].b, p[3].b])
            for op_ in range(4):
                ocol = og * 2048 + op_ * 512
                psl = p[op_][0:128, 0:512]
                pbf = p[op_].b
                if phase == 0:
                    sgb = sgbR(tc * 4 + op_)
                    bld.op('act', lambda e, psl=psl, sgb=sgb: e.activation(
                               out=sgb.ap, in_=psl, func=AF.Silu,
                               scale=cmb[:, tc:tc + 1]),
                           reads=[pbf, cmb.b], writes=[sgb.b])
                    bld.op('actd', lambda e, rtok=rtok, ocol=ocol, sgb=sgb: e.dma_start(
                               out=Dm['sg'][rtok, ocol:ocol + 512], in_=sgb.ap),
                           reads=[sgb.b], writes=[Dm['sg'].b])
                else:
                    usb = usbR(tc * 4 + op_)
                    sgb = sgbR(tc * 4 + op_)
                    bld.op('act', lambda e, psl=psl, usb=usb: e.activation(
                               out=usb.ap, in_=psl, func=AF.Identity,
                               scale=cmb[:, 32 + tc:32 + tc + 1]),
                           reads=[pbf, cmb.b], writes=[usb.b])
                    bld.op('sp', lambda e, rtok=rtok, ocol=ocol, sgb=sgb: e.dma_start(
                               out=sgb.ap, in_=Dm['sg'][rtok, ocol:ocol + 512]),
                           reads=[Dm['sg'].b], writes=[sgb.b])
                    bld.op('dve', lambda e, ocol=ocol, usb=usb, sgb=sgb: e.tensor_tensor(
                               out=gu[:, ocol:ocol + 512], in0=usb.ap, in1=sgb.ap,
                               op=ALU.mult),
                           reads=[usb.b, sgb.b], writes=[gu.b])
        if phase == 1:
            bld.op('dve', lambda e: e.tensor_reduce(out=stat_[:, 0:1], in_=gu.ap,
                                                    axis=AX.X, op=ALU.max,
                                                    apply_absolute_value=True),
                   reads=[gu.b], writes=[stat_.b])
            bld.op('dve', lambda e: e.tensor_scalar(out=stat_[:, 1:2], in0=stat_[:, 0:1],
                                                    scalar1=1e-5, scalar2=None,
                                                    op0=ALU.max),
                   reads=[stat_.b], writes=[stat_.b])
            bld.op('dve', lambda e: e.reciprocal(out=stat_[:, 2:3], in_=stat_[:, 1:2]),
                   reads=[stat_.b], writes=[stat_.b])
            bld.op('dve', lambda e: e.tensor_scalar(out=stat_[:, 3:4], in0=stat_[:, 2:3],
                                                    scalar1=127.0, scalar2=None,
                                                    op0=ALU.mult),
                   reads=[stat_.b], writes=[stat_.b])
            bld.op('dve', lambda e: e.tensor_scalar(
                       out=invgu[:, tc:tc + 1], in0=stat_[:, 1:2],
                       scalar1=1.0 / 127.0, scalar2=None, op0=ALU.mult),
                   reads=[stat_.b], writes=[invgu.b])
            bld.op('act', lambda e: e.activation(out=gu.ap, in_=gu.ap, func=AF.Identity,
                                                 scale=stat_[:, 3:4]),
                   reads=[gu.b, stat_.b], writes=[gu.b])
            for hf in range(2):
                hs = slice(hf * 2048, (hf + 1) * 2048)
                bld.op('pool', lambda e, hs=hs: e.tensor_copy(out=gui.ap,
                           in_=gu[:, hs]),
                       reads=[gu.b], writes=[gui.b])
                bld.op('dve', lambda e: e.tensor_copy(out=gub.ap, in_=gui.ap),
                       reads=[gui.b], writes=[gub.b])
                bld.op('poold', lambda e, rtok=rtok, hs=hs: e.dma_start(
                           out=Dm['oB'][rtok, hs], in_=gub.ap),
                       reads=[gub.b], writes=[Dm['oB'].b])

    for phase in range(2):
        dq = Dm['wqg'] if phase == 0 else Dm['wqu']
        striped_load(bld, wstr_rb,
                     [(lambda e, j=j, dq=dq: e.dma_start(
                           out=wstr[:, j * 4096:(j + 1) * 4096],
                           in_=dq[:, j * 128:(j + 1) * 128], transpose=True))
                      for j in range(8)],
                     reads=[dq.b])
        for tc in range(32):
            b12_body(phase, tc, guR(tc), guiR(tc), gubR(tc), statR2(tc))
    bld.pop_scope()

    bld.push_scope()
    ostr = SB(bld, 'b_ostr', [128, 32 * 1024], BF16)
    wdt = SB(bld, 'b_wdt', [128, 32 * 1024], BF16)
    cmbd = SB(bld, 'b_cmbd', [128, 32], F32)
    xresR3 = Rot(bld, 'b_xr', [128, 1024], F32, 2)
    xnewR3 = Rot(bld, 'b_xn', [128, 1024], F32, 2)
    xoutR3 = Rot(bld, 'b_xo', [128, 1024], F32, 2)
    statR3 = Rot(bld, 'b_st3', [128, 8], F32, 2)
    st1R3 = Rot(bld, 'b_s13', [1, 8], F32, 2)
    bld.op('dve', lambda e: e.tensor_tensor(
               out=cmbd.ap, in0=invgu.ap,
               in1=minv[:, 2:3].broadcast_to([128, 32]), op=ALU.mult),
           reads=[invgu.b, minv.b], writes=[cmbd.b])

    wdt_rb = [Buf(f'bwdt{g}') for g in range(8)]
    ostr_rb = [Buf(f'bostr{g}') for g in range(8)]
    striped_load(bld, wdt_rb,
                 [(lambda e, k=k: e.dma_start(
                       out=wdt[:, k * 1024:(k + 1) * 1024],
                       in_=Dm['wqd'][:, k * 128:(k + 1) * 128], transpose=True))
                  for k in range(32)],
                 reads=[Dm['wqd'].b])
    for w in range(4):
        striped_load(bld, ostr_rb,
                     [(lambda e, w=w, k=k: e.dma_start(
                           out=ostr[:, k * 1024:(k + 1) * 1024],
                           in_=Dm['oB'][w * 1024:(w + 1) * 1024,
                                        k * 128:(k + 1) * 128],
                           transpose=True))
                      for k in range(32)],
                     reads=[Dm['oB'].b])
        def b3_body(tci, xres, xnew, xout, stat, st1):
            tc = w * 8 + tci
            rtok = slice(tc * 128, (tc + 1) * 128)
            pln = p[5] if tc % 2 == 0 else p[4]
            def dmm(e):
                last = None
                for k in range(32):
                    e.ldweights(ostr[:, k * 1024 + tci * 128:k * 1024 + (tci + 1) * 128])
                    for np_ in range(2):
                        last = e.matmul(
                            out=p[np_][0:128, 0:512],
                            lhsT=ostr[:, k * 1024 + tci * 128:k * 1024 + (tci + 1) * 128],
                            rhs=wdt[:, k * 1024 + np_ * 512:k * 1024 + (np_ + 1) * 512],
                            start=(k == 0), stop=(k == 31))
                return last
            bld.op('pe', dmm, reads=[*ostr_rb, *wdt_rb], writes=[p[0].b, p[1].b])
            bld.op('sp', lambda e: e.dma_start(out=xres.ap, in_=cur[rtok, :]),
                   reads=[cur.b], writes=[xres.b])
            for np_ in range(2):
                bld.op('dve', lambda e, np_=np_: e.tensor_scalar(
                           out=xnew[:, np_ * 512:(np_ + 1) * 512],
                           in0=p[np_][0:128, 0:512], scalar1=cmbd[:, tc:tc + 1],
                           scalar2=None, op0=ALU.mult),
                       reads=[p[np_].b, cmbd.b], writes=[xnew.b])
            bld.op('dve', lambda e: e.tensor_tensor(out=xnew.ap, in0=xnew.ap, in1=xres.ap,
                                                    op=ALU.add),
                   reads=[xnew.b, xres.b], writes=[xnew.b])
            emit_ln(bld, C, xnew, xout, stat, st1, 1024, pln)
            bld.op('actd', lambda e: e.dma_start(out=out_dram[rtok, :],
                                                 in_=xout.ap),
                   reads=[xout.b], writes=[out_dram.b])
            bld.op('poold', lambda e: e.dma_start(out=Dm['xbf'][rtok, :],
                                                  in_=xout.ap),
                   reads=[xout.b], writes=[Dm['xbf'].b])
        for tci in range(8):
            tc = w * 8 + tci
            b3_body(tci, xresR3(tc), xnewR3(tc), xoutR3(tc), statR3(tc), st1R3(tc))
    bld.pop_scope()
    bld.pop_scope()

def emit_stage_c(bld, C, I, Dm, cur, out_dram):
    nc = bld.nc
    bld.push_scope()
    p = C['p']
    pbx = C['pbx']
    ident_f = C['ident_f32']
    ident_b = C['ident_bf16']
    maskb = C['maskb_f32']
    ones = C['ones_f32']
    cb = C['cb']
    H, DK, DV, LSEG, NSEG = 16, 64, 64, 512, 8

    stat = SB(bld, 'c_stat', [128, 8], F32)
    st1 = SB(bld, 'c_st1', [1, 8], F32)
    xres = SB(bld, 'c_xr', [128, 1024], F32)
    xnew = SB(bld, 'c_xn', [128, 1024], F32)
    xout = SB(bld, 'c_xo', [128, 1024], F32)

    bld.push_scope()
    xf = SB(bld, 'c_xf', [128, 8 * T], BF16)
    wpj = SB(bld, 'c_wpj', [128, 8 * 1024], BF16)
    evR = Rot(bld, 'c_ev', [128, 512], F32, 3)
    evbR = Rot(bld, 'c_evb', [128, 512], BF16, 3)
    xf_rb = [Buf(f'cxf{i}') for i in range(4)]
    striped_load(bld, xf_rb,
                 [(lambda e, j=j: e.dma_start(
                       out=xf[:, j * T:(j + 1) * T],
                       in_=Dm['xbf'][:, j * 128:(j + 1) * 128], transpose=True))
                  for j in range(8)],
                 reads=[Dm['xbf'].b])
    for (W, dst, isbf) in [(I['L1_attn_Wq'], Dm['q'], False),
                           (I['L1_attn_Wk'], Dm['k'], False),
                           (I['L1_attn_Wv'], Dm['v'], True)]:
        wa = W.a.rearrange('(k p) m -> p k m', p=128)
        for k in range(8):
            bld.op('poold', lambda e, k=k, wa=wa: e.dma_start(
                       out=wpj[:, k * 1024:(k + 1) * 1024], in_=wa[:, k, :]),
                   reads=[W.b], writes=[wpj.b])
        for tc in range(32):
            rtok = slice(tc * 128, (tc + 1) * 128)
            pb = 0 if tc % 2 == 0 else 2
            def pmm(e, tc=tc, pb=pb):
                last = None
                for k in range(8):
                    e.ldweights(xf[:, k * T + tc * 128:k * T + (tc + 1) * 128])
                    for n in range(2):
                        last = e.matmul(
                            out=p[pb + n][0:128, 0:512],
                            lhsT=xf[:, k * T + tc * 128:k * T + (tc + 1) * 128],
                            rhs=wpj[:, k * 1024 + n * 512:k * 1024 + (n + 1) * 512],
                            start=(k == 0), stop=(k == 7))
                return last
            bld.op('pe', pmm, reads=[*xf_rb, wpj.b], writes=[p[pb].b, p[pb + 1].b])
            for n in range(2):
                ebuf = (evbR if isbf else evR)(tc * 2 + n)
                if isbf:
                    bld.op('act', lambda e, n=n, pb=pb, ebuf=ebuf: e.activation(out=ebuf.ap,
                               in_=p[pb + n][0:128, 0:512], func=AF.Identity),
                           reads=[p[pb + n].b], writes=[ebuf.b])
                else:
                    bld.op('act', lambda e, n=n, pb=pb, ebuf=ebuf: e.copy(out=ebuf.ap,
                               in_=p[pb + n][0:128, 0:512]),
                           reads=[p[pb + n].b], writes=[ebuf.b])
                bld.op('actd', lambda e, rtok=rtok, n=n, dst=dst, ebuf=ebuf: e.dma_start(
                           out=dst[rtok, n * 512:(n + 1) * 512], in_=ebuf.ap),
                       reads=[ebuf.b], writes=[dst.b])
    bld.pop_scope()

    bld.push_scope()
    gb = SB(bld, 'c_gb', [128, 1024], F32)
    MZ = SB(bld, 'c_MZ', [64, H * 65], F32)
    MZb = SB(bld, 'c_MZb', [64, H * 65], BF16)
    qblkR = Rot(bld, 'c_qblk', [128, 256], F32, 2)
    kblkR = Rot(bld, 'c_kblk', [128, 256], F32, 2)
    v1R = Rot(bld, 'c_v1', [128, 4 * 65], BF16, 2)
    sqR = Rot(bld, 'c_sq', [128, 256], F32, 2)
    skR = Rot(bld, 'c_sk', [128, 256], F32, 2)
    tm1R = Rot(bld, 'c_tm1', [128, 256], F32, 2)
    qbbR = Rot(bld, 'c_qbb', [128, 256], BF16, 2)
    sqbR = Rot(bld, 'c_sqb', [128, 256], BF16, 2)
    kbbR = Rot(bld, 'c_kbb', [128, 256], BF16, 2)
    skbR = Rot(bld, 'c_skb', [128, 256], BF16, 2)
    fmbR = Rot(bld, 'c_fmb', [64, 1536], BF16, 2)
    sdgR = Rot(bld, 'c_sdg', [128, 512], F32, 2)
    mxR = Rot(bld, 'c_mx', [128, 8], F32, 2)
    m1R = Rot(bld, 'c_m1', [1, 8], F32, 2)
    negmR = Rot(bld, 'c_negm', [128, 1], F32, 2)
    ptallR = Rot(bld, 'c_pt', [128, 4 * 512], BF16, 2)
    rcR = Rot(bld, 'c_rc', [128, 8], F32, 2)
    amnR = Rot(bld, 'c_amn', [128, 256], F32, 2)
    adnR = Rot(bld, 'c_adn', [128, 256], F32, 2)
    attR = Rot(bld, 'c_att', [128, 256], F32, 2)
    bet1 = SB(bld, 'c_bet1', [1, 1024], F32)

    bld.op('sp', lambda e: e.dma_start(out=bet1.ap,
               in_=I['L1_attn_betas'].a.rearrange('o h q d -> o (h q d)')),
           reads=[I['L1_attn_betas'].b], writes=[bet1.b])
    bld.op('act', lambda e: e.activation(out=bet1.ap, in_=bet1.ap, func=AF.Sigmoid),
           reads=[bet1.b], writes=[bet1.b])
    for n in range(2):
        bld.op('pe', lambda e, n=n: e.matmul(out=p[5][0:128, 0:512],
                   lhsT=ones[0:1, 0:128], rhs=bet1[0:1, n * 512:(n + 1) * 512],
                   start=True, stop=True),
               reads=[ones.b, bet1.b], writes=[p[5].b])
        bld.op('dve', lambda e, n=n: e.tensor_copy(out=gb[:, n * 512:(n + 1) * 512],
                   in_=p[5][0:128, 0:512]),
               reads=[p[5].b], writes=[gb.b])
    bld.op('sp', lambda e: e.dma_start(
               out=MZ.t[:].rearrange('p (h c) -> p h c', c=65)[:, :, 0:64],
               in_=I['L1_attn_memory0'].a.rearrange('o h d v -> d (o h) v')),
           reads=[I['L1_attn_memory0'].b], writes=[MZ.b])
    z16 = SB(bld, 'c_z16', [16, 64], F32)
    bld.op('sp', lambda e: e.dma_start(out=z16.ap,
               in_=I['L1_attn_z0'].a.rearrange('o h q d -> (o h q) d')),
           reads=[I['L1_attn_z0'].b], writes=[z16.b])
    bld.op('pe', lambda e: e.transpose(out=p[5][0:64, 0:16], in_=z16.ap,
               identity=ident_f[0:16, 0:16]),
           reads=[z16.b, ident_f.b], writes=[p[5].b])
    bld.op('dve', lambda e: e.tensor_copy(
               out=MZ.t[:].rearrange('p (h c) -> p h c', c=65)[:, :, 64:65],
               in_=p[5][0:64, 0:16].rearrange('p (h o) -> p h o', o=1)),
           reads=[p[5].b], writes=[MZ.b])
    bld.op('pool', lambda e: e.tensor_copy(out=MZb.ap, in_=MZ.ap),
           reads=[MZ.b], writes=[MZb.b])
    for vv1 in v1R.sbs:
        for c in range(4):
            bld.op('pool', lambda e, c=c, vv1=vv1: e.memset(vv1[:, c * 65 + 64:c * 65 + 65], 1.0),
                   reads=[], writes=[vv1.b])

    qv = Dm['q'].a.rearrange('(b t) (a d) -> b (t a) d', t=32, d=64)
    kv = Dm['k'].a.rearrange('(b t) (a d) -> b (t a) d', t=32, d=64)
    vv = Dm['v'].a.rearrange('(b t) (a d) -> b (t a) d', t=32, d=64)
    av = Dm['att'].a.rearrange('(b t) (a d) -> b (t a) d', t=32, d=64)

    def c_body(s, h, qblk, kblk, v1, sq, sk, tm1, qbb, sqb, kbb, skb, fmb,
               sdg, mx, m1, negm, ptall, rc, amn, adn, att):
        if True:
            base = s * H + h
            hc = slice(h * 65, (h + 1) * 65)
            bld.op('sp', lambda e, base=base: e.dma_start(
                       out=qblk.t[:].rearrange('p (c d) -> p c d', d=64),
                       in_=qv[base].rearrange('(c p) d -> p c d', p=128)),
                   reads=[Dm['q'].b], writes=[qblk.b])
            bld.op('sp', lambda e, base=base: e.dma_start(
                       out=kblk.t[:].rearrange('p (c d) -> p c d', d=64),
                       in_=kv[base].rearrange('(c p) d -> p c d', p=128)),
                   reads=[Dm['k'].b], writes=[kblk.b])
            bld.op('actd', lambda e, base=base: e.dma_start(
                       out=v1.t[:].rearrange('p (c e) -> p c e', e=65)[:, :, 0:64],
                       in_=vv[base].rearrange('(c p) d -> p c d', p=128)),
                   reads=[Dm['v'].b], writes=[v1.b])
            for (blk, so, sb_, bb) in [(qblk, sq, sqb, qbb), (kblk, sk, skb, kbb)]:
                bld.op('dve', lambda e, blk=blk: e.tensor_scalar(
                           out=tm1.ap, in0=blk.ap, scalar1=0.0, scalar2=None,
                           op0=ALU.min),
                       reads=[blk.b], writes=[tm1.b])
                bld.op('act', lambda e, so=so: e.activation(out=so.ap, in_=tm1.ap,
                           func=AF.Exp),
                       reads=[tm1.b], writes=[so.b])
                bld.op('dve', lambda e, blk=blk, so=so: e.tensor_tensor(
                           out=so.ap, in0=so.ap, in1=blk.ap, op=ALU.add),
                       reads=[so.b, blk.b], writes=[so.b])
                bld.op('dve', lambda e, so=so: e.tensor_tensor(
                           out=so.ap, in0=so.ap, in1=tm1.ap, op=ALU.subtract),
                       reads=[so.b, tm1.b], writes=[so.b])
                bld.op('pool', lambda e, so=so, sb_=sb_: e.tensor_copy(out=sb_.ap,
                           in_=so.ap),
                       reads=[so.b], writes=[sb_.b])
                bld.op('pool', lambda e, blk=blk, bb=bb: e.tensor_copy(out=bb.ap,
                           in_=blk.ap),
                       reads=[blk.b], writes=[bb.b])
            def ctr(e):
                last = None
                for ci in range(4):
                    last = e.transpose(out=pbx[0:64, ci * 128:(ci + 1) * 128],
                                       in_=qbb[:, ci * 64:(ci + 1) * 64],
                                       identity=ident_b.ap)
                    last = e.transpose(out=pbx[0:64, 512 + ci * 128:512 + (ci + 1) * 128],
                                       in_=sqb[:, ci * 64:(ci + 1) * 64],
                                       identity=ident_b.ap)
                    last = e.transpose(out=pbx[0:64, 1024 + ci * 128:1024 + (ci + 1) * 128],
                                       in_=kbb[:, ci * 64:(ci + 1) * 64],
                                       identity=ident_b.ap)
                return last
            bld.op('pe', ctr, reads=[qbb.b, sqb.b, kbb.b, ident_b.b], writes=[pbx.b])
            bld.op('dve', lambda e: e.tensor_copy(out=fmb.ap, in_=pbx[0:64, 0:1536]),
                   reads=[pbx.b], writes=[fmb.b])
            for j in range(4):
                w = 512 - j * 128
                bld.op('pe', lambda e, j=j, w=w: (
                           e.ldweights(fmb[0:64, 1024 + j * 128:1024 + (j + 1) * 128]),
                           e.matmul(out=p[j][0:128, 0:w],
                                    lhsT=fmb[0:64, 1024 + j * 128:1024 + (j + 1) * 128],
                                    rhs=fmb[0:64, j * 128:512],
                                    start=True, stop=True))[-1],
                       reads=[fmb.b], writes=[p[j].b])
                bld.op('dve', lambda e, j=j: e.scalar_tensor_tensor(
                           out=sdg[:, j * 128:(j + 1) * 128], in0=p[j][0:128, 0:128],
                           scalar=0.125, in1=maskb.ap, op0=ALU.mult, op1=ALU.add),
                       reads=[p[j].b, maskb.b], writes=[sdg.b])
                bld.op('dve', lambda e, j=j: e.tensor_reduce(
                           out=mx[:, j:j + 1], in_=sdg[:, j * 128:(j + 1) * 128],
                           axis=AX.X, op=ALU.max),
                       reads=[sdg.b], writes=[mx.b])
                if j < 3:
                    bld.op('dve', lambda e, j=j, w=w: e.tensor_reduce(
                               out=mx[:, 4 + j:5 + j], in_=p[j][0:128, 128:w],
                               axis=AX.X, op=ALU.max),
                           reads=[p[j].b], writes=[mx.b])
            bld.op('dve', lambda e: e.tensor_reduce(out=mx[:, 7:8], in_=mx[:, 4:7],
                       axis=AX.X, op=ALU.max),
                   reads=[mx.b], writes=[mx.b])
            bld.op('dve', lambda e: e.tensor_scalar(out=mx[:, 7:8], in0=mx[:, 7:8],
                       scalar1=0.125, scalar2=None, op0=ALU.mult),
                   reads=[mx.b], writes=[mx.b])
            bld.op('dve', lambda e: e.tensor_reduce(out=mx[:, 3:4], in_=mx[:, 0:4],
                       axis=AX.X, op=ALU.max),
                   reads=[mx.b], writes=[mx.b])
            bld.op('dve', lambda e: e.tensor_tensor(out=mx[:, 3:4], in0=mx[:, 3:4],
                       in1=mx[:, 7:8], op=ALU.max),
                   reads=[mx.b], writes=[mx.b])
            part_reduce(bld, C, p[4], mx[:, 3:4], mx.b, m1, m1[0:1, 0:1], ALU.max)
            bld.op('dve', lambda e: e.tensor_scalar(out=m1[0:1, 1:2], in0=m1[0:1, 0:1],
                       scalar1=-1.0, scalar2=None, op0=ALU.mult),
                   reads=[m1.b], writes=[m1.b])
            bc_row(bld, C, p[4], (negm, negm.ap), m1[0:1, 1:2], 1, [m1.b])
            pt3 = ptall.t[:].rearrange('p (c l) -> p c l', l=512)
            for j in range(4):
                w = 512 - j * 128
                bld.op('act', lambda e, j=j: e.activation(
                           out=pt3[:, j, j * 128:(j + 1) * 128],
                           in_=sdg[:, j * 128:(j + 1) * 128], func=AF.Exp,
                           bias=negm.ap),
                       reads=[sdg.b, negm.b], writes=[ptall.b])
                if j < 3:
                    bld.op('act', lambda e, j=j, w=w: e.activation(
                               out=pt3[:, j, (j + 1) * 128:512],
                               in_=p[j][0:128, 128:w], func=AF.Exp, scale=0.125,
                               bias=negm.ap),
                           reads=[p[j].b, negm.b], writes=[ptall.b])
            def admm(e):
                last = None
                for i in range(4):
                    for j in range(i + 1):
                        e.ldweights(ptall[:, j * 512 + i * 128:j * 512 + (i + 1) * 128])
                        last = e.matmul(out=p[4][0:128, i * 65:(i + 1) * 65],
                                        lhsT=ptall[:, j * 512 + i * 128:j * 512 + (i + 1) * 128],
                                        rhs=v1[:, j * 65:(j + 1) * 65],
                                        start=(j == 0), stop=(j == i))
                return last
            bld.op('pe', admm, reads=[ptall.b, v1.b], writes=[p[4].b])
            def ammm(e, hc=hc):
                last = None
                for i in range(4):
                    e.ldweights(fmb[0:64, 512 + i * 128:512 + (i + 1) * 128])
                    last = e.matmul(out=p[5][0:128, i * 65:(i + 1) * 65],
                                    lhsT=fmb[0:64, 512 + i * 128:512 + (i + 1) * 128],
                                    rhs=MZb[:, hc], start=True, stop=True)
                return last
            bld.op('pe', ammm, reads=[fmb.b, MZb.b], writes=[p[5].b])
            p4v = p[4].t[:, 0:260].rearrange('p (c e) -> p c e', e=65)
            p5v = p[5].t[:, 0:260].rearrange('p (c e) -> p c e', e=65)
            bld.op('dve', lambda e, p5v=p5v: e.reciprocal(out=rc[:, 0:4],
                       in_=p5v[:, 0:4, 64]),
                   reads=[p[5].b], writes=[rc.b])
            bld.op('dve', lambda e, p4v=p4v: e.reciprocal(out=rc[:, 4:8],
                       in_=p4v[:, 0:4, 64]),
                   reads=[p[4].b], writes=[rc.b])
            bld.op('dve', lambda e, p5v=p5v: e.tensor_tensor(
                       out=amn.t[:].rearrange('p (c d) -> p c d', d=64),
                       in0=p5v[:, 0:4, 0:64],
                       in1=rc[:, 0:4].rearrange('p (c o) -> p c o', o=1)
                           .broadcast_to([128, 4, 64]),
                       op=ALU.mult),
                   reads=[p[5].b, rc.b], writes=[amn.b])
            bld.op('dve', lambda e, p4v=p4v: e.tensor_tensor(
                       out=adn.t[:].rearrange('p (c d) -> p c d', d=64),
                       in0=p4v[:, 0:4, 0:64],
                       in1=rc[:, 4:8].rearrange('p (c o) -> p c o', o=1)
                           .broadcast_to([128, 4, 64]),
                       op=ALU.mult),
                   reads=[p[4].b, rc.b], writes=[adn.b])
            bld.op('dve', lambda e: e.tensor_tensor(out=amn.ap, in0=amn.ap, in1=adn.ap,
                       op=ALU.subtract),
                   reads=[amn.b, adn.b], writes=[amn.b])
            bld.op('pool', lambda e, h=h: e.tensor_tensor(
                       out=amn.t[:].rearrange('p (c d) -> p c d', d=64),
                       in0=amn.t[:].rearrange('p (c d) -> p c d', d=64),
                       in1=gb[:, h * 64:(h + 1) * 64]
                           .rearrange('p (o d) -> p o d', o=1)
                           .broadcast_to([128, 4, 64]),
                       op=ALU.mult),
                   reads=[amn.b, gb.b], writes=[amn.b])
            bld.op('dve', lambda e: e.tensor_tensor(out=att.ap, in0=adn.ap, in1=amn.ap,
                       op=ALU.add),
                   reads=[adn.b, amn.b], writes=[att.b])
            bld.op('poold', lambda e, base=base: e.dma_start(
                       out=av[base].rearrange('(c p) d -> p c d', p=128),
                       in_=att.t[:].rearrange('p (c d) -> p c d', d=64)),
                   reads=[att.b], writes=[Dm['att'].b])
            def mzmm(e):
                last = None
                for ci in range(4):
                    e.ldweights(skb[:, ci * 64:(ci + 1) * 64])
                    last = e.matmul(out=p[5][0:64, 0:65],
                                    lhsT=skb[:, ci * 64:(ci + 1) * 64],
                                    rhs=v1[:, ci * 65:(ci + 1) * 65],
                                    start=(ci == 0), stop=(ci == 3))
                return last
            bld.op('pe', mzmm, reads=[skb.b, v1.b], writes=[p[5].b])
            bld.op('dve', lambda e, hc=hc: e.tensor_tensor(out=MZ[:, hc],
                       in0=MZ[:, hc], in1=p[5][0:64, 0:65], op=ALU.add),
                   reads=[MZ.b, p[5].b], writes=[MZ.b])
            bld.op('pool', lambda e, hc=hc: e.tensor_copy(out=MZb[:, hc], in_=MZ[:, hc]),
                   reads=[MZ.b], writes=[MZb.b])

    for s in range(NSEG):
        for h in range(H):
            it = s * H + h
            c_body(s, h, qblkR(it), kblkR(it), v1R(it), sqR(it), skR(it),
                   tm1R(it), qbbR(it), sqbR(it), kbbR(it), skbR(it), fmbR(it),
                   sdgR(it), mxR(it), m1R(it), negmR(it), ptallR(it), rcR(it),
                   amnR(it), adnR(it), attR(it))
    bld.pop_scope()

    bld.push_scope()
    af = SB(bld, 'c_af', [128, 8 * T], BF16)
    wo3 = SB(bld, 'c_wo3', [128, 8 * 1024], BF16)
    af_rb = [Buf(f'caf{i}') for i in range(4)]
    striped_load(bld, af_rb,
                 [(lambda e, j=j: e.dma_start(
                       out=af[:, j * T:(j + 1) * T],
                       in_=Dm['att'][:, j * 128:(j + 1) * 128], transpose=True))
                  for j in range(8)],
                 reads=[Dm['att'].b])
    woa = I['L1_attn_Wo'].a.rearrange('(k p) m -> p k m', p=128)
    for k in range(8):
        bld.op('poold', lambda e, k=k: e.dma_start(
                   out=wo3[:, k * 1024:(k + 1) * 1024], in_=woa[:, k, :]),
               reads=[I['L1_attn_Wo'].b], writes=[wo3.b])
    xresR = Rot(bld, 'c_xr2', [128, 1024], F32, 2)
    xnewR = Rot(bld, 'c_xn2', [128, 1024], F32, 2)
    xoutR = Rot(bld, 'c_xo2', [128, 1024], F32, 2)
    statR = Rot(bld, 'c_ls2', [128, 8], F32, 2)
    for tc in range(32):
        rtok = slice(tc * 128, (tc + 1) * 128)
        pb = 0 if tc % 2 == 0 else 2
        xres2, xnew2, xout2, stat2 = xresR(tc), xnewR(tc), xoutR(tc), statR(tc)
        def omm2(e, tc=tc, pb=pb):
            last = None
            for k in range(8):
                e.ldweights(af[:, k * T + tc * 128:k * T + (tc + 1) * 128])
                for n in range(2):
                    last = e.matmul(out=p[pb + n][0:128, 0:512],
                                    lhsT=af[:, k * T + tc * 128:k * T + (tc + 1) * 128],
                                    rhs=wo3[:, k * 1024 + n * 512:k * 1024 + (n + 1) * 512],
                                    start=(k == 0), stop=(k == 7))
            return last
        bld.op('pe', omm2, reads=[*af_rb, wo3.b], writes=[p[pb].b, p[pb + 1].b])
        bld.op('sp', lambda e, rtok=rtok, xres2=xres2: e.dma_start(out=xres2.ap, in_=cur[rtok, :]),
               reads=[cur.b], writes=[xres2.b])
        for n in range(2):
            bld.op('dve', lambda e, n=n, pb=pb, xnew2=xnew2, xres2=xres2: e.scalar_tensor_tensor(
                       out=xnew2[:, n * 512:(n + 1) * 512], in0=p[pb + n][0:128, 0:512],
                       scalar=1.0, in1=xres2[:, n * 512:(n + 1) * 512],
                       op0=ALU.mult, op1=ALU.add),
                   reads=[p[pb + n].b, xres2.b], writes=[xnew2.b])
        emit_ln(bld, C, xnew2, xout2, stat2, st1, 1024, p[5])
        bld.op('actd', lambda e, rtok=rtok, xout2=xout2: e.dma_start(out=out_dram[rtok, :], in_=xout2.ap),
               reads=[xout2.b], writes=[out_dram.b])
        bld.op('poold', lambda e, rtok=rtok, xout2=xout2: e.dma_start(out=Dm['xbf'][rtok, :], in_=xout2.ap),
               reads=[xout2.b], writes=[Dm['xbf'].b])
    bld.pop_scope()
    bld.pop_scope()


def cc_allreduce_max(bld, C, Dm, vals_sb, vals_buf, k, out_sb):
    bld.op('actd', lambda e: e.dma_start(out=Dm['ccin'][0:1, 0:k], in_=vals_sb),
           reads=[vals_buf], writes=[Dm['ccin'].b])
    bld.op('cc', lambda e: e.collective_compute(
               'AllReduce', ALU.max, replica_groups=[list(range(NCORES))],
               ins=[Dm['ccin'][0:1, 0:k]], outs=[Dm['ccout'][0:1, 0:k]]),
           reads=[Dm['ccin'].b], writes=[Dm['ccout'].b])
    bld.op('sp', lambda e: e.dma_start(out=out_sb.ap[0:1, 0:k], in_=Dm['ccout'][0:1, 0:k]),
           reads=[Dm['ccout'].b], writes=[out_sb.b])


def gmax_out(bld, vals_ap, vals_buf, k, out_dr):
    bld.op('actd', lambda e: e.dma_start(out=out_dr[0:1, 0:k], in_=vals_ap),
           reads=[vals_buf], writes=[out_dr.b])


def gmax_in(bld, C, in_dr, k, out_sb, dst=0):
    tmp = SB(bld, 'gmx', [k, 8], F32)
    bld.op('sp', lambda e: e.dma_start(out=tmp.ap, in_=in_dr[:, :]),
           reads=[in_dr.b], writes=[tmp.b])
    col = SB(bld, 'gmc', [k, 1], F32)
    bld.op('dve', lambda e: e.tensor_reduce(out=col.ap, in_=tmp.ap, axis=AX.X,
               op=ALU.max),
           reads=[tmp.b], writes=[col.b])
    p = C['p'][5]
    ident = C['ident_f32']
    bld.op('pe', lambda e: e.transpose(out=p[0:1, 0:k], in_=col.ap,
               identity=ident[0:k, 0:k]),
           reads=[col.b, ident.b], writes=[p.b])
    bld.op('dve', lambda e: e.tensor_copy(out=out_sb.ap[0:1, dst:dst + k],
               in_=p[0:1, 0:k]),
           reads=[p.b], writes=[out_sb.b])


def emit_d_wq(bld, C, I, Dm, mats):
    bld.push_scope()
    wld = SB(bld, 'd_wld', [128, 3072], F32)
    bmax = SB(bld, 'd_bmax', [128, 96], F32)
    binv = SB(bld, 'd_binv', [128, 96], F32)
    wt = SB(bld, 'd_wt', [128, 3072], F32)
    wqi2 = SB(bld, 'd_wqi2', [128, 3072], I32)
    wb_ = SB(bld, 'd_wb', [128, 3072], BF16)
    for (W, dq, nr, ncol) in mats:
        nb = ncol // 32
        for r in range(nr):
            w3 = wld.t[:, 0:ncol].rearrange('p (b q) -> p b q', q=32)
            bld.op('sp', lambda e, W=W, r=r, ncol=ncol: e.dma_start(
                       out=wld[:, 0:ncol], in_=W[r * 128:(r + 1) * 128, :]),
                   reads=[W.b], writes=[wld.b])
            bld.op('dve', lambda e, w3=w3, nb=nb: e.tensor_reduce(
                       out=bmax[:, 0:nb], in_=w3, axis=AX.X, op=ALU.max,
                       apply_absolute_value=True),
                   reads=[wld.b], writes=[bmax.b])
            bld.op('dve', lambda e, nb=nb: e.tensor_scalar(
                       out=bmax[:, 0:nb], in0=bmax[:, 0:nb], scalar1=1.0 / 7.0,
                       scalar2=1e-12, op0=ALU.mult, op1=ALU.max),
                   reads=[bmax.b], writes=[bmax.b])
            bld.op('dve', lambda e, nb=nb: e.reciprocal(out=binv[:, 0:nb],
                       in_=bmax[:, 0:nb]),
                   reads=[bmax.b], writes=[binv.b])
            bld.op('dve', lambda e, w3=w3, nb=nb, ncol=ncol: e.tensor_tensor(
                       out=wt.t[:, 0:ncol].rearrange('p (b q) -> p b q', q=32),
                       in0=w3,
                       in1=binv[:, 0:nb].rearrange('p (b o) -> p b o', o=1)
                           .broadcast_to([128, nb, 32]),
                       op=ALU.mult),
                   reads=[wld.b, binv.b], writes=[wt.b])
            bld.op('pool', lambda e, ncol=ncol: e.tensor_copy(
                       out=wqi2[:, 0:ncol], in_=wt[:, 0:ncol]),
                   reads=[wt.b], writes=[wqi2.b])
            bld.op('dve', lambda e, ncol=ncol: e.tensor_scalar(
                       out=wt[:, 0:ncol], in0=wqi2[:, 0:ncol], scalar1=-8.0,
                       scalar2=7.0, op0=ALU.max, op1=ALU.min),
                   reads=[wqi2.b], writes=[wt.b])
            bld.op('dve', lambda e, ncol=ncol, nb=nb: e.tensor_tensor(
                       out=wb_.t[:, 0:ncol].rearrange('p (b q) -> p b q', q=32),
                       in0=wt.t[:, 0:ncol].rearrange('p (b q) -> p b q', q=32),
                       in1=bmax[:, 0:nb].rearrange('p (b o) -> p b o', o=1)
                           .broadcast_to([128, nb, 32]),
                       op=ALU.mult),
                   reads=[wt.b, bmax.b], writes=[wb_.b])
            bld.op('actd', lambda e, dq=dq, r=r, ncol=ncol: e.dma_start(
                       out=dq[r * 128:(r + 1) * 128, :], in_=wb_[:, 0:ncol]),
                   reads=[wb_.b], writes=[dq.b])
    bld.pop_scope()


def emit_d_scales(bld, C, p5, gsc, src, dst_inv, dst_sc, bcs):
    t0, t1 = 5, 6
    bld.op('dve', lambda e: e.tensor_scalar(out=gsc[0:1, t0:t0 + 1],
               in0=gsc[0:1, src:src + 1], scalar1=1.0 / 32767.0, scalar2=1e-12,
               op0=ALU.mult, op1=ALU.max),
           reads=[gsc.b], writes=[gsc.b])
    bld.op('dve', lambda e: e.reciprocal(out=gsc[0:1, t1:t1 + 1],
               in_=gsc[0:1, t0:t0 + 1]),
           reads=[gsc.b], writes=[gsc.b])
    bc_row(bld, C, p5, (bcs, bcs[:, dst_inv:dst_inv + 1]), gsc[0:1, t1:t1 + 1], 1, [gsc.b])
    bc_row(bld, C, p5, (bcs, bcs[:, dst_sc:dst_sc + 1]), gsc[0:1, t0:t0 + 1], 1, [gsc.b])


def emit_d_maxx(bld, C, D_, cur, st1, stat):
    bld.push_scope()
    xt = SB(bld, 'dm_xt', [128, 1024], F32)
    rmax = SB(bld, 'dm_rmax', [128, 32], F32)
    for tc in range(32):
        rtok = slice(tc * 128, (tc + 1) * 128)
        bld.op('sp', lambda e, rtok=rtok: e.dma_start(out=xt.ap, in_=cur[rtok, :]),
               reads=[cur.b], writes=[xt.b])
        bld.op('dve', lambda e, tc=tc: e.tensor_reduce(out=rmax[:, tc:tc + 1],
                   in_=xt.ap, axis=AX.X, op=ALU.max, apply_absolute_value=True),
               reads=[xt.b], writes=[rmax.b])
    bld.op('dve', lambda e: e.tensor_reduce(out=stat[:, 0:1], in_=rmax.ap,
               axis=AX.X, op=ALU.max),
           reads=[rmax.b], writes=[stat.b])
    part_reduce(bld, C, C['p'][5], stat[:, 0:1], stat.b, st1, st1[0:1, 0:1], ALU.max)
    bld.pop_scope()


def emit_d_xq(bld, C, Dm, cur, bcs):
    bld.push_scope()
    xt = Rot(bld, 'd_xt', [128, 1024], F32, 3)
    qt = Rot(bld, 'd_qt', [128, 1024], F32, 2)
    qi2 = Rot(bld, 'd_qi2', [128, 1024], I32, 2)
    qb2 = Rot(bld, 'd_qb2', [128, 1024], BF16, 2)
    for tc in range(32):
        rtok = slice(tc * 128, (tc + 1) * 128)
        a, b_, c_, d_ = xt(tc), qt(tc), qi2(tc), qb2(tc)
        bld.op('sp', lambda e, rtok=rtok, a=a: e.dma_start(out=a.ap, in_=cur[rtok, :]),
               reads=[cur.b], writes=[a.b])
        bld.op('act', lambda e, a=a, b_=b_: e.activation(out=b_.ap, in_=a.ap,
                   func=AF.Identity, scale=bcs[:, 0:1]),
               reads=[a.b, bcs.b], writes=[b_.b])
        bld.op('pool', lambda e, b_=b_, c_=c_: e.tensor_copy(out=c_.ap, in_=b_.ap),
               reads=[b_.b], writes=[c_.b])
        bld.op('dve', lambda e, c_=c_, d_=d_: e.tensor_copy(out=d_.ap, in_=c_.ap),
               reads=[c_.b], writes=[d_.b])
        bld.op('poold', lambda e, rtok=rtok, d_=d_: e.dma_start(out=Dm['xq'][rtok, :],
                   in_=d_.ap),
               reads=[d_.b], writes=[Dm['xq'].b])
    bld.pop_scope()


def emit_d_upgate(bld, C, Dm, bcs, st1, stat, d_up, d_gt):
    O = 3072
    p = C['p']
    bld.push_scope()
    xqf = SB(bld, 'd_xqf', [128, 8 * T], BF16)
    wstr = SB(bld, 'd_wstr', [128, 8 * O], BF16)
    ubR = Rot(bld, 'd_ub', [128, 512], F32, 3)
    um6 = SB(bld, 'd_um6', [128, 8], F32)
    mxc = SB(bld, 'd_mxc', [128, 64], F32)
    xqf_rb = [Buf(f'dxqf{i}') for i in range(4)]
    wstr_rb = [Buf(f'dwstr{i}') for i in range(4)]
    striped_load(bld, xqf_rb,
                 [(lambda e, j=j: e.dma_start(
                       out=xqf[:, j * T:(j + 1) * T],
                       in_=Dm['xq'][:, j * 128:(j + 1) * 128], transpose=True))
                  for j in range(8)],
                 reads=[Dm['xq'].b])
    for (mi, dqw, ddst) in [(0, Dm['wqu2'], d_up), (1, Dm['wqg2'], d_gt)]:
        striped_load(bld, wstr_rb,
                     [(lambda e, j=j, dqw=dqw: e.dma_start(
                           out=wstr[:, j * O:(j + 1) * O],
                           in_=dqw[:, j * 128:(j + 1) * 128], transpose=True))
                      for j in range(8)],
                     reads=[dqw.b])
        for tc in range(32):
            rtok = slice(tc * 128, (tc + 1) * 128)
            for og in range(2):
                pb = 0 if og == 0 else 3
                def umm(e, tc=tc, og=og, pb=pb):
                    last = None
                    for i in range(8):
                        e.ldweights(xqf[:, i * T + tc * 128:i * T + (tc + 1) * 128])
                        for op_ in range(3):
                            last = e.matmul(
                                out=p[pb + op_][0:128, 0:512],
                                lhsT=xqf[:, i * T + tc * 128:i * T + (tc + 1) * 128],
                                rhs=wstr[:, i * O + og * 1536 + op_ * 512:
                                         i * O + og * 1536 + (op_ + 1) * 512],
                                start=(i == 0), stop=(i == 7))
                    return last
                bld.op('pe', umm, reads=[*xqf_rb, *wstr_rb],
                       writes=[p[pb].b, p[pb + 1].b, p[pb + 2].b])
                for op_ in range(3):
                    ocol = og * 1536 + op_ * 512
                    bi = og * 3 + op_
                    ub = ubR(bi)
                    bld.op('act', lambda e, op_=op_, pb=pb, ub=ub: e.activation(out=ub.ap,
                               in_=p[pb + op_][0:128, 0:512], func=AF.Identity,
                               scale=bcs[:, 1:2]),
                           reads=[p[pb + op_].b, bcs.b], writes=[ub.b])
                    bld.op('dve', lambda e, bi=bi, ub=ub: e.tensor_reduce(
                               out=um6[:, bi:bi + 1], in_=ub.ap, axis=AX.X,
                               op=ALU.max, apply_absolute_value=True),
                           reads=[ub.b], writes=[um6.b])
                    bld.op('actd', lambda e, rtok=rtok, ocol=ocol, ddst=ddst, ub=ub: e.dma_start(
                               out=ddst[rtok, ocol:ocol + 512], in_=ub.ap),
                           reads=[ub.b], writes=[ddst.b])
            bld.op('dve', lambda e, mi=mi, tc=tc: e.tensor_reduce(
                       out=mxc[:, mi * 32 + tc:mi * 32 + tc + 1], in_=um6[:, 0:6],
                       axis=AX.X, op=ALU.max),
                   reads=[um6.b], writes=[mxc.b])
    for mi in range(2):
        bld.op('dve', lambda e, mi=mi: e.tensor_reduce(out=stat[:, 0:1],
                   in_=mxc[:, mi * 32:(mi + 1) * 32], axis=AX.X, op=ALU.max),
               reads=[mxc.b], writes=[stat.b])
        part_reduce(bld, C, C['p'][5], stat[:, 0:1], stat.b, st1,
                    st1[0:1, mi:mi + 1], ALU.max)
    bld.pop_scope()


def emit_d_g2raw(bld, C, bcs, st1, stat, d_gt_in, d_g2_out):
    bld.push_scope()
    gt_R = Rot(bld, 'd_gt2', [128, 3072], F32, 2)
    ti2R = Rot(bld, 'd_ti2', [128, 3072], I32, 2)
    g1R = Rot(bld, 'd_g1', [128, 3072], F32, 2)
    sg2R = Rot(bld, 'd_sg2', [128, 3072], F32, 2)
    g2m = SB(bld, 'd_g2m', [128, 32], F32)
    for tc in range(32):
        rtok = slice(tc * 128, (tc + 1) * 128)
        gt_, ti2, g1, sg2 = gt_R(tc), ti2R(tc), g1R(tc), sg2R(tc)
        bld.op('sp', lambda e, rtok=rtok, gt_=gt_: e.dma_start(out=gt_.ap, in_=d_gt_in[rtok, :]),
               reads=[d_gt_in.b], writes=[gt_.b])
        bld.op('act', lambda e, gt_=gt_, g1=g1: e.activation(out=g1.ap, in_=gt_.ap, func=AF.Identity,
                   scale=bcs[:, 4:5]),
               reads=[gt_.b, bcs.b], writes=[g1.b])
        bld.op('pool', lambda e, g1=g1, ti2=ti2: e.tensor_copy(out=ti2.ap, in_=g1.ap),
               reads=[g1.b], writes=[ti2.b])
        bld.op('dve', lambda e, ti2=ti2, g1=g1: e.tensor_scalar(out=g1.ap, in0=ti2.ap,
                   scalar1=bcs[:, 5:6], scalar2=None, op0=ALU.mult),
               reads=[ti2.b, bcs.b], writes=[g1.b])
        bld.op('act', lambda e, g1=g1, sg2=sg2: e.activation(out=sg2.ap, in_=g1.ap, func=AF.Sigmoid),
               reads=[g1.b], writes=[sg2.b])
        bld.op('act', lambda e, sg2=sg2: e.activation(out=sg2.ap, in_=sg2.ap, func=AF.Identity,
                   scale=65536.0),
               reads=[sg2.b], writes=[sg2.b])
        bld.op('pool', lambda e, sg2=sg2, ti2=ti2: e.tensor_copy(out=ti2.ap, in_=sg2.ap),
               reads=[sg2.b], writes=[ti2.b])
        bld.op('dve', lambda e, ti2=ti2, sg2=sg2: e.tensor_scalar(out=sg2.ap, in0=ti2.ap, scalar1=65535.0,
                   scalar2=1.0 / 65536.0, op0=ALU.min, op1=ALU.mult),
               reads=[ti2.b], writes=[sg2.b])
        bld.op('dve', lambda e, g1=g1, sg2=sg2: e.tensor_tensor(out=g1.ap, in0=g1.ap, in1=sg2.ap,
                   op=ALU.mult),
               reads=[g1.b, sg2.b], writes=[g1.b])
        bld.op('dve', lambda e, tc=tc, g1=g1: e.tensor_reduce(out=g2m[:, tc:tc + 1], in_=g1.ap,
                   axis=AX.X, op=ALU.max, apply_absolute_value=True),
               reads=[g1.b], writes=[g2m.b])
        bld.op('actd', lambda e, rtok=rtok, g1=g1: e.dma_start(out=d_g2_out[rtok, :], in_=g1.ap),
               reads=[g1.b], writes=[d_g2_out.b])
    bld.op('dve', lambda e: e.tensor_reduce(out=stat[:, 0:1], in_=g2m.ap, axis=AX.X,
               op=ALU.max),
           reads=[g2m.b], writes=[stat.b])
    part_reduce(bld, C, C['p'][5], stat[:, 0:1], stat.b, st1, st1[0:1, 0:1], ALU.max)
    bld.pop_scope()


def emit_d_oraw(bld, C, bcs, st1, stat, d_g2_in, d_up_in, d_or_out):
    bld.push_scope()
    gt_R = Rot(bld, 'd_gt2', [128, 3072], F32, 2)
    ti2R = Rot(bld, 'd_ti2', [128, 3072], I32, 2)
    tj2R = Rot(bld, 'd_tj2', [128, 3072], I32, 2)
    g1R = Rot(bld, 'd_g1', [128, 3072], F32, 2)
    sg2R = Rot(bld, 'd_sg2', [128, 3072], F32, 2)
    up_R = Rot(bld, 'd_up2', [128, 3072], F32, 2)
    om = SB(bld, 'd_om', [128, 32], F32)
    for tc in range(32):
        rtok = slice(tc * 128, (tc + 1) * 128)
        gt_, ti2, g1, sg2, up_, tj2 = (gt_R(tc), ti2R(tc), g1R(tc), sg2R(tc),
                                       up_R(tc), tj2R(tc))
        bld.op('sp', lambda e, rtok=rtok, gt_=gt_: e.dma_start(out=gt_.ap, in_=d_g2_in[rtok, :]),
               reads=[d_g2_in.b], writes=[gt_.b])
        bld.op('actd', lambda e, rtok=rtok, up_=up_: e.dma_start(out=up_.ap, in_=d_up_in[rtok, :]),
               reads=[d_up_in.b], writes=[up_.b])
        bld.op('act', lambda e, gt_=gt_, g1=g1: e.activation(out=g1.ap, in_=gt_.ap, func=AF.Identity,
                   scale=bcs[:, 6:7]),
               reads=[gt_.b, bcs.b], writes=[g1.b])
        bld.op('pool', lambda e, g1=g1, ti2=ti2: e.tensor_copy(out=ti2.ap, in_=g1.ap),
               reads=[g1.b], writes=[ti2.b])
        bld.op('dve', lambda e, ti2=ti2, g1=g1: e.tensor_scalar(out=g1.ap, in0=ti2.ap,
                   scalar1=bcs[:, 7:8], scalar2=None, op0=ALU.mult),
               reads=[ti2.b, bcs.b], writes=[g1.b])
        bld.op('act', lambda e, up_=up_, sg2=sg2: e.activation(out=sg2.ap, in_=up_.ap, func=AF.Identity,
                   scale=bcs[:, 2:3]),
               reads=[up_.b, bcs.b], writes=[sg2.b])
        bld.op('pool', lambda e, sg2=sg2, tj2=tj2: e.tensor_copy(out=tj2.ap, in_=sg2.ap),
               reads=[sg2.b], writes=[tj2.b])
        bld.op('dve', lambda e, tj2=tj2, sg2=sg2: e.tensor_scalar(out=sg2.ap, in0=tj2.ap,
                   scalar1=bcs[:, 3:4], scalar2=None, op0=ALU.mult),
               reads=[tj2.b, bcs.b], writes=[sg2.b])
        bld.op('dve', lambda e, g1=g1, sg2=sg2: e.tensor_tensor(out=g1.ap, in0=g1.ap, in1=sg2.ap,
                   op=ALU.mult),
               reads=[g1.b, sg2.b], writes=[g1.b])
        bld.op('dve', lambda e, tc=tc, g1=g1: e.tensor_reduce(out=om[:, tc:tc + 1], in_=g1.ap,
                   axis=AX.X, op=ALU.max, apply_absolute_value=True),
               reads=[g1.b], writes=[om.b])
        bld.op('actd', lambda e, rtok=rtok, g1=g1: e.dma_start(out=d_or_out[rtok, :], in_=g1.ap),
               reads=[g1.b], writes=[d_or_out.b])
    bld.op('dve', lambda e: e.tensor_reduce(out=stat[:, 0:1], in_=om.ap, axis=AX.X,
               op=ALU.max),
           reads=[om.b], writes=[stat.b])
    part_reduce(bld, C, C['p'][5], stat[:, 0:1], stat.b, st1, st1[0:1, 0:1], ALU.max)
    bld.pop_scope()


def emit_d_down(bld, C, Dm, bcs, stat, st1, cur, d_or_in, out_dram):
    p = C['p']
    bld.push_scope()
    up_R = Rot(bld, 'd_up2', [128, 3072], F32, 2)
    ti2R = Rot(bld, 'd_ti2', [128, 3072], I32, 2)
    g1R = Rot(bld, 'd_g1', [128, 3072], F32, 2)
    ob2R = Rot(bld, 'd_ob2', [128, 3072], BF16, 2)
    for tc in range(32):
        rtok = slice(tc * 128, (tc + 1) * 128)
        up_, ti2, g1, ob2 = up_R(tc), ti2R(tc), g1R(tc), ob2R(tc)
        bld.op('sp', lambda e, rtok=rtok, up_=up_: e.dma_start(out=up_.ap, in_=d_or_in[rtok, :]),
               reads=[d_or_in.b], writes=[up_.b])
        bld.op('act', lambda e, up_=up_, g1=g1: e.activation(out=g1.ap, in_=up_.ap, func=AF.Identity,
                   scale=bcs[:, 8:9]),
               reads=[up_.b, bcs.b], writes=[g1.b])
        bld.op('pool', lambda e, g1=g1, ti2=ti2: e.tensor_copy(out=ti2.ap, in_=g1.ap),
               reads=[g1.b], writes=[ti2.b])
        bld.op('dve', lambda e, ti2=ti2, ob2=ob2: e.tensor_copy(out=ob2.ap, in_=ti2.ap),
               reads=[ti2.b], writes=[ob2.b])
        bld.op('poold', lambda e, rtok=rtok, ob2=ob2: e.dma_start(out=Dm['oD'][rtok, :], in_=ob2.ap),
               reads=[ob2.b], writes=[Dm['oD'].b])
    bld.pop_scope()
    bld.push_scope()
    xresR = Rot(bld, 'd_xr', [128, 1024], F32, 2)
    xnewR = Rot(bld, 'd_xn', [128, 1024], F32, 2)
    xoutR = Rot(bld, 'd_xo', [128, 1024], F32, 2)
    statR = Rot(bld, 'd_lst', [128, 8], F32, 2)
    ostrR = Rot(bld, 'do_str', [128, 24 * 512], BF16, 2)
    wdt = SB(bld, 'do_wdt', [128, 24 * 1024], BF16)
    wdt_rb = [Buf(f'dwdt{g}') for g in range(6)]
    ostr_rb = [[Buf(f'dostr{par}_{g}') for g in range(6)] for par in range(2)]
    striped_load(bld, wdt_rb,
                 [(lambda e, k=k: e.dma_start(
                       out=wdt[:, k * 1024:(k + 1) * 1024],
                       in_=Dm['wqd2'][:, k * 128:(k + 1) * 128], transpose=True))
                  for k in range(24)],
                 reads=[Dm['wqd2'].b])
    for w in range(8):
        ostr = ostrR(w)
        striped_load(bld, ostr_rb[w % 2],
                     [(lambda e, w=w, k=k, ostr=ostr: e.dma_start(
                           out=ostr[:, k * 512:(k + 1) * 512],
                           in_=Dm['oD'][w * 512:(w + 1) * 512,
                                        k * 128:(k + 1) * 128],
                           transpose=True))
                      for k in range(24)],
                     reads=[Dm['oD'].b])
        for tci in range(4):
            tc = w * 4 + tci
            rtok = slice(tc * 128, (tc + 1) * 128)
            xres, xnew, xout, stat_ = xresR(tc), xnewR(tc), xoutR(tc), statR(tc)
            pp = (p[0], p[1]) if tc % 2 == 0 else (p[2], p[3])
            def dmm2(e, tci=tci, ostr=ostr, pp=pp):
                last = None
                for k in range(24):
                    e.ldweights(ostr[:, k * 512 + tci * 128:k * 512 + (tci + 1) * 128])
                    for np_ in range(2):
                        last = e.matmul(
                            out=pp[np_][0:128, 0:512],
                            lhsT=ostr[:, k * 512 + tci * 128:k * 512 + (tci + 1) * 128],
                            rhs=wdt[:, k * 1024 + np_ * 512:k * 1024 + (np_ + 1) * 512],
                            start=(k == 0), stop=(k == 23))
                return last
            bld.op('pe', dmm2, reads=[*ostr_rb[w % 2], *wdt_rb], writes=[pp[0].b, pp[1].b])
            bld.op('sp', lambda e, rtok=rtok, xres=xres: e.dma_start(out=xres.ap, in_=cur[rtok, :]),
                   reads=[cur.b], writes=[xres.b])
            for np_ in range(2):
                bld.op('dve', lambda e, np_=np_, xnew=xnew, xres=xres, pp=pp: e.scalar_tensor_tensor(
                           out=xnew[:, np_ * 512:(np_ + 1) * 512],
                           in0=pp[np_][0:128, 0:512], scalar=bcs[:, 9:10],
                           in1=xres[:, np_ * 512:(np_ + 1) * 512],
                           op0=ALU.mult, op1=ALU.add),
                       reads=[pp[np_].b, bcs.b, xres.b], writes=[xnew.b])
            emit_ln(bld, C, xnew, xout, stat_, st1, 1024, p[4])
            bld.op('actd', lambda e, rtok=rtok, xout=xout: e.dma_start(out=out_dram[rtok, :],
                       in_=xout.ap),
                   reads=[xout.b], writes=[out_dram.b])
    bld.pop_scope()


def emit_stage_d(bld, C, I, Dm, cur, out_dram):
    stat = SB(bld, 'd_stat', [128, 8], F32)
    st1 = SB(bld, 'd_st1', [1, 8], F32)
    gsc = SB(bld, 'd_gsc', [1, 8], F32)
    bcs = SB(bld, 'd_bcs', [128, 16], F32)
    p5 = C['p'][5]
    emit_d_wq(bld, C, I, Dm, [(I['L1_ffn_w_up'], Dm['wqu2'], 24, 1024),
                              (I['L1_ffn_w_gate'], Dm['wqg2'], 24, 1024),
                              (I['L1_ffn_w_down'], Dm['wqd2'], 8, 3072)])
    emit_d_maxx(bld, C, Dm, cur, st1, stat)
    cc_allreduce_max(bld, C, Dm, st1[0:1, 0:1], st1.b, 1, gsc)
    emit_d_scales(bld, C, p5, gsc, 0, 0, 1, bcs)
    emit_d_xq(bld, C, Dm, cur, bcs)
    emit_d_upgate(bld, C, Dm, bcs, st1, stat, Dm['up'], Dm['gt'])
    cc_allreduce_max(bld, C, Dm, st1[0:1, 0:2], st1.b, 2, gsc)
    emit_d_scales(bld, C, p5, gsc, 0, 2, 3, bcs)
    emit_d_scales(bld, C, p5, gsc, 1, 4, 5, bcs)
    emit_d_g2raw(bld, C, bcs, st1, stat, Dm['gt'], Dm['gt'])
    cc_allreduce_max(bld, C, Dm, st1[0:1, 0:1], st1.b, 1, gsc)
    emit_d_scales(bld, C, p5, gsc, 0, 6, 7, bcs)
    emit_d_oraw(bld, C, bcs, st1, stat, Dm['gt'], Dm['up'], Dm['up'])
    cc_allreduce_max(bld, C, Dm, st1[0:1, 0:1], st1.b, 1, gsc)
    emit_d_scales(bld, C, p5, gsc, 0, 8, 9, bcs)
    emit_d_down(bld, C, Dm, bcs, stat, st1, cur, Dm['up'], out_dram)
